# version 01
# baseline (speedup 1.0000x reference)
import math
import numpy as np
import ml_dtypes
import concourse.bass as bass
import concourse.mybir as mybir
from concourse.bass_utils import run_bass_kernel_spmd

F32 = mybir.dt.float32
BF16 = mybir.dt.bfloat16
I16 = mybir.dt.int16
AF = mybir.ActivationFunctionType
ALU = mybir.AluOpType
AX = mybir.AxisListType
BF = ml_dtypes.bfloat16

D, T, NCH, I8, ICH = 2048, 1024, 16, 8192, 64
H, DK, DV, KD, CH, NGC = 16, 64, 128, 1024, 64, 16
EPS = 1e-5
SQD, SQ8 = math.sqrt(2048.0), math.sqrt(8192.0)
LN8 = math.log(1.0 / 8.0)
GSIZE = 45568
CSZ = 512
R0, R1, R2, R3A, R3X = 0, 16384, 24576, 32768, 40960
R3X = 40960

R3X_END = GSIZE


def es_enter(es, cm):
    v = cm.__enter__()
    es.append(cm)
    return v


class DStream:

    _made = {}

    def __init__(self, kb, name, depth=2, incs=1):
        self.kb = kb
        self.depth = depth
        self.incs = incs
        key = (name, depth)
        if key not in DStream._made:
            sems = [es_enter(kb._es, kb.nc.semaphore(f"{name}{d}"))
                    for d in range(depth)]
            DStream._made[key] = sems
            if not hasattr(kb, "all_dsems"):
                kb.all_dsems = []
            kb.all_dsems.extend(sems)
        self.sems = DStream._made[key]
        self.base = [kb.g(s) for s in self.sems]

    def expected(self, i):
        return self.base[i % self.depth] + 16 * self.incs * (i // self.depth + 1)

    def load(self, eng, out, in_, i):
        s = self.sems[i % self.depth]
        if isinstance(out, (list, tuple)):
            assert len(out) == self.incs
            for o, ii in zip(out, in_):
                self.kb.inc(eng.dma_start(out=o, in_=ii), s, 16)
        else:
            assert self.incs == 1
            self.kb.inc(eng.dma_start(out=out, in_=in_), s, 16)
        assert self.kb.g(s) == self.expected(i), (self.kb.g(s), self.expected(i))

    def wait(self, eng, i):
        eng.wait_ge(self.sems[i % self.depth], self.expected(i))


class Scratch:

    def __init__(self, kb):
        self.kb = kb
        self.off = R3X

    def f(self, n):
        o = self.off
        self.off += n
        assert self.off <= R3X_END, f"R3X overflow {self.off}"
        return self.kb.gf(o, 128, n)

    def b(self, n):
        o = self.off
        self.off += (n + 1) // 2
        assert self.off <= R3X_END, f"R3X overflow {self.off}"
        return self.kb.gb(o, 128, n)

    def i(self, n):
        o = self.off
        self.off += (n + 1) // 2
        assert self.off <= R3X_END, f"R3X overflow {self.off}"
        return self.kb.gi(o, 128, n)


class KB:
    def __init__(self, debug=False, upto=99, aupto=99):
        self.debug = debug
        self.upto = upto
        self.aupto = aupto
        self.nc = bass.Bass("TRN2", target_bir_lowering=False)
        self.cnt = {}
        self.dbg = []
        self._es = []
        DStream._made = {}

    def g(self, sem):
        return self.cnt.get(sem.name, 0)

    def inc(self, inst, sem, n=1):
        inst.then_inc(sem, n)
        self.cnt[sem.name] = self.g(sem) + n
        return self.cnt[sem.name]

    def sec(self, name, fn):
        getattr(self.block, name)(fn)

    def dstream(self, name, depth=2, incs=1):
        return DStream(self, name, depth, incs)

    def fence(self, eng, ins, which="v"):
        sem = {"v": self.srv, "s": self.srs, "g": self.srg}[which]
        n = self.inc(ins, sem, 1)
        eng.wait_ge(sem, n)
        return ins

    def full_barrier(self):
        sems = [self.sd, self.sg, self.sp, self.sv, self.ss, self.sc,
                self.sgc]
        sems += getattr(self, "all_dsems", [])
        c = {s.name: self.g(s) for s in sems}
        sems = tuple(sems)
        for name in ("sync", "gpsimd", "tensor", "vector", "scalar"):
            def f(e, c=c, sems=sems):
                for s in sems:
                    if c[s.name] > 0:
                        e.wait_ge(s, c[s.name])
            self.sec(name, f)

    def gf(self, off, p, n):
        return self.G[0:p, off:off + n]

    def gb(self, off, p, n):
        return self.G[0:p, off:off + (n + 1) // 2].bitcast(BF16)[0:p, 0:n]

    def gi(self, off, p, n):
        return self.G[0:p, off:off + (n + 1) // 2].bitcast(I16)[0:p, 0:n]

    def zbf(self, slot):
        if slot < 0:
            off = (-slot - 1) * 512
            return self.WA[:, off:off + 512].bitcast(BF16)[:, 0:1024]
        return self.gb(slot, 128, 1024)

    def zi16(self, slot):
        if slot < 0:
            off = (-slot - 1) * 512
            return self.WA[:, off:off + 512].bitcast(I16)[:, 0:1024]
        return self.gi(slot, 128, 1024)

    def snap(self, name, ap):
        if not self.debug:
            return
        self.full_barrier()
        d = self.nc.dram_tensor("dbg_" + name, list(ap.shape), ap.dtype,
                                kind="ExternalOutput").ap()
        sd0 = self.g(self.sd)

        def f(sy):
            self.inc(sy.dma_start(out=d, in_=ap), self.sd, 16)
            sy.wait_ge(self.sd, sd0 + 16)
        self.sec("sync", f)
        self.full_barrier()

    def tap(self, name, ap, dtype=None, shape=None):
        if not self.debug:
            return
        dt = dtype or ap.dtype
        sh = shape or list(ap.shape)
        d = self.nc.dram_tensor("dbg_" + name, sh, dt, kind="ExternalOutput").ap()
        self.dbg.append((d, ap))

    def build(self):
        nc = self.nc
        din = lambda n, s, dt=F32: nc.dram_tensor(n, s, dt, kind="ExternalInput").ap()
        self.xT = din("xT", [D, T])
        self.wq = din("wq", [D, KD]); self.wk = din("wk", [D, KD])
        self.wv = din("wv", [D, D]); self.wgx = din("wgx", [D, D])
        self.wo = din("wo", [D, D])
        self.wgk1 = din("wgk1", [D, 16]); self.wgk2 = din("wgk2", [16, KD])
        self.bgk2r = din("bgk2r", [1, KD])
        self.nwrow = din("nwrow", [1, D])
        self.wgt_sh = din("wgt_sh", [32768, 128])
        self.wdt_sh = din("wdt_sh", [16384, 128])
        self.h128 = din("h128", [128, 128], BF16)
        self.idbf = din("idbf", [128, 128], BF16)
        self.idf32 = din("idf32", [128, 128])
        self.u64s = din("u64s", [128, 64], BF16)
        self.u64m = din("u64m", [128, 64], BF16)
        self.su16 = din("su16", [16, 16], BF16)
        self.oh16 = din("oh16", [128, 256], BF16)
        self.maskv = din("maskv", [1, 8])

        self.out = nc.dram_tensor("out", [D, T], F32, kind="ExternalOutput").ap()

        self.ag_s_in = nc.dram_tensor("ag_s_in", [1, 4], F32).ap()
        self.ag_s_out = nc.dram_tensor("ag_s_out", [8, 4], F32).ap()
        self.ag_w_in = nc.dram_tensor("ag_w_in", [32768, 128], BF16).ap()
        self.ag_w_out = nc.dram_tensor("ag_w_out", [8 * 32768, 128], BF16).ap()
        self.ag_d_in = nc.dram_tensor("ag_d_in", [16384, 128], BF16).ap()
        self.ag_d_out = nc.dram_tensor("ag_d_out", [8 * 16384, 128], BF16).ap()
        self.ag_m_in = nc.dram_tensor("ag_m_in", [128, KD], F32).ap()
        self.ag_m_out = nc.dram_tensor("ag_m_out", [1024, KD], F32).ap()
        self.x_spill = nc.dram_tensor("x_spill", [D, T], F32).ap()

        es = self._es
        sb = lambda n, s, dt=F32: es_enter(es, nc.sbuf_tensor(n, s, dt))
        self.G = sb("G", [128, GSIZE])
        self.WA = sb("WA", [128, 4096])
        self.MISC = sb("MISC", [128, 1408])
        self.ROW = sb("ROW", [1, 1152])
        self.TMS = sb("TMS", [128, 96])
        self.CB = sb("CB", [128, 658], BF16)
        self.CF = sb("CF", [128, 148])
        self.pA = es_enter(es, nc.psum_tensor("pA", [128, 1024], F32))
        self.pB = es_enter(es, nc.psum_tensor("pB", [128, 1024], F32))
        self.pC = es_enter(es, nc.psum_tensor("pC", [128, 2048], F32))
        for nm in ("sd", "sg", "sp", "sv", "ss", "sc", "sst", "srv", "srs",
                   "srg", "sgc"):
            setattr(self, nm, es_enter(es, nc.semaphore(nm)))
        self.block = es_enter(es, nc.Block())

        self.c_h128 = self.CB[:, 0:128]
        self.c_idbf = self.CB[:, 128:256]
        self.c_u64s = self.CB[:, 256:320]
        self.c_u64m = self.CB[:, 320:384]
        self.c_su16 = self.CB[0:16, 384:400]
        self.c_oh = self.CB[:, 400:656]
        self.c_e0 = self.CB[:, 656:657]
        self.c_onesb = self.CB[:, 657:658]
        self.c_idf = self.CF[:, 0:128]
        self.c_ones = self.CF[:, 128:129]
        self.c_ones1 = self.CF[0:1, 129:130]
        self.c_mask = self.CF[:, 130:138]
        self.c_scl = self.CF[:, 138:146]
        self.c_eps = self.CF[:, 146:147]
        self.c_onesrow = self.ROW[0:1, 1024:1152]

        t = self.TMS
        self.t_r = t[:, 0:8]; self.t_lnr8 = t[:, 8:16]; self.t_lnr2 = t[:, 16:24]
        self.t_amax = t[:, 24:32]; self.t_m = t[:, 32:40]; self.t_dq = t[:, 40:48]
        self.t_tmp = t[:, 48:56]; self.t_tmp2 = t[:, 56:64]; self.t_c = t[:, 64:72]
        self.t_e = t[:, 72:80]; self.t_lnr = t[:, 80:88]; self.t_rg = t[:, 88:96]

        self.ph_startup()
        if self.upto >= 1:
            self.ph_quant()
            self.tap("scl", self.c_scl[:, 0:6])
            self.tap("wsh", self.ag_w_in)
            self.tap("dsh", self.ag_d_in)
        if self.upto >= 2:
            self.full_barrier()
            self.ffn(1)
            self.tap("x1", self.x_spill)
            self.tap("tr", self.t_r)
            self.tap("tdq", self.t_dq)
        if self.upto >= 3:
            self.full_barrier()
            self.attn()
            self.tap("x2", self.gf(R0, 128, 16384))
        if self.upto >= 4:
            self.full_barrier()
            self.ffn(2)
        self.ph_finish()
        return nc

    def ph_startup(self):
        P = self

        def f_sync(sy):
            for d, s in ((P.c_h128, P.h128), (P.c_idbf, P.idbf),
                         (P.c_u64s, P.u64s), (P.c_u64m, P.u64m),
                         (P.c_su16, P.su16), (P.c_oh, P.oh16),
                         (P.c_idf, P.idf32)):
                P.inc(sy.dma_start(out=d, in_=s), P.sd, 16)
            P.inc(sy.dma_start(out=P.c_mask[0:1, :], in_=P.maskv), P.sd, 16)
            P.inc(sy.dma_start(
                out=P.gf(R0, 128, 16384).rearrange("p (c t) -> p c t", c=16),
                in_=P.xT.rearrange("(c p) t -> p c t", p=128)), P.sd, 16)
        P.sec("sync", f_sync)


        def f_vec(v):
            v.memset(P.c_ones, 1.0)
            v.memset(P.c_ones1, 1.0)
            v.memset(P.c_onesrow, 1.0)
            v.memset(P.c_onesb, 1.0)
            v.memset(P.c_eps, EPS)
            P.fence(v, v.memset(P.c_e0, 0.0))
            P.inc(v.memset(P.c_e0[0:1, 0:1], 1.0), P.sv, 1)
        P.sec("vector", f_vec)

        sd0, sv0 = P.g(P.sd), P.g(P.sv)

        def f_tensor(t):
            t.wait_ge(P.sd, sd0)
            t.wait_ge(P.sv, sv0)
            P.inc(t.matmul(P.pA[:, 0:8], P.c_onesrow, P.c_mask[0:1, :],
                           start=True, stop=True), P.sp, 1)
        P.sec("tensor", f_tensor)

        sp0 = P.g(P.sp)

        def f_vec2(v):
            v.wait_ge(P.sp, sp0)
            P.inc(v.tensor_copy(out=P.c_mask, in_=P.pA[:, 0:8]), P.sv, 1)
        P.sec("vector", f_vec2)

    def ph_quant(self):
        P = self
        NT = 96
        gu_in = P.wgt_sh.rearrange("(i p a) n -> i p (a n)", p=128, a=4)
        d_in = P.wdt_sh.rearrange("(i p a) n -> i p (a n)", p=128, a=4)
        gu_o = P.ag_w_in.rearrange("(i p a) n -> i p (a n)", p=128, a=4)
        d_o = P.ag_d_in.rearrange("(i p a) n -> i p (a n)", p=128, a=4)
        src = lambda i: gu_in[i] if i < 64 else d_in[i - 64]
        dst = lambda i: gu_o[i] if i < 64 else d_o[i - 64]

        bufs = [P.gf(R1, 128, 512), P.gf(R1 + 512, 128, 512)]
        obufs = [P.gb(R1 + 1024, 128, 512), P.gb(R1 + 1280, 128, 512)]
        i16 = P.gi(R1 + 1536, 128, 512)

        ss0 = P.g(P.ss)
        sv_pre = P.g(P.sv)
        ds_q = P.dstream("dq")

        def f_sync(sy):
            for i in range(NT):
                if i >= 2:
                    sy.wait_ge(P.ss, ss0 + (i - 1))
                ds_q.load(sy, bufs[i % 2], src(i), i)
        P.sec("sync", f_sync)

        def f_scal(s_):
            for i in range(NT):
                ds_q.wait(s_, i)
                P.inc(s_.activation(out=bufs[i % 2], in_=bufs[i % 2], func=AF.Abs,
                                    accum_out=P.WA[:, i:i + 1]), P.ss, 1)
        P.sec("scalar", f_scal)

        sp0 = P.g(P.sp)

        def f_tensor(t):
            t.wait_ge(P.ss, ss0 + NT)
            t.wait_ge(P.sv, sv_pre)
            P.inc(t.matmul(P.pA[0:1, 0:NT], P.c_ones, P.WA[:, 0:NT],
                           start=True, stop=True), P.sp, 1)
        P.sec("tensor", f_tensor)

        sv0 = P.g(P.sv)

        def f_vec(v):
            v.wait_ge(P.sp, sp0 + 1)
            v.reduce_sum(out=P.ROW[0:1, 0:1], in_=P.pA[0:1, 0:32], axis=AX.X)
            v.reduce_sum(out=P.ROW[0:1, 1:2], in_=P.pA[0:1, 32:64], axis=AX.X)
            v.reduce_sum(out=P.ROW[0:1, 2:3], in_=P.pA[0:1, 64:96], axis=AX.X)
            P.inc(v.memset(P.ROW[0:1, 3:4], 0.0), P.sv, 1)
        P.sec("vector", f_vec)

        def f_gps(g):
            g.wait_ge(P.sv, sv0 + 1)
            P.inc(g.dma_start(out=P.ag_s_in, in_=P.ROW[0:1, 0:4]), P.sg, 16)
            g.wait_ge(P.sg, P.g(P.sg))
            P.inc(g.collective_compute(
                "AllGather", ALU.bypass, replica_groups=[list(range(8))],
                ins=[P.ag_s_in.opt()], outs=[P.ag_s_out.opt()]), P.sc, 1)
            g.wait_ge(P.sc, P.g(P.sc))
            P.inc(g.dma_start(out=P.ROW[0:1, 16:48],
                              in_=P.ag_s_out.rearrange("a b -> (a b)")[None, :]), P.sg, 16)
        P.sec("gpsimd", f_gps)

        sg1 = P.g(P.sg)
        sv1 = P.g(P.sv)

        def f_vec2(v):
            v.wait_ge(P.sg, sg1)
            P.fence(v, v.reduce_sum(
                out=P.ROW[0:1, 4:8],
                in_=P.ROW[0:1, 16:48].rearrange("p (r k) -> p k r", r=8),
                axis=AX.X))
            P.fence(v, v.tensor_scalar(out=P.ROW[0:1, 8:11], in0=P.ROW[0:1, 4:7],
                            scalar1=1.0 / 16777216.0, scalar2=EPS,
                            op0=ALU.mult, op1=ALU.max))
            P.inc(v.reciprocal(out=P.ROW[0:1, 12:15], in_=P.ROW[0:1, 8:11]), P.sv, 1)
        P.sec("vector", f_vec2)

        sv2, sp1 = P.g(P.sv), P.g(P.sp)

        def f_tensor2(t):
            t.wait_ge(P.sv, sv2)
            P.inc(t.matmul(P.pA[:, 16:19], P.c_onesrow, P.ROW[0:1, 8:11],
                           start=True, stop=True), P.sp, 1)
            P.inc(t.matmul(P.pA[:, 19:22], P.c_onesrow, P.ROW[0:1, 12:15],
                           start=True, stop=True), P.sp, 1)
        P.sec("tensor", f_tensor2)

        def f_vec3(v):
            v.wait_ge(P.sp, sp1 + 2)
            v.tensor_copy(out=P.c_scl[:, 3:6], in_=P.pA[:, 16:19])
            P.inc(v.tensor_copy(out=P.c_scl[:, 0:3], in_=P.pA[:, 19:22]), P.sv, 1)
        P.sec("vector", f_vec3)

        sv3 = P.g(P.sv)
        ds_q2 = P.dstream("dq2")
        ds_qs = P.dstream("dqs")

        def f_sync2(sy):
            sy.wait_ge(P.sv, sv3)
            for i in range(NT):
                if i >= 2:
                    sy.wait_ge(P.sv, sv3 + 3 * (i - 1))
                ds_q2.load(sy, bufs[i % 2], src(i), i)
        P.sec("sync", f_sync2)

        def f_vec4(v):
            for i in range(NT):
                ds_q2.wait(v, i)
                if i >= 2:
                    ds_qs.wait(v, i - 2)
                iv = P.c_scl[:, 0:1] if i < 32 else (
                    P.c_scl[:, 1:2] if i < 64 else P.c_scl[:, 2:3])
                P.fence(v, v.tensor_scalar(out=bufs[i % 2], in0=bufs[i % 2],
                                            scalar1=iv, scalar2=-1.0,
                                            op0=ALU.mult, op1=ALU.max))
                P.fence(v, v.tensor_scalar(out=i16, in0=bufs[i % 2], scalar1=1.0,
                                           scalar2=None, op0=ALU.min))
                P.inc(v.tensor_copy(out=obufs[i % 2], in_=i16), P.sv, 3)
        P.sec("vector", f_vec4)

        def f_gps2(g):
            for i in range(NT):
                g.wait_ge(P.sv, sv3 + 3 * (i + 1))
                ds_qs.load(g, dst(i), obufs[i % 2], i)
            for s in ds_qs.sems:
                g.wait_ge(s, P.g(s))
            P.inc(g.collective_compute(
                "AllGather", ALU.bypass, replica_groups=[list(range(8))],
                ins=[P.ag_w_in.opt()], outs=[P.ag_w_out.opt()]), P.sc, 1)
            P.inc(g.collective_compute(
                "AllGather", ALU.bypass, replica_groups=[list(range(8))],
                ins=[P.ag_d_in.opt()], outs=[P.ag_d_out.opt()]), P.sc, 1)
        P.sec("gpsimd", f_gps2)
        self.cc_w = P.g(P.sc)

    def stats(self, x_resident):
        P = self
        S = Scratch(P)
        sq = [S.f(1024), S.f(1024)]
        sp0, ss0, sd0 = P.g(P.sp), P.g(P.ss), P.g(P.sd)

        if x_resident:
            xch = lambda c: P.gf(R0 + c * 1024, 128, 1024)
        else:
            xbufs = [P.gf(R3A, 128, 1024), P.gf(R3A + 1024, 128, 1024)]
            ds_st = P.dstream("dst")

            def f_sync(sy):
                for c in range(16):
                    if c >= 2:
                        sy.wait_ge(P.sp, sp0 + 2 * (c - 1))
                        sy.wait_ge(P.ss, ss0 + (c - 1))
                    ds_st.load(sy, xbufs[c % 2],
                               P.x_spill[c * 128:(c + 1) * 128, :], c)
            P.sec("sync", f_sync)
            xch = lambda c: xbufs[c % 2]
            self._ds_st = ds_st

        def f_scal(s_):
            for c in range(16):
                if not x_resident:
                    self._ds_st.wait(s_, c)
                if c >= 2:
                    s_.wait_ge(P.sp, sp0 + 2 * (c - 1))
                P.inc(s_.activation(out=sq[c % 2], in_=xch(c), func=AF.Square),
                      P.ss, 1)
        P.sec("scalar", f_scal)

        def f_tensor(t):
            for c in range(16):
                if not x_resident:
                    self._ds_st.wait(t, c)
                t.wait_ge(P.ss, ss0 + (c + 1))
                for hh in range(2):
                    t.matmul(P.pA[0:1, hh * 512:(hh + 1) * 512], P.c_ones,
                             xch(c)[:, hh * 512:(hh + 1) * 512],
                             start=(c == 0), stop=(c == 15))
                    ins = t.matmul(P.pB[0:1, hh * 512:(hh + 1) * 512], P.c_ones,
                                   sq[c % 2][:, hh * 512:(hh + 1) * 512],
                                   start=(c == 0), stop=(c == 15))
                P.inc(ins, P.sp, 2)
        P.sec("tensor", f_tensor)

        sp1 = P.g(P.sp)

        def f_v1(v):
            v.wait_ge(P.sp, sp1)
            P.inc(v.tensor_copy(out=P.ROW[0:1, 0:1024], in_=P.pA[0:1, :]),
                  P.sv, 1)
        P.sec("vector", f_v1)

        sp2 = P.g(P.sp)

        def f_t1(t):
            t.wait_ge(P.sv, P.g(P.sv))
            for ti in range(8):
                ins = t.matmul(P.pC[:, ti:ti + 1],
                               P.ROW[0:1, ti * 128:(ti + 1) * 128],
                               P.c_ones1, start=True, stop=True)
            P.inc(ins, P.sp, 1)
        P.sec("tensor", f_t1)

        def f_v2(v):
            v.wait_ge(P.sp, sp2 + 1)
            P.fence(v, v.tensor_copy(out=P.t_rg, in_=P.pC[:, 0:8]))
            P.inc(v.tensor_copy(out=P.ROW[0:1, 0:1024], in_=P.pB[0:1, :]),
                  P.sv, 1)
        P.sec("vector", f_v2)

        sp3 = P.g(P.sp)

        def f_t2(t):
            t.wait_ge(P.sv, P.g(P.sv))
            for ti in range(8):
                ins = t.matmul(P.pC[:, 8 + ti:9 + ti],
                               P.ROW[0:1, ti * 128:(ti + 1) * 128],
                               P.c_ones1, start=True, stop=True)
            P.inc(ins, P.sp, 1)
        P.sec("tensor", f_t2)

        def f_v3(v):
            v.wait_ge(P.sp, sp3 + 1)
            P.fence(v, v.tensor_scalar(out=P.t_rg, in0=P.t_rg,
                                       scalar1=1.0 / 2048.0, scalar2=None,
                                       op0=ALU.mult))
            P.fence(v, v.tensor_scalar(out=P.t_tmp, in0=P.pC[:, 8:16],
                                       scalar1=1.0 / 2048.0, scalar2=None,
                                       op0=ALU.mult))
            P.fence(v, v.tensor_tensor(out=P.t_tmp2, in0=P.t_rg, in1=P.t_rg,
                                       op=ALU.mult))
            P.inc(v.tensor_sub(P.t_tmp, P.t_tmp, P.t_tmp2), P.sv, 1)
        P.sec("vector", f_v3)

        def f_s1(s_):
            s_.wait_ge(P.sv, P.g(P.sv))
            P.inc(s_.activation(out=P.t_lnr, in_=P.t_tmp, func=AF.Ln,
                                bias=P.c_eps[:, 0:1]), P.ss, 1)
        P.sec("scalar", f_s1)

        def f_v4(v):
            v.wait_ge(P.ss, P.g(P.ss))
            P.fence(v, v.tensor_scalar(out=P.t_lnr, in0=P.t_lnr, scalar1=-0.5,
                                       scalar2=None, op0=ALU.mult))
            v.tensor_scalar(out=P.t_lnr8, in0=P.t_lnr, scalar1=LN8,
                            scalar2=None, op0=ALU.add)
            P.inc(v.tensor_scalar(out=P.t_lnr2, in0=P.t_lnr, scalar1=2.0,
                                  scalar2=None, op0=ALU.mult), P.sv, 1)
        P.sec("vector", f_v4)

        def f_s2(s_):
            s_.wait_ge(P.sv, P.g(P.sv))
            P.inc(s_.activation(out=P.t_r, in_=P.t_lnr, func=AF.Exp), P.ss, 1)
        P.sec("scalar", f_s2)

    def amax_fm(self, fbase, nch, t_out):
        P = self
        S = Scratch(P)
        runm = S.f(1024)
        sv0 = P.g(P.sv)

        def f_vec(v):
            v.wait_ge(P.ss, P.g(P.ss))
            src = P.gf(fbase, 128, nch * 1024).rearrange(
                "p (c t) -> p t c", c=nch)
            P.inc(v.reduce_max(out=runm, in_=src, axis=AX.X,
                               apply_absolute_value=True), P.sv, 1)
        P.sec("vector", f_vec)

        sp0 = P.g(P.sp)

        def f_tensor(t):
            t.wait_ge(P.sv, sv0 + 1)
            for ti in range(8):
                ins = t.transpose(P.pC[:, ti * 128:(ti + 1) * 128],
                                  runm[:, ti * 128:(ti + 1) * 128], P.c_idf)
            P.inc(ins, P.sp, 1)
        P.sec("tensor", f_tensor)

        sv1 = P.g(P.sv)

        def f_vec2(v):
            v.wait_ge(P.sp, sp0 + 1)
            ins = v.reduce_max(
                out=t_out,
                in_=P.pC[:, 0:1024].rearrange("p (a b) -> p a b", a=8),
                axis=AX.X)
            P.fence(v, ins)
            P.inc(v.tensor_copy(out=P.t_tmp2[:, 0:1], in_=t_out[:, 0:1]),
                  P.sv, 1)
        P.sec("vector", f_vec2)

    def col2row(self, cols, rowdst8, k):
        P = self
        sp0 = P.g(P.sp)

        def f_tensor(t):
            t.wait_ge(P.sv, P.g(P.sv))
            for j in range(k):
                ins = t.transpose(P.pB[0:1, j * 128:(j + 1) * 128],
                                  cols[:, j:j + 1], P.c_idf)
            P.inc(ins, P.sp, 1)
        P.sec("tensor", f_tensor)

        sv0 = P.g(P.sv)

        def f_vec(v):
            v.wait_ge(P.sp, sp0 + 1)
            P.inc(v.tensor_copy(out=rowdst8, in_=P.pB[0:1, 0:k * 128]), P.sv, 1)
        P.sec("vector", f_vec)

    def ffn(self, call):
        P = self
        x_old_src = P.xT if call == 1 else P.x_spill

        self.stats(x_resident=True)

        if call == 2:
            def f_gspill(g):
                g.wait_ge(P.ss, P.g(P.ss))
                g.wait_ge(P.sp, P.g(P.sp))
                P.inc(g.dma_start(
                    out=P.x_spill.rearrange("(c p) t -> p c t", p=128),
                    in_=P.gf(R0, 128, 16384).rearrange("p (c t) -> p c t", c=16)),
                    P.sg, 16)
            P.sec("gpsimd", f_gspill)

        P.full_barrier()
        u = [R1 + c * CSZ for c in range(16)]
        pong = [R2 + c * CSZ for c in range(16)]
        self.butterfly(u, pong, 16,
                       first_src=lambda c: P.gf(R0 + c * 1024, 128, 1024))
        P.full_barrier()
        u_slots = list(u)

        sv0 = P.g(P.sv)

        def f_vece0(v):
            P.fence(v, v.tensor_scalar(out=P.t_tmp, in0=P.t_rg, scalar1=-2048.0,
                                       scalar2=None, op0=ALU.mult))
            P.fence(v, v.memset(P.ROW[0:1, 512:576].bitcast(BF16)[0:1, 0:128],
                                0.0))
            P.inc(v.memset(P.ROW[0:1, 512:576].bitcast(BF16)[0:1, 0:1], 1.0),
                  P.sv, 1)
        P.sec("vector", f_vece0)
        e0row = P.ROW[0:1, 0:512].bitcast(BF16)[0:1, 0:1024]
        e0lhs = P.ROW[0:1, 512:576].bitcast(BF16)[0:1, 0:128]
        self.col2row(P.t_tmp, e0row, 8)

        sp0, ss0 = P.g(P.sp), P.g(P.ss)

        def f_mmh(t):
            t.wait_ge(P.sv, P.g(P.sv))
            for c in range(16):
                if c >= 2:
                    t.wait_ge(P.ss, ss0 + (c - 1))
                pp = P.pA if c % 2 == 0 else P.pB
                for hh in range(2):
                    ins = t.matmul(pp[:, hh * 512:(hh + 1) * 512], P.c_h128,
                                   P.gb(u_slots[c], 128, 1024)
                                   [:, hh * 512:(hh + 1) * 512],
                                   start=True, stop=(c != 0))
                    if c == 0:
                        ins = t.matmul(pp[:, hh * 512:(hh + 1) * 512], e0lhs,
                                       e0row[0:1, hh * 512:(hh + 1) * 512],
                                       start=False, stop=True)
                P.inc(ins, P.sp, 1)
        P.sec("tensor", f_mmh)

        def f_scevac(s_):
            for c in range(16):
                s_.wait_ge(P.sp, sp0 + (c + 1))
                pp = P.pA if c % 2 == 0 else P.pB
                P.inc(s_.activation(out=P.gf(R0 + c * 1024, 128, 1024),
                                    in_=pp[:, :], func=AF.Copy), P.ss, 1)
        P.sec("scalar", f_scevac)

        self.amax_fm(R0, 16, P.t_amax)
        sv1 = P.g(P.sv)

        def f_vrows(v):
            P.fence(v, v.tensor_tensor(out=P.t_tmp, in0=P.t_amax, in1=P.t_r,
                                       op=ALU.mult))
            P.fence(v, v.tensor_scalar(out=P.t_tmp, in0=P.t_tmp, scalar1=1.0 / SQD,
                            scalar2=EPS, op0=ALU.mult, op1=ALU.max))
            v.tensor_scalar(out=P.t_dq, in0=P.t_tmp, scalar1=1.0 / 127.0,
                            scalar2=None, op0=ALU.mult)
            P.fence(v, v.reciprocal(out=P.t_tmp2, in_=P.t_tmp))
            P.fence(v, v.tensor_tensor(out=P.t_m, in0=P.t_tmp2, in1=P.t_r,
                                       op=ALU.mult))
            P.inc(v.tensor_scalar(out=P.t_m, in0=P.t_m, scalar1=127.0 / SQD,
                                  scalar2=None, op0=ALU.mult), P.sv, 1)
        P.sec("vector", f_vrows)

        self.col2row(P.t_m, P.ROW[0:1, 0:1024], 8)

        sp1 = P.g(P.sp)

        def f_mmrep(t):
            t.wait_ge(P.sv, P.g(P.sv))
            for hh in range(2):
                ins = t.matmul(P.pC[:, hh * 512:(hh + 1) * 512], P.c_onesrow,
                               P.ROW[0:1, hh * 512:(hh + 1) * 512],
                               start=True, stop=True)
            P.inc(ins, P.sp, 1)
        P.sec("tensor", f_mmrep)

        sv2 = P.g(P.sv)

        def f_vround(v):
            v.wait_ge(P.sp, sp1 + 1)
            for c in range(16):
                ins = v.tensor_tensor(out=P.gi(R1 + c * CSZ, 128, 1024),
                                in0=P.gf(R0 + c * 1024, 128, 1024),
                                in1=P.pC[:, 0:1024], op=ALU.mult)
            P.fence(v, ins)
            P.inc(v.tensor_copy(out=P.gb(R2, 128, 16384),
                                in_=P.gi(R1, 128, 16384)), P.sv, 2)
        P.sec("vector", f_vround)
        fq = lambda c: P.gb(R2 + c * CSZ, 128, 1024)
        if call == 1:
            self.snap("fq0", P.gb(R2, 128, 2048))
            self.snap("amax", P.t_amax)
            self.snap("tm", P.t_m)
            self.snap("tdq1", P.t_dq)
            self.snap("fraw0", P.gf(R0, 128, 1024))

        self.col2row(P.t_dq, P.ROW[0:1, 0:1024], 8)

        S = Scratch(P)
        dqgrep = S.b(1024)
        gsc = [S.b(1024), S.b(1024)]
        sgb = [S.b(1024), S.b(1024)]
        tb = [S.b(1024)]
        upb = [S.b(1024)]

        sp2 = P.g(P.sp)

        def f_mmdq(t):
            t.wait_ge(P.sv, P.g(P.sv))
            for hh in range(2):
                ins = t.matmul(P.pA[:, hh * 512:(hh + 1) * 512], P.c_onesrow,
                               P.ROW[0:1, hh * 512:(hh + 1) * 512],
                               start=True, stop=True)
            P.inc(ins, P.sp, 1)
        P.sec("tensor", f_mmdq)

        sv3 = P.g(P.sv)

        def f_vdq(v):
            v.wait_ge(P.sp, sp2 + 1)
            P.inc(v.tensor_scalar(out=dqgrep, in0=P.pA[:, 0:1024],
                                  scalar1=P.c_scl[:, 3:4], scalar2=None,
                                  op0=ALU.mult), P.sv, 1)
        P.sec("vector", f_vdq)
        P.full_barrier()

        wview = P.ag_w_out.rearrange("(r w a o p) n -> r w a o p n",
                                     r=8, w=2, a=2, o=64)
        wbuf = [P.WA.bitcast(BF16)[:, 0:2048], P.WA.bitcast(BF16)[:, 2048:4096]]
        sp3 = P.g(P.sp)
        sv4 = P.g(P.sv)
        ss1 = P.g(P.ss)
        ds_w = P.dstream("dw", incs=2)

        def f_wload(sy):
            sy.wait_ge(P.sc, P.cc_w)
            for oc in range(64):
                for w in range(2):
                    i = oc * 2 + w
                    if i >= 2:
                        sy.wait_ge(P.sp, sp3 + (i - 1))
                    outs = [wbuf[i % 2].rearrange("p (k n) -> p k n",
                                                  k=16)[:, a::2, :]
                            for a in range(2)]
                    ins_ = [wview[:, w, a, oc, :, :].rearrange("r p n -> p r n")
                            for a in range(2)]
                    ds_w.load(sy, outs, ins_, i)
        P.sec("sync", f_wload)

        def f_mmgu(t):
            for oc in range(64):
                for w in range(2):
                    i = oc * 2 + w
                    ds_w.wait(t, i)
                    if oc >= 2 and w == 0:
                        t.wait_ge(P.sv, sv4 + 5 * (oc - 1))
                    pp = (P.pA if w == 0 else P.pB) if oc % 2 == 0 else \
                        (P.pC[:, 0:1024] if w == 0 else P.pC[:, 1024:2048])
                    for kc in range(16):
                        for hh in range(2):
                            ins = t.matmul(
                                pp[:, hh * 512:(hh + 1) * 512],
                                wbuf[i % 2][:, kc * 128:(kc + 1) * 128],
                                fq(kc)[:, hh * 512:(hh + 1) * 512],
                                start=(kc == 0), stop=(kc == 15))
                    P.inc(ins, P.sp, 1)
        P.sec("tensor", f_mmgu)

        def f_vzchain(v):
            for oc in range(64):
                v.wait_ge(P.sp, sp3 + 2 * (oc + 1))
                gp = P.pA[:, :] if oc % 2 == 0 else P.pC[:, 0:1024]
                up = P.pB[:, :] if oc % 2 == 0 else P.pC[:, 1024:2048]
                P.inc(v.tensor_tensor(out=gsc[oc % 2], in0=gp, in1=dqgrep,
                                      op=ALU.mult), P.sv, 1)
                v.wait_ge(P.ss, ss1 + oc + 1)
                v.tensor_tensor(out=tb[0], in0=gp, in1=sgb[oc % 2],
                                op=ALU.mult)
                P.fence(v, v.tensor_copy(out=upb[0], in_=up))
                P.inc(v.tensor_tensor(out=P.gb(self.zslot(oc), 128, 1024),
                                      in0=tb[0], in1=upb[0],
                                      op=ALU.mult), P.sv, 4)
        P.sec("vector", f_vzchain)

        def f_ssig(s_):
            for oc in range(64):
                s_.wait_ge(P.sv, sv4 + 5 * oc + 1)
                P.inc(s_.activation(out=sgb[oc % 2], in_=gsc[oc % 2],
                                    func=AF.Sigmoid), P.ss, 1)
        P.sec("scalar", f_ssig)
        P.full_barrier()
        if call == 1:
            self.snap("z0", P.gb(self.zslot(0), 128, 1024))
            self.snap("z63", P.gb(self.zslot(63), 128, 1024))
            self.snap("gsc63", gsc[1])
            self.snap("sgb63", sgb[1])
            self.snap("upb63", upb[0])
            self.snap("dqgrep", dqgrep)
            self.snap("fqall", P.gb(R2, 128, 16384))
            self.snap("wbuf0", P.WA.bitcast(BF16)[:, 0:2048])
            self.snap("wbuf1", P.WA.bitcast(BF16)[:, 2048:4096])

        zsl = [self.zslot(c) for c in range(64)]
        spares = ([R3X + k * CSZ for k in range(9)]
                  + [R2 + k * CSZ for k in range(16)]
                  + [-(k + 1) for k in range(8)])
        self.butterfly(zsl, spares, 64)
        bad = lambda s: s < 0 or (R2 <= s < R2 + 8192)
        all_g = set([self.zslot(c) for c in range(64)]
                    + [R3X + k * CSZ for k in range(9)])
        free_g = sorted(all_g - set(s for s in zsl if not bad(s))
                        - set([s for s in zsl if bad(s)]))
        free_g = sorted(all_g - set(zsl))
        free_ok = [s for s in free_g if not bad(s)]
        moves = []
        for c in range(64):
            if bad(zsl[c]):
                dst = free_ok.pop(0)
                moves.append((dst, zsl[c]))
                zsl[c] = dst
        sv_mv = P.g(P.sv)

        def f_mv(v):
            v.wait_ge(P.sgc, P.g(P.sgc))
            v.wait_ge(P.sv, P.g(P.sv))
            ins = None
            for dst, srcslot in moves:
                ins = v.tensor_copy(out=P.zbf(dst), in_=P.zbf(srcslot))
            if ins is not None:
                P.inc(ins, P.sv, 1)
        P.sec("vector", f_mv)
        P.full_barrier()
        self.z_final = list(zsl)

        self.quant8(call)
        P.full_barrier()
        if call == 1:
            self.snap("zq0", P.zbf(self.z_final[0]))
            self.snap("te", P.t_e)

        self.down(call, x_old_src)

    def butterfly(self, slots, spares, nchunks, first_src=None):
        P = self
        nst = int(math.log2(nchunks))
        npairs = nchunks // 2
        n_g = sum(1 for k in range(npairs) if k % 3 == 2)
        n_v = npairs - n_g
        nsp = len(spares)
        pool_g = spares[:min(n_g, nsp)]
        pool_v = spares[min(n_g, nsp):]
        vops, gops = [], []
        marks_v, marks_g = [], []
        for s in range(nst):
            h = 1 << s
            pairs = []
            for base in range(0, nchunks, 2 * h):
                for i in range(h):
                    pairs.append((base + i, base + h + i))
            freed_v, freed_g = [], []
            for k, (ca, cb) in enumerate(pairs):
                gps = (k % 3 == 2)
                tgt = gops if gps else vops
                if s == 0 and first_src is not None:
                    tgt.append((slots[ca], ("x", ca), ("x", cb), ALU.add))
                    tgt.append((slots[cb], ("x", ca), ("x", cb), ALU.subtract))
                else:
                    pool = pool_g if gps else pool_v
                    freed = freed_g if gps else freed_v
                    if pool:
                        sp_off = pool.pop(0)
                    else:
                        sp_off = freed.pop(0)
                        tgt.append(("fence",))
                    tgt.append((sp_off, ("s", slots[ca]), ("s", slots[cb]),
                                ALU.subtract))
                    tgt.append((slots[ca], ("s", slots[ca]), ("s", slots[cb]),
                                ALU.add))
                    freed.append(slots[cb])
                    slots[cb] = sp_off
            pool_v.extend(freed_v)
            pool_g.extend(freed_g)
            marks_v.append(sum(1 for o in vops if o[0] != "fence"))
            marks_g.append(sum(1 for o in gops if o[0] != "fence"))

        sv0, sg0 = P.g(P.sv), P.g(P.sgc)

        def slot_ap(o):
            return P.zbf(o)

        def ap_of(x):
            kind, v = x
            if kind == "x":
                return first_src(v)
            return slot_ap(v)

        def emit(eng, ops, marks, own_sem, other_sem, other_marks):
            other0 = sg0 if own_sem is P.sv else sv0
            own0 = sv0 if own_sem is P.sv else sg0
            idx = 0
            for s in range(nst):
                if s > 0 and other_marks[s - 1] > 0:
                    eng.wait_ge(other_sem, other0 + other_marks[s - 1])
                if s > 0 and marks[s - 1] > 0:
                    eng.wait_ge(own_sem, own0 + marks[s - 1])
                while idx < marks[s]:
                    op_t = ops[idx]
                    if op_t[0] == "fence":
                        eng.wait_ge(own_sem, P.g(own_sem))
                        idx += 1
                        continue
                    o, a, b, op = op_t
                    ins = eng.tensor_tensor(out=slot_ap(o),
                                            in0=ap_of(a), in1=ap_of(b), op=op)
                    P.inc(ins, own_sem, 1)
                    idx += 1

        def f_vec(v):
            emit(v, vops, marks_v, P.sv, P.sgc, marks_g)
        def f_gps(g):
            emit(g, gops, marks_g, P.sgc, P.sv, marks_v)
        P.sec("vector", f_vec)
        P.sec("gpsimd", f_gps)

    def zslot(self, c):
        if c < 32:
            return R0 + c * CSZ
        if c < 48:
            return R1 + (c - 32) * CSZ
        return R3A + (c - 48) * CSZ

    def quant8(self, call):
        P = self
        zf = self.z_final
        sv0 = P.g(P.sv)

        def f_vc(v):
            P.fence(v, v.tensor_tensor(out=P.t_c, in0=P.t_dq, in1=P.t_dq,
                                       op=ALU.mult))
            P.fence(v, v.tensor_scalar(out=P.t_c, in0=P.t_c,
                                       scalar1=P.c_scl[:, 3:4],
                                       scalar2=None, op0=ALU.mult))
            P.inc(v.tensor_scalar(out=P.t_c, in0=P.t_c,
                                  scalar1=P.c_scl[:, 4:5], scalar2=None,
                                  op0=ALU.mult), P.sv, 1)
        P.sec("vector", f_vc)

        for tq in range(8):
            self._quant8_tq(tq)
        self.quant8_fin()

    def _quant8_tq(self, tq):
        P = self
        zf = self.z_final
        tsl = slice(tq * 128, (tq + 1) * 128)
        sp0, ss0, sv0 = P.g(P.sp), P.g(P.ss), P.g(P.sv)

        def f_mm(t):
            if tq > 0:
                t.wait_ge(P.sv, sv0)
            for c in range(64):
                grp = c // 8
                if c % 8 == 0 and grp >= 2:
                    t.wait_ge(P.ss, ss0 + (grp - 1))
                pp = P.pA if grp % 2 == 0 else P.pB
                ins = t.matmul(pp[:, (c % 8) * 128:(c % 8 + 1) * 128],
                               P.c_h128, P.zbf(zf[c])[:, tsl],
                               start=True, stop=True)
                if c % 8 == 7:
                    P.inc(ins, P.sp, 1)
        P.sec("tensor", f_mm)

        def f_sevac(s_):
            for grp in range(8):
                s_.wait_ge(P.sp, sp0 + grp + 1)
                pp = P.pA if grp % 2 == 0 else P.pB
                P.inc(s_.activation(
                    out=P.gf(R2 + grp * 1024, 128, 1024), in_=pp[:, :],
                    func=AF.Copy), P.ss, 1)
        P.sec("scalar", f_sevac)

        sv1 = P.g(P.sv)
        ss1 = P.g(P.ss)

        def f_vamax(v):
            v.wait_ge(P.ss, ss1)
            P.inc(v.reduce_max(out=P.WA[:, 0:128],
                         in_=P.gf(R2, 128, 8192).rearrange("p (c t) -> p t c", c=64),
                         axis=AX.X, apply_absolute_value=True), P.sv, 1)
        P.sec("vector", f_vamax)

        sp1 = P.g(P.sp)

        def f_tp(t):
            t.wait_ge(P.sv, sv1 + 1)
            ins = t.transpose(P.pC[:, 0:128], P.WA[:, 0:128], P.c_idf)
            P.inc(ins, P.sp, 1)
        P.sec("tensor", f_tp)

        sv2 = P.g(P.sv)

        def f_vscale(v):
            v.wait_ge(P.sp, sp1 + 1)
            P.fence(v, v.reduce_max(out=P.t_tmp[:, 0:1], in_=P.pC[:, 0:128],
                                    axis=AX.X))
            P.fence(v, v.tensor_tensor(out=P.t_tmp[:, 1:2], in0=P.t_tmp[:, 0:1],
                            in1=P.t_c[:, tq:tq + 1], op=ALU.mult))
            P.fence(v, v.tensor_scalar(out=P.t_tmp[:, 1:2], in0=P.t_tmp[:, 1:2],
                            scalar1=1.0 / SQ8, scalar2=EPS,
                            op0=ALU.mult, op1=ALU.max))
            v.tensor_scalar(out=P.t_e[:, tq:tq + 1], in0=P.t_tmp[:, 1:2],
                            scalar1=P.c_scl[:, 5:6], scalar2=0.5 / 127.0,
                            op0=ALU.mult, op1=ALU.mult)
            P.fence(v, v.reciprocal(out=P.t_tmp[:, 2:3], in_=P.t_tmp[:, 1:2]))
            P.fence(v, v.tensor_tensor(out=P.t_tmp[:, 2:3], in0=P.t_tmp[:, 2:3],
                            in1=P.t_c[:, tq:tq + 1], op=ALU.mult))
            P.inc(v.tensor_scalar(out=P.t_tmp[:, 2:3], in0=P.t_tmp[:, 2:3],
                                  scalar1=127.0 / SQ8, scalar2=None,
                                  op0=ALU.mult), P.sv, 1)
        P.sec("vector", f_vscale)

        sp2 = P.g(P.sp)

        def f_tp2(t):
            t.wait_ge(P.sv, sv2 + 1)
            P.inc(t.transpose(P.pC[0:1, 128:256], P.t_tmp[:, 2:3], P.c_idf),
                  P.sp, 1)
        P.sec("tensor", f_tp2)

        sv3 = P.g(P.sv)

        def f_vrow(v):
            v.wait_ge(P.sp, sp2 + 1)
            P.inc(v.tensor_copy(out=P.ROW[0:1, 0:128], in_=P.pC[0:1, 128:256]),
                  P.sv, 1)
        P.sec("vector", f_vrow)

        sp3 = P.g(P.sp)

        def f_rep(t):
            t.wait_ge(P.sv, sv3 + 1)
            P.inc(t.matmul(P.pC[:, 256:384], P.c_onesrow, P.ROW[0:1, 0:128],
                           start=True, stop=True), P.sp, 1)
        P.sec("tensor", f_rep)

        def f_vround(v):
            v.wait_ge(P.sp, sp3 + 1)
            runs = []
            c = 0
            while c < 64:
                c2 = c + 1
                while (c2 < 64 and zf[c2] >= 0 and zf[c2 - 1] >= 0
                       and zf[c2] == zf[c2 - 1] + CSZ):
                    c2 += 1
                runs.append((c, c2))
                c = c2
            ins = None
            for (a, b) in runs:
                n = b - a
                if n == 1:
                    out = P.zi16(zf[a])[:, None, :].broadcast_to([128, 1, 1024]) \
                        if False else P.zi16(zf[a]).rearrange(
                            "p (c t) -> p c t", c=1)[:, :, tsl]
                else:
                    out = P.G[:, zf[a]:zf[a] + n * CSZ].bitcast(I16) \
                        .rearrange("p (c t) -> p c t", c=n)[:, :, tsl]
                in0 = P.gf(R2, 128, 8192).rearrange(
                    "p (c t) -> p c t", c=64)[:, a:b, :]
                in1 = P.pC[:, 256:384][:, None, :].broadcast_to([128, n, 128])
                ins = v.tensor_tensor(out=out, in0=in0, in1=in1, op=ALU.mult)
            P.inc(ins, P.sv, 1)
        P.sec("vector", f_vround)

    def quant8_fin(self):
        P = self
        zf = self.z_final
        sv0 = P.g(P.sv)

        def f_v(v):
            v.wait_ge(P.sv, P.g(P.sv))
            runs = []
            c = 0
            while c < 64:
                c2 = c + 1
                while (c2 < 64 and zf[c2] >= 0 and zf[c2 - 1] >= 0
                       and zf[c2] == zf[c2 - 1] + CSZ):
                    c2 += 1
                runs.append((c, c2))
                c = c2
            ins = None
            for (a, b) in runs:
                n = b - a
                if n == 1:
                    ins = v.tensor_copy(out=P.zbf(zf[a]), in_=P.zi16(zf[a]))
                else:
                    ins = v.tensor_copy(
                        out=P.G[:, zf[a]:zf[a] + n * CSZ].bitcast(BF16),
                        in_=P.G[:, zf[a]:zf[a] + n * CSZ].bitcast(I16))
            P.inc(ins, P.sv, 1)
        P.sec("vector", f_v)

    def down(self, call, x_old_src):
        P = self
        zf = self.z_final
        dst = P.x_spill if call == 1 else P.out

        self.col2row(P.t_e, P.ROW[0:1, 0:1024], 8)
        erep = P.gf(R2, 128, 1024)
        xold = [P.gf(R2 + 1024, 128, 1024), P.gf(R2 + 2048, 128, 1024)]
        xnew = [P.gf(R2 + 3072, 128, 1024), P.gf(R2 + 4096, 128, 1024)]
        tbuf = P.gf(R2 + 5120, 128, 1024)

        sp0 = P.g(P.sp)

        def f_erep(t):
            t.wait_ge(P.sv, P.g(P.sv))
            for hh in range(2):
                ins = t.matmul(P.pC[:, hh * 512:(hh + 1) * 512], P.c_onesrow,
                               P.ROW[0:1, hh * 512:(hh + 1) * 512],
                               start=True, stop=True)
            P.inc(ins, P.sp, 1)
        P.sec("tensor", f_erep)

        def f_verep(v):
            v.wait_ge(P.sp, sp0 + 1)
            P.inc(v.tensor_copy(out=erep, in_=P.pC[:, 0:1024]), P.sv, 1)
        P.sec("vector", f_verep)

        wv = P.ag_d_out.rearrange("(r m k q) n -> r m k q n", r=8, m=16, k=4)
        wbuf = [P.WA.bitcast(BF16)[:, 0:1024], P.WA.bitcast(BF16)[:, 1024:2048]]

        ds_dw = P.dstream("ddw")
        ds_dxo = P.dstream("dxo")
        ds_dxs = P.dstream("dxs")
        wl_done = {}
        ld_done = {}
        st_done = {}
        mm_done = {}
        ev_done = {}

        sd0 = P.g(P.sd)
        sv_base = P.g(P.sv)

        sd_c = P.g(P.sd)
        for m in range(16):
            for r in range(8):
                sd_c += 16
                wl_done[(m, r)] = sd_c
        sp_c = P.g(P.sp)
        for m in range(16):
            sp_c += 8
            mm_done[m] = sp_c
        sv_c = P.g(P.sv) + 1
        sv_c = P.g(P.sv)
        for m in range(16):
            sv_c += 2
            ev_done[m] = sv_c
        sg_c = P.g(P.sg)
        for m in range(16):
            sg_c += 16
            ld_done[m] = sg_c
            if m >= 2:
                sg_c += 16
                st_done[m - 2] = sg_c
        for m in (14, 15):
            sg_c += 16
            st_done[m] = sg_c

        def f_wload(sy):
            for m in range(16):
                for r in range(8):
                    i = m * 8 + r
                    if i >= 2:
                        m2, r2 = (i - 2) // 8, (i - 2) % 8
                        t_sp = P.g(P.sp) - 0
                        pass
                    P.inc(sy.dma_start(
                        out=wbuf[i % 2].rearrange("p (k j n) -> p k j n",
                                                  k=4, j=2),
                        in_=wv[r, m].rearrange("k (p j) n -> p k j n", p=128)),
                        P.sd, 16)
        mmr_done = {}
        sp_c = P.g(P.sp) + 1
        sp_c = P.g(P.sp)
        for m in range(16):
            for r in range(8):
                sp_c += 1
                mmr_done[(m, r)] = sp_c

        def f_wload2(sy):
            for m in range(16):
                for r in range(8):
                    i = m * 8 + r
                    if i >= 2:
                        m2, r2 = (i - 2) // 8, (i - 2) % 8
                        sy.wait_ge(P.sp, mmr_done[(m2, r2)])
                    ds_dw.load(sy,
                               wbuf[i % 2].rearrange("p (k j n) -> p k j n",
                                                     k=4, j=2),
                               wv[r, m].rearrange("k (p j) n -> p k j n", p=128),
                               i)
        P.sec("sync", f_wload2)

        def f_gx(g):
            for m in range(16):
                if m >= 2:
                    g.wait_ge(P.sv, ev_done[m - 2])
                ds_dxo.load(g, xold[m % 2],
                            x_old_src[m * 128:(m + 1) * 128, :], m)
                if m >= 2:
                    ds_dxs.load(g, dst[(m - 2) * 128:(m - 1) * 128, :],
                                xnew[(m - 2) % 2], m - 2)
            for m in (14, 15):
                g.wait_ge(P.sv, ev_done[m])
                ds_dxs.load(g, dst[m * 128:(m + 1) * 128, :], xnew[m % 2], m)
        P.sec("gpsimd", f_gx)

        def f_mm(t):
            for m in range(16):
                pp = P.pA if m % 2 == 0 else P.pB
                if m >= 2:
                    t.wait_ge(P.sv, ev_done[m - 2])
                for r in range(8):
                    ds_dw.wait(t, m * 8 + r)
                    for kp in range(4):
                        for j in range(2):
                            kc = r * 8 + kp * 2 + j
                            for hh in range(2):
                                ins = t.matmul(
                                    pp[:, hh * 512:(hh + 1) * 512],
                                    wbuf[(m * 8 + r) % 2].rearrange(
                                        "p (k j n) -> p k j n", k=4, j=2)
                                    [:, kp, j, :],
                                    P.zbf(zf[kc])
                                    [:, hh * 512:(hh + 1) * 512],
                                    start=(kc == 0), stop=(kc == 63))
                    P.inc(ins, P.sp, 1)
        P.sec("tensor", f_mm)

        def f_vevac(v):
            v.wait_ge(P.sv, P.g(P.sv))
            for m in range(16):
                v.wait_ge(P.sp, mmr_done[(m, 7)])
                ds_dxo.wait(v, m)
                if m >= 2:
                    ds_dxs.wait(v, m - 2)
                pp = P.pA if m % 2 == 0 else P.pB
                P.fence(v, v.tensor_tensor(out=tbuf, in0=pp[:, :], in1=erep,
                                           op=ALU.mult))
                n = P.inc(v.tensor_tensor(out=xnew[m % 2], in0=tbuf,
                                          in1=xold[m % 2], op=ALU.add), P.sv, 2)
                v.wait_ge(P.sv, n)
        P.sec("vector", f_vevac)
        if call == 2:
            self.out_streams = ds_dxs

    def attn(self):
        P = self
        self.stats(x_resident=False)

        sp0 = P.g(P.sp)

        self.col2row(P.t_rg, P.ROW[0:1, 0:1024], 8)

        def f_mmu(t):
            t.wait_ge(P.sv, P.g(P.sv))
            t.wait_ge(P.ss, P.g(P.ss))
            for hh in range(2):
                ins = t.matmul(P.pC[:, hh * 512:(hh + 1) * 512], P.c_onesrow,
                               P.ROW[0:1, hh * 512:(hh + 1) * 512],
                               start=True, stop=True)
            P.inc(ins, P.sp, 1)
        P.sec("tensor", f_mmu)

        xbufs = [P.gf(R3A, 128, 1024), P.gf(R3A + 1024, 128, 1024)]
        sv0 = P.g(P.sv)
        ss_pre, sp_pre = P.g(P.ss), P.g(P.sp)
        ds_xb = P.dstream("dst")

        def f_xs(sy):
            sy.wait_ge(P.ss, ss_pre)
            sy.wait_ge(P.sp, sp_pre)
            for c in range(16):
                if c >= 2:
                    sy.wait_ge(P.sv, sv0 + (c - 1))
                ds_xb.load(sy, xbufs[c % 2],
                           P.x_spill[c * 128:(c + 1) * 128, :], c)
        P.sec("sync", f_xs)

        xb = lambda c: P.gb(R0 + c * CSZ, 128, 1024)

        def f_vxb(v):
            v.wait_ge(P.sp, sp0 + 1)
            for c in range(16):
                ds_xb.wait(v, c)
                P.inc(v.tensor_sub(xb(c), xbufs[c % 2], P.pC[:, 0:1024]),
                      P.sv, 1)
        P.sec("vector", f_vxb)
        P.full_barrier()
        if self.aupto <= 1:
            return

        S = Scratch(P)
        bgk2rep = S.f(1024)
        ds_row = P.dstream("drow")

        def f_rows(sy):
            ds_row.load(sy, P.ROW[0:1, 0:KD], P.bgk2r, 0)
        P.sec("sync", f_rows)
        self._ds_row = ds_row

        spb = P.g(P.sp)

        def f_mmb(t):
            ds_row.wait(t, 0)
            for hh in range(2):
                ins = t.matmul(P.pC[:, hh * 512:(hh + 1) * 512], P.c_onesrow,
                               P.ROW[0:1, hh * 512:(hh + 1) * 512],
                               start=True, stop=True)
            P.inc(ins, P.sp, 1)
        P.sec("tensor", f_mmb)

        def f_vb(v):
            v.wait_ge(P.sp, spb + 1)
            P.inc(v.tensor_copy(out=bgk2rep, in_=P.pC[:, 0:1024]), P.sv, 1)
        P.sec("vector", f_vb)

        wbufb = [P.WA.bitcast(BF16)[:, 0:2048], P.WA.bitcast(BF16)[:, 2048:4096]]

        def proj_sweep(groups):
            ds_aw = P.dstream("daw")
            mmr = {}
            sp_c = P.g(P.sp)
            idx = [0]
            total = sum(n for _, n, _ in groups)
            for gi, (w, n, ef) in enumerate(groups):
                for oc in range(n):
                    sp_c += 1
                    mmr[idx[0]] = sp_c
                    idx[0] += 1
            ev = {}
            sv_c = P.g(P.sv)
            for i in range(total):
                sv_c += 1
                ev[i] = sv_c

            def f_wl(g):
                i = 0
                for w, n, ef in groups:
                    for oc in range(n):
                        if i >= 2:
                            g.wait_ge(P.sp, mmr[i - 2])
                        ds_aw.load(g,
                                   wbufb[i % 2].rearrange("p (k n) -> p k n",
                                                          k=16),
                                   w[:, oc * 128:(oc + 1) * 128].rearrange(
                                       "(k p) n -> p k n", p=128), i)
                        i += 1
            P.sec("gpsimd", f_wl)

            def f_pm(t):
                i = 0
                for w, n, ef in groups:
                    for oc in range(n):
                        ds_aw.wait(t, i)
                        if i >= 2:
                            t.wait_ge(P.sv, ev[i - 2])
                        pp = P.pA if i % 2 == 0 else P.pB
                        for kc in range(16):
                            for hh in range(2):
                                ins = t.matmul(
                                    pp[:, hh * 512:(hh + 1) * 512],
                                    wbufb[i % 2][:, kc * 128:(kc + 1) * 128],
                                    xb(kc)[:, hh * 512:(hh + 1) * 512],
                                    start=(kc == 0), stop=(kc == 15))
                        P.inc(ins, P.sp, 1)
                        i += 1
            P.sec("tensor", f_pm)

            def f_pv(v):
                i = 0
                for w, n, ef in groups:
                    for oc in range(n):
                        v.wait_ge(P.sp, mmr[i])
                        pp = P.pA if i % 2 == 0 else P.pB
                        P.inc(v.tensor_copy(out=ef(oc), in_=pp[:, :]), P.sv, 1)
                        i += 1
            P.sec("vector", f_pv)

        def transpose_phase(src_base, dst_base):
            sp_c = P.g(P.sp)
            tp = {}
            for oc in range(16):
                sp_c += 1
                tp[oc] = sp_c
            ev = {}
            sv_c = P.g(P.sv)
            for oc in range(16):
                sv_c += 1
                ev[oc] = sv_c

            def f_t(t):
                t.wait_ge(P.sv, P.g(P.sv))
                for oc in range(16):
                    if oc >= 2:
                        t.wait_ge(P.sv, ev[oc - 2])
                    half = P.pC[:, 0:1024] if oc % 2 == 0 else P.pC[:, 1024:2048]
                    for ti in range(8):
                        ins = t.transpose(
                            half.bitcast(BF16)[:, ti * 256:ti * 256 + 128]
                            if False else half[:, ti * 128:(ti + 1) * 128],
                            P.gb(src_base + oc * CSZ, 128, 1024)
                            [:, ti * 128:(ti + 1) * 128],
                            P.c_idbf)
                    P.inc(ins, P.sp, 1)
            P.sec("tensor", f_t)

            def f_v(v):
                for oc in range(16):
                    v.wait_ge(P.sp, tp[oc])
                    half = P.pC[:, 0:1024] if oc % 2 == 0 else P.pC[:, 1024:2048]
                    P.inc(v.tensor_copy(
                        out=P.gb(dst_base, 128, 16384).rearrange(
                            "p (ti n) -> p ti n", ti=8)[:, :, oc * 128:
                                                        (oc + 1) * 128],
                        in_=half.rearrange("p (ti n) -> p ti n", ti=8)),
                        P.sv, 1)
            P.sec("vector", f_v)

        def transpose_phase2(src_base, dst_base):
            sp_c = P.g(P.sp)
            tp = {}
            for oc in range(16):
                sp_c += 1
                tp[oc] = sp_c
            ev = {}
            sv_c = P.g(P.sv)
            for oc in range(16):
                sv_c += 1
                ev[oc] = sv_c
            pCb = P.pC.bitcast(BF16)

            def f_t(t):
                t.wait_ge(P.sv, P.g(P.sv))
                for oc in range(16):
                    if oc >= 2:
                        t.wait_ge(P.sv, ev[oc - 2])
                    base = 0 if oc % 2 == 0 else 1024
                    for ti in range(8):
                        ins = t.transpose(
                            pCb[:, base + ti * 128:base + (ti + 1) * 128],
                            P.gb(src_base + oc * CSZ, 128, 1024)
                            [:, ti * 128:(ti + 1) * 128],
                            P.c_idbf)
                    P.inc(ins, P.sp, 1)
            P.sec("tensor", f_t)

            def f_v(v):
                for oc in range(16):
                    v.wait_ge(P.sp, tp[oc])
                    base = 0 if oc % 2 == 0 else 1024
                    P.inc(v.tensor_copy(
                        out=P.gb(dst_base, 128, 16384).rearrange(
                            "p (ti n) -> p ti n", ti=8)[:, :, oc * 128:
                                                        (oc + 1) * 128],
                        in_=pCb[:, base:base + 1024].rearrange(
                            "p (ti n) -> p ti n", ti=8)), P.sv, 1)
            P.sec("vector", f_v)

        q1 = lambda oc: P.gb(8192 + oc * CSZ, 128, 1024)
        k1 = lambda oc: P.gb(12288 + oc * CSZ, 128, 1024)
        stage = lambda oc: P.gb(R3A + oc * CSZ, 128, 1024)

        if self.aupto <= 2:
            return
        proj_sweep([(P.wq, 8, q1), (P.wk, 8, k1), (P.wv, 16, stage)])
        P.full_barrier()
        if self.aupto <= 3:
            return
        transpose_phase2(R3A, R1)
        P.full_barrier()
        if self.aupto <= 4:
            return
        proj_sweep([(P.wgx, 16, stage)])
        P.full_barrier()
        transpose_phase2(R3A, R2)
        P.full_barrier()
        if self.aupto <= 5:
            return

        self.attn_gk(bgk2rep, S)
        P.full_barrier()
        if self.aupto <= 6:
            return

        self.attn_core()

    def attn_gk(self, bgk2rep, S):
        P = self
        xb = lambda c: P.gb(R0 + c * CSZ, 128, 1024)
        upre = S.b(1024)
        ubuf = S.f(1024)
        wgk2b = S.b(1024)
        w1 = P.MISC[:, 0:128].bitcast(BF16)[:, 0:256]

        sg0 = P.g(P.sg)

        def f_wl(g):
            P.inc(g.dma_start(
                out=w1.rearrange("p (k n) -> p k n", k=16),
                in_=P.wgk1.rearrange("(k p) n -> p k n", p=128)), P.sg, 16)
            P.inc(g.dma_start(out=wgk2b[0:16, :], in_=P.wgk2), P.sg, 16)
        P.sec("gpsimd", f_wl)

        sp0 = P.g(P.sp)

        def f_mm1(t):
            t.wait_ge(P.sg, sg0 + 32)
            for kc in range(16):
                for hh in range(2):
                    ins = t.matmul(P.pA[0:16, hh * 512:(hh + 1) * 512],
                                   w1.rearrange("p (k n) -> p k n", k=16)[:, kc, :],
                                   xb(kc)[:, hh * 512:(hh + 1) * 512],
                                   start=(kc == 0), stop=(kc == 15))
            P.inc(ins, P.sp, 1)
        P.sec("tensor", f_mm1)

        sv0 = P.g(P.sv)

        def f_vu(v):
            v.wait_ge(P.sp, sp0 + 1)
            P.inc(v.tensor_copy(out=upre[0:16, :], in_=P.pA[0:16, :]), P.sv, 1)
        P.sec("vector", f_vu)

        sp1, sv1, ss0 = P.g(P.sp), P.g(P.sv), P.g(P.ss)

        def f_mm2(t):
            t.wait_ge(P.sv, sv1)
            for ti in range(8):
                if ti >= 2:
                    t.wait_ge(P.sv, sv1 + 2 * (ti - 1))
                pp = P.pA if ti % 2 == 0 else P.pB
                for hh in range(2):
                    ins = t.matmul(pp[:, hh * 512:(hh + 1) * 512],
                                   upre[0:16, ti * 128:(ti + 1) * 128],
                                   wgk2b[0:16, hh * 512:(hh + 1) * 512],
                                   start=True, stop=True)
                    P.inc(ins, P.sp, 1)
        P.sec("tensor", f_mm2)

        def f_vu2(v):
            for ti in range(8):
                v.wait_ge(P.sp, sp1 + 2 * (ti + 1))
                if ti >= 1:
                    v.wait_ge(P.ss, ss0 + ti)
                pp = P.pA if ti % 2 == 0 else P.pB
                P.fence(v, v.tensor_scalar(out=ubuf, in0=pp[:, :],
                                scalar1=P.t_r[:, ti:ti + 1], scalar2=None,
                                op0=ALU.mult))
                P.inc(v.tensor_add(ubuf, ubuf, bgk2rep), P.sv, 2)
        P.sec("vector", f_vu2)

        gkn = lambda ti: P.gb(R0 + ti * CSZ, 128, 1024)

        def f_su(s_):
            for ti in range(8):
                s_.wait_ge(P.sv, sv1 + 2 * (ti + 1))
                P.fence(s_, s_.activation(out=ubuf, in_=ubuf, func=AF.Exp,
                                          scale=-1.0), "s")
                P.inc(s_.activation(out=gkn(ti), in_=ubuf,
                                    func=AF.Ln, bias=1.0), P.ss, 1)
        P.sec("scalar", f_su)
        self.gkn = gkn

    def attn_core(self):
        P = self
        gkn = self.gkn
        SL = 4096
        s_loc = P.gf(SL, 128, 1024)
        s_init = P.gf(SL + 1024, 128, 1024)
        s_bar = [P.gb(SL + 2048, 128, 1024),
                 P.gb(R3X + 1536, 128, 1024)]
        tmA = lambda b: P.gb(SL + 2560, 128, 1024)
        tmB = lambda b: P.gb(SL + 3072, 128, 1024)
        eaqf = P.gb(SL + 3584, 128, 512)
        eakf = P.gb(SL + 3840, 128, 512)
        kbarf = P.gb(R3X, 128, 512)
        kbtm = P.gb(R3X + 256, 128, 1024)
        pbar = P.gb(R3X + 768, 128, 1024)
        qhat = P.gb(R3X + 1280, 128, 512)
        gath = [P.gf(R3X + 1536, 128, 1024), P.gf(R3X + 2560, 128, 1024)]

        q1v = P.gb(8192, 128, 8192).rearrange("p (kc n) -> p kc n", kc=8)
        k1v = P.gb(12288, 128, 8192).rearrange("p (kc n) -> p kc n", kc=8)
        vtm = P.gb(R1, 128, 16384).rearrange("p (ti n) -> p ti n", ti=8)
        gtm = P.gb(R2, 128, 16384).rearrange("p (ti n) -> p ti n", ti=8)
        otm = P.gb(R3A, 128, 16384).rearrange("p (ti n) -> p ti n", ti=8)
        eG = P.MISC[:, 128:256].rearrange("p (a b) -> p a b", a=8)
        eA = P.MISC[:, 256:384].rearrange("p (a b) -> p a b", a=8)
        g16b = P.MISC[0:16, 384:896].bitcast(BF16)

        sv0, ss0 = P.g(P.sv), P.g(P.ss)

        def f_vsg(v):
            for ti in range(8):
                P.inc(v.tensor_scalar(out=gtm[:, ti, :], in0=gtm[:, ti, :],
                                      scalar1=P.t_r[:, ti:ti + 1], scalar2=None,
                                      op0=ALU.mult), P.sv, 1)
        P.sec("vector", f_vsg)

        def f_ssg(s_):
            s_.wait_ge(P.sv, sv0 + 8)
            P.inc(s_.activation(out=P.gb(R2, 128, 16384), in_=P.gb(R2, 128, 16384),
                                func=AF.Silu), P.ss, 1)
        P.sec("scalar", f_ssg)
        if self.aupto == 61:
            P.full_barrier()
            return

        sp0 = P.g(P.sp)

        def f_gpre(t):
            t.wait_ge(P.ss, P.g(P.ss))
            for j in range(16):
                b = (j % 2) * 64
                pp = P.pA if j % 2 == 0 else P.pB
                for hh in range(2):
                    ins = t.matmul(pp[0:16, hh * 512:(hh + 1) * 512],
                                   P.c_oh[b:b + 64, j * 16:(j + 1) * 16],
                                   gkn(j // 2)[b:b + 64, hh * 512:(hh + 1) * 512],
                                   start=(j // 2 == 0), stop=(j // 2 == 7))
            P.inc(ins, P.sp, 1)
        P.sec("tensor", f_gpre)

        sv1 = P.g(P.sv)

        def f_vg16(v):
            v.wait_ge(P.sp, sp0 + 1)
            P.fence(v, v.tensor_copy(out=g16b[0:16, :], in_=P.pA[0:16, :]))
            P.inc(v.tensor_tensor(out=g16b[0:16, :], in0=g16b[0:16, :],
                                  in1=P.pB[0:16, :], op=ALU.add), P.sv, 1)
        P.sec("vector", f_vg16)
        if self.aupto == 62:
            P.full_barrier()
            return

        sp1 = P.g(P.sp)

        def f_aex(t):
            t.wait_ge(P.sv, sv1 + 1)
            for hh in range(2):
                ins = t.matmul(P.pB[0:16, hh * 512:(hh + 1) * 512], P.c_su16,
                               g16b[0:16, hh * 512:(hh + 1) * 512],
                               start=True, stop=True)
            P.inc(ins, P.sp, 1)
            for bb in range(8):
                t.transpose(P.pC.bitcast(BF16)[:, bb * 16:(bb + 1) * 16],
                            g16b[0:16, bb * 128:(bb + 1) * 128],
                            P.c_idbf[0:16, 0:16])
            P.inc(t.matmul(P.pA[0:1, 0:1], P.c_ones1, P.c_ones1,
                           start=True, stop=True), P.sp, 1)
        P.sec("tensor", f_aex)
        if self.aupto == 63:
            P.full_barrier()
            return

        sv2 = P.g(P.sv)
        ss1 = P.g(P.ss)

        def f_veg(v):
            v.wait_ge(P.sp, sp1 + 2)
            P.inc(v.tensor_copy(out=P.MISC[:, 128:256],
                                in_=P.pC.bitcast(BF16)[:, 0:128]), P.sv, 1)
        P.sec("vector", f_veg)

        sp2 = P.g(P.sp)

        def f_aex2(t):
            t.wait_ge(P.sv, sv2 + 1)
            pass
        aexb = P.MISC[0:16, 896:1408].bitcast(BF16)

        def f_vaex(v):
            v.wait_ge(P.sp, sp1 + 2)
            P.inc(v.tensor_copy(out=aexb[0:16, :], in_=P.pB[0:16, :]), P.sv, 1)
        P.sec("vector", f_vaex)
        if self.aupto == 64:
            P.full_barrier()
            return

        sp3 = P.g(P.sp)
        sv3 = P.g(P.sv)

        def f_taex(t):
            t.wait_ge(P.sv, sv3)
            for bb in range(8):
                ins = t.transpose(P.pC.bitcast(BF16)[:, 256 + bb * 16:
                                                     256 + (bb + 1) * 16],
                                  aexb[0:16, bb * 128:(bb + 1) * 128],
                                  P.c_idbf[0:16, 0:16])
            P.inc(ins, P.sp, 1)
        P.sec("tensor", f_taex)

        sv4 = P.g(P.sv)

        def f_veA(v):
            v.wait_ge(P.sp, sp3 + 1)
            P.inc(v.tensor_copy(out=P.MISC[:, 256:384],
                                in_=P.pC.bitcast(BF16)[:, 256:384]), P.sv, 1)
        P.sec("vector", f_veA)

        ss2 = P.g(P.ss)

        def f_seg(s_):
            s_.wait_ge(P.sv, P.g(P.sv))
            s_.activation(out=P.MISC[:, 128:256], in_=P.MISC[:, 128:256],
                          func=AF.Exp, scale=-1.0)
            P.inc(s_.activation(out=P.MISC[:, 256:384], in_=P.MISC[:, 256:384],
                                func=AF.Exp, scale=-1.0), P.ss, 1)
        P.sec("scalar", f_seg)

        sv5 = P.g(P.sv)

        def f_vz(v):
            P.inc(v.memset(s_loc[:, :], 0.0), P.sv, 1)
        P.sec("vector", f_vz)
        P.full_barrier()
        self.snap("gkn", P.gb(R0, 128, 8192))
        self.snap("eg", P.MISC[:, 128:384])
        if self.aupto <= 7:
            return

        self._s1 = {k: {} for k in
                    ("exp", "tp", "ev", "kt", "kv", "mp", "mask", "o",
                     "sup", "oev", "ointer", "oadd")}
        nchunks_s1 = NGC if not (80 <= self.aupto < 80 + NGC) else \
            (self.aupto - 80)
        if self.aupto > 100:
            nchunks_s1 = 1 if self.aupto < 200 else 2
        if self.aupto in (1071, 1072):
            nchunks_s1 = 1
        for j in range(nchunks_s1):
            self._sweep1_chunk(j, s_loc, s_bar, tmA, tmB, eaqf, eakf, kbarf,
                               kbtm, pbar, q1v, k1v, vtm, otm, eG)

        P.full_barrier()
        self.snap("sloc", s_loc[:, :])
        self.snap("otm1", P.gb(R3A, 128, 16384))
        if self.aupto <= 8 or (80 <= self.aupto < 97) or self.aupto > 100:
            return

        sg1 = P.g(P.sg)

        def f_gag(g):
            P.inc(g.dma_start(out=P.ag_m_in, in_=s_loc[:, :]), P.sg, 16)
            g.wait_ge(P.sg, P.g(P.sg))
            P.inc(g.collective_compute(
                "AllGather", ALU.bypass, replica_groups=[list(range(8))],
                ins=[P.ag_m_in.opt()], outs=[P.ag_m_out.opt()]), P.sc, 1)
            g.wait_ge(P.sc, P.g(P.sc))
            for rr in range(8):
                if rr >= 2:
                    g.wait_ge(P.sv, self._agv0 + (rr - 1))
                self._ds_g.load(g, gath[rr % 2],
                                P.ag_m_out[rr * 128:(rr + 1) * 128, :], rr)
        self._agv0 = P.g(P.sv)
        self._ds_g = P.dstream("dg")
        P.sec("gpsimd", f_gag)

        sv6 = P.g(P.sv)

        def f_vmask(v):
            v.memset(s_init[:, :], 0.0)
            for rr in range(8):
                self._ds_g.wait(v, rr)
                P.fence(v, v.tensor_scalar(out=gath[rr % 2], in0=gath[rr % 2],
                                scalar1=P.c_mask[:, rr:rr + 1], scalar2=None,
                                op0=ALU.mult))
                P.inc(v.tensor_add(s_init[:, :], s_init[:, :], gath[rr % 2]),
                      P.sv, 1)
            P.fence(v, ins_last_add := v.tensor_copy(out=P.TMS[:, 48:49],
                                                     in_=P.TMS[:, 48:49]))
            P.inc(v.tensor_copy(out=s_bar[0], in_=s_init[:, :]), P.sv, 1)
        P.sec("vector", f_vmask)
        P.full_barrier()

        self._s2 = {k: {} for k in ("q", "mm", "add")}
        for j in range(NGC):
            self._sweep2_chunk(j, s_bar[0], qhat, q1v, otm, eA)
        P.full_barrier()
        self.snap("otm2", P.gb(R3A, 128, 16384))
        if self.aupto <= 9:
            return

        self.attn_epilogue(otm, gtm)

    def _sweep1_chunk(self, j, s_loc, s_bar, tmA, tmB, eaqf, eakf, kbarf,
                      kbtm, pbar, q1v, k1v, vtm, otm, eG):
        P = self
        b = (j % 2) * 64
        ti = j // 2
        gkn = self.gkn
        pAb = P.pA.bitcast(BF16)

        st = self._s1

        sp0 = P.g(P.sp)

        def f_cs(t):
            if j >= 1:
                t.wait_ge(P.sv, st["mask"][j - 1])
            for hh in range(2):
                ins = t.matmul(P.pA[b:b + 64, hh * 512:(hh + 1) * 512],
                               P.c_u64s[b:b + 64, :],
                               gkn(ti)[b:b + 64, hh * 512:(hh + 1) * 512],
                               start=True, stop=True)
            P.inc(ins, P.sp, 1)
        P.sec("tensor", f_cs)
        if self.aupto % 100 == 1 and 100 < self.aupto < 1000:
            return

        ss0 = P.g(P.ss)

        def f_exp(s_):
            s_.wait_ge(P.sp, sp0 + 1)
            if j >= 2:
                s_.wait_ge(P.sp, st["tp"][j - 2])
            s_.activation(out=tmA(b)[b:b + 64, :], in_=P.pA[b:b + 64, :],
                          func=AF.Exp, scale=-1.0,
                          bias=P.t_lnr8[b:b + 64, ti:ti + 1])
            P.inc(s_.activation(out=tmB(b)[b:b + 64, :], in_=P.pA[b:b + 64, :],
                                func=AF.Exp, scale=1.0,
                                bias=P.t_lnr2[b:b + 64, ti:ti + 1]), P.ss, 1)
        P.sec("scalar", f_exp)
        if self.aupto % 100 == 2 and 100 < self.aupto < 1000:
            return
        st["exp"][j] = P.g(P.ss)

        def f_tp(t):
            t.wait_ge(P.ss, st["exp"][j])
            for bb in range(8):
                t.transpose(pAb[:, bb * 64:(bb + 1) * 64],
                            tmA(b)[b:b + 64, bb * 128:(bb + 1) * 128],
                            P.c_idbf[b:b + 64, b:b + 64])
            for bb in range(8):
                ins = t.transpose(pAb[:, 1024 + bb * 64:1024 + (bb + 1) * 64],
                                  tmB(b)[b:b + 64, bb * 128:(bb + 1) * 128],
                                  P.c_idbf[b:b + 64, b:b + 64])
            P.inc(ins, P.sp, 1)
        P.sec("tensor", f_tp)
        if self.aupto % 100 == 3 and 100 < self.aupto < 1000:
            return
        st["tp"][j] = P.g(P.sp)

        def f_ev(v):
            v.wait_ge(P.sp, st["tp"][j])
            v.tensor_copy(out=eaqf, in_=pAb[:, 0:512])
            P.fence(v, v.tensor_copy(out=eakf, in_=pAb[:, 1024:1536]))
            qs = q1v[:, :, j * 64:(j + 1) * 64]
            ks = k1v[:, :, j * 64:(j + 1) * 64]
            ea3 = eaqf.rearrange("p (a c) -> p a c", a=8)
            ek3 = eakf.rearrange("p (a c) -> p a c", a=8)
            v.tensor_tensor(out=qs, in0=qs, in1=ea3, op=ALU.mult)
            P.fence(v, v.tensor_tensor(out=ks, in0=ks, in1=ek3, op=ALU.mult))
            kb3 = kbarf.rearrange("p (a c) -> p a c", a=8)
            for hp in range(8):
                ins = v.tensor_scalar(out=kb3[:, hp, :], in0=ks[:, hp, :],
                                      scalar1=eG[:, hp, j:j + 1], scalar2=None,
                                      op0=ALU.mult)
            P.inc(ins, P.sv, 1)
        P.sec("vector", f_ev)
        if self.aupto % 100 == 4 and 100 < self.aupto < 1000:
            return
        st["ev"][j] = P.g(P.sv)

        def f_kt(t):
            t.wait_ge(P.sv, st["ev"][j])
            for bb in range(8):
                ins = t.transpose(pAb[b:b + 64, bb * 128:(bb + 1) * 128],
                                  kbarf[:, bb * 64:(bb + 1) * 64], P.c_idbf)
            P.inc(ins, P.sp, 1)
        P.sec("tensor", f_kt)
        if self.aupto % 100 == 5 and 100 < self.aupto < 1000:
            return
        st["kt"][j] = P.g(P.sp)

        sbj = s_bar[j % 2]

        def f_kv(v):
            v.wait_ge(P.sp, st["kt"][j])
            if j >= 2:
                v.wait_ge(P.sp, st["ointer"][(j - 2) // 2])
            v.tensor_copy(out=kbtm[b:b + 64, :], in_=pAb[b:b + 64, 0:1024])
            P.inc(v.tensor_copy(out=sbj, in_=s_loc[:, :]), P.sv, 1)
        P.sec("vector", f_kv)
        if self.aupto % 100 == 6 and 100 < self.aupto < 1000:
            return
        st["kv"][j] = P.g(P.sv)

        def f_mp(t):
            t.wait_ge(P.sv, st["kv"][j])
            if j >= 1:
                t.wait_ge(P.sv, st["sup"][j - 1])
            kb3 = kbtm.rearrange("p (a c) -> p a c", a=8)
            ins = None
            if self.aupto != 1072:
                for h in range(H):
                    hb = (h % 2) * 64
                    ins = t.matmul(
                        P.pB[hb:hb + 64, (h // 2) * 128:(h // 2 + 1) * 128],
                        kb3[b:b + 64, h // 2, (h % 2) * 64:(h % 2) * 64 + 64],
                        vtm[b:b + 64, ti, h * 128:(h + 1) * 128],
                        start=True, stop=True)
            if self.aupto != 1071:
                for h in range(H):
                    hb = (h % 2) * 64
                    qsh = P.gb(8192 + (h // 2) * 512 - 32, 128, 1152)[
                        hb:hb + 64, j * 64:j * 64 + 128]
                    ins = t.matmul(P.pA[hb:hb + 64,
                                        (h // 2) * 128:(h // 2 + 1) * 128],
                                   k1v[hb:hb + 64, h // 2,
                                       j * 64:(j + 1) * 64],
                                   qsh, start=True, stop=True)
            P.inc(ins, P.sp, 1)
        P.sec("tensor", f_mp)
        if self.aupto % 100 == 7 and 100 < self.aupto < 1000 or self.aupto in (1071, 1072):
            return
        st["mp"][j] = P.g(P.sp)

        def f_mask(v):
            v.wait_ge(P.sp, st["mp"][j])
            pb4 = pbar[b:b + 64, :].rearrange("p (g two c) -> p g two c",
                                              g=8, two=2)
            ins = None
            for par in range(2):
                pbp = par * 64
                in0 = P.pA[pbp:pbp + 64, :].rearrange(
                    "p (g c) -> p g c", g=8)[:, :, 64:128]
                m3 = P.c_u64m[pbp:pbp + 64, :][:, None, :].broadcast_to(
                    [64, 8, 64])
                ins = v.tensor_tensor(out=pb4[:, :, par, :], in0=in0,
                                      in1=m3, op=ALU.mult)
            P.inc(ins, P.sv, 1)
        P.sec("vector", f_mask)
        if self.aupto % 100 == 8 and 100 < self.aupto < 1000:
            return
        st["mask"][j] = P.g(P.sv)

        def f_o(t):
            t.wait_ge(P.sv, st["mask"][j])
            if j >= 2:
                t.wait_ge(P.sv, st["oadd"][(j - 2) // 2])
            pb3 = pbar.rearrange("p (a c) -> p a c", a=16)
            for h in range(H):
                ins = t.matmul(P.pC[b:b + 64, h * 128:(h + 1) * 128],
                               pb3[b:b + 64, h, :],
                               vtm[b:b + 64, ti, h * 128:(h + 1) * 128],
                               start=True, stop=True)
            P.inc(ins, P.sp, 1)
        P.sec("tensor", f_o)
        if self.aupto % 100 == 9 and 100 < self.aupto < 1000:
            return
        st["o"][j] = P.g(P.sp)

        def f_sup(v):
            v.wait_ge(P.sp, st["mp"][j])
            v.wait_ge(P.sv, st["kv"][j])
            sl3 = s_loc.rearrange("p (a c) -> p a c", a=8)
            ins = None
            for hp in range(8):
                ins = v.tensor_scalar(out=sl3[:, hp, :], in0=sl3[:, hp, :],
                                scalar1=eG[:, hp, j:j + 1], scalar2=None,
                                op0=ALU.mult)
            P.fence(v, ins)
            P.inc(v.tensor_add(s_loc[:, :], s_loc[:, :], P.pB[:, :]), P.sv, 1)
        P.sec("vector", f_sup)
        if self.aupto % 100 == 10 and 100 < self.aupto < 1000:
            return
        st["sup"][j] = P.g(P.sv)

        if j % 2 == 1:
            def f_oev(s_):
                s_.wait_ge(P.sp, st["o"][j])
                P.inc(s_.activation(out=otm[:, ti, :], in_=P.pC[:, :],
                                    func=AF.Copy), P.ss, 1)
            P.sec("scalar", f_oev)
            st["oev"][ti] = P.g(P.ss)
            if self.aupto % 100 == 11 and 100 < self.aupto < 1000:
                return

            def f_ointer(t):
                t.wait_ge(P.ss, st["oev"][ti])
                t.wait_ge(P.sv, st["kv"][j])
                for jj, bb in ((j - 1, 0), (j, 64)):
                    qs = q1v[:, :, jj * 64:(jj + 1) * 64]
                    sb3 = s_bar[jj % 2].rearrange("p (a c) -> p a c", a=8)
                    for h in [x for x in range(H) if x % 2 == 0] + \
                             [x for x in range(H) if x % 2 == 1]:
                        hb = (h % 2) * 64
                        ins = t.matmul(P.pC[bb:bb + 64, h * 128:(h + 1) * 128],
                                       qs[hb:hb + 64, h // 2, :],
                                       sb3[hb:hb + 64, h // 2, :],
                                       start=True, stop=True)
                P.inc(ins, P.sp, 1)
            P.sec("tensor", f_ointer)
            st["ointer"][ti] = P.g(P.sp)
            if self.aupto % 100 == 12 and 100 < self.aupto < 1000:
                return

            def f_oadd(v):
                v.wait_ge(P.sp, st["ointer"][ti])
                P.inc(v.tensor_tensor(out=otm[:, ti, :], in0=otm[:, ti, :],
                                      in1=P.pC[:, :], op=ALU.add), P.sv, 1)
            P.sec("vector", f_oadd)
            st["oadd"][ti] = P.g(P.sv)

    def _sweep2_chunk(self, j, s_bar, qhat, q1v, otm, eA):
        P = self
        b = (j % 2) * 64
        ti = j // 2
        st = self._s2

        def f_q(v):
            if j >= 2:
                v.wait_ge(P.sp, st["mm"][j - 2])
            q3 = qhat.rearrange("p (a c) -> p a c", a=8)
            qs = q1v[:, :, j * 64:(j + 1) * 64]
            for hp in range(8):
                ins = v.tensor_scalar(out=q3[:, hp, :], in0=qs[:, hp, :],
                                      scalar1=eA[:, hp, j:j + 1], scalar2=None,
                                      op0=ALU.mult)
            P.inc(ins, P.sv, 1)
        P.sec("vector", f_q)
        st["q"][j] = P.g(P.sv)

        def f_mm(t):
            t.wait_ge(P.sv, st["q"][j])
            if j >= 2:
                t.wait_ge(P.sv, st["add"][j - 2])
            q3 = qhat.rearrange("p (a c) -> p a c", a=8)
            s3 = s_bar.rearrange("p (a c) -> p a c", a=8)
            for h in [x for x in range(H) if x % 2 == 0] + \
                     [x for x in range(H) if x % 2 == 1]:
                hb = (h % 2) * 64
                ins = t.matmul(P.pC[b:b + 64, h * 128:(h + 1) * 128],
                               q3[hb:hb + 64, h // 2, :],
                               s3[hb:hb + 64, h // 2, :],
                               start=True, stop=True)
            P.inc(ins, P.sp, 1)
        P.sec("tensor", f_mm)
        st["mm"][j] = P.g(P.sp)

        def f_add(v):
            v.wait_ge(P.sp, st["mm"][j])
            P.inc(v.tensor_tensor(out=otm[b:b + 64, ti, :],
                                  in0=otm[b:b + 64, ti, :],
                                  in1=P.pC[b:b + 64, :], op=ALU.add), P.sv, 1)
        P.sec("vector", f_add)
        st["add"][j] = P.g(P.sv)

    def attn_epilogue(self, otm, gtm):
        P = self
        o2 = P.gb(8192, 128, 16384)
        sv0 = P.g(P.sv)

        def f_vo2(v):
            P.fence(v, v.tensor_tensor(out=o2, in0=P.gb(R3A, 128, 16384),
                                  in1=P.gb(R3A, 128, 16384), op=ALU.mult))
            P.fence(v, v.reduce_sum(out=P.WA[:, 0:128],
                               in_=o2.rearrange("p (s c) -> p s c", s=128),
                               axis=AX.X))
            P.inc(v.tensor_scalar(out=P.WA[:, 128:256], in0=P.WA[:, 0:128],
                                  scalar1=1.0 / 128.0, scalar2=None,
                                  op0=ALU.mult), P.sv, 3)
        P.sec("vector", f_vo2)

        ss0 = P.g(P.ss)

        def f_sr(s_):
            s_.wait_ge(P.sv, sv0 + 3)
            P.inc(s_.activation(out=P.WA[:, 128:256], in_=P.WA[:, 128:256],
                                func=AF.Ln, bias=P.c_eps[:, 0:1]), P.ss, 1)
        P.sec("scalar", f_sr)

        sv1 = P.g(P.sv)

        def f_vr(v):
            v.wait_ge(P.ss, ss0 + 1)
            P.inc(v.tensor_scalar(out=P.WA[:, 128:256], in0=P.WA[:, 128:256],
                                  scalar1=-0.5, scalar2=None, op0=ALU.mult),
                  P.sv, 1)
        P.sec("vector", f_vr)

        ss1 = P.g(P.ss)

        def f_se(s_):
            s_.wait_ge(P.sv, sv1 + 1)
            P.inc(s_.activation(out=P.WA[:, 128:256], in_=P.WA[:, 128:256],
                                func=AF.Exp), P.ss, 1)
        P.sec("scalar", f_se)

        ds_row = P.dstream("drow")

        nwstage = P.gf(R3X, 1, 2048)

        def f_nw(sy):
            ds_row.load(sy, nwstage, P.nwrow, 1)
        P.sec("sync", f_nw)

        sp0 = P.g(P.sp)

        def f_mmnw(t):
            ds_row.wait(t, 1)
            for hh in range(4):
                ins = t.matmul(P.pC[:, hh * 512:(hh + 1) * 512], P.c_onesrow,
                               nwstage[0:1, hh * 512:(hh + 1) * 512],
                               start=True, stop=True)
            P.inc(ins, P.sp, 1)
        P.sec("tensor", f_mmnw)

        sv2 = P.g(P.sv)

        def f_vap(v):
            v.wait_ge(P.ss, ss1 + 1)
            v.wait_ge(P.sp, sp0 + 1)
            r3 = P.WA[:, 128:256].rearrange("p (a c) -> p a c", a=8)
            for ti in range(8):
                v.tensor_tensor(
                    out=otm[:, ti, :].rearrange("p (a c) -> p a c", a=16),
                    in0=otm[:, ti, :].rearrange("p (a c) -> p a c", a=16),
                    in1=r3[:, ti, :][:, :, None].broadcast_to([128, 16, 128]),
                    op=ALU.mult)
                ins = v.tensor_tensor(out=gtm[:, ti, :], in0=gtm[:, ti, :],
                                in1=P.pC[:, :], op=ALU.mult)
            P.fence(v, ins)
            P.inc(v.tensor_tensor(out=P.gb(R3A, 128, 16384),
                                  in0=P.gb(R3A, 128, 16384),
                                  in1=P.gb(R2, 128, 16384), op=ALU.mult),
                  P.sv, 1)
        P.sec("vector", f_vap)
        P.full_barrier()
        self.snap("ofin", P.gb(R3A, 128, 16384))

        tp = {}
        sp1 = P.g(P.sp)
        sv3 = P.g(P.sv)
        pAb = P.pA.bitcast(BF16)
        pBb = P.pB.bitcast(BF16)

        def f_t(t):
            for oc in range(16):
                if oc >= 2:
                    t.wait_ge(P.sv, sv3 + (oc - 1))
                pb = pAb if oc % 2 == 0 else pBb
                for ti in range(8):
                    ins = t.transpose(pb[:, ti * 128:(ti + 1) * 128],
                                      otm[:, ti, oc * 128:(oc + 1) * 128],
                                      P.c_idbf)
                P.inc(ins, P.sp, 1)
        P.sec("tensor", f_t)

        def f_tv(v):
            for oc in range(16):
                v.wait_ge(P.sp, sp1 + oc + 1)
                pb = pAb if oc % 2 == 0 else pBb
                P.inc(v.tensor_copy(out=P.gb(R1 + oc * CSZ, 128, 1024),
                                    in_=pb[:, 0:1024]), P.sv, 1)
        P.sec("vector", f_tv)
        P.full_barrier()

        wbufb = [P.WA.bitcast(BF16)[:, 0:2048], P.WA.bitcast(BF16)[:, 2048:4096]]
        spw = P.g(P.sp)
        svw = P.g(P.sv)
        sdx = P.g(P.sd)
        ds_wo = P.dstream("daw")

        def f_wl(g):
            for oc in range(16):
                if oc >= 2:
                    g.wait_ge(P.sp, spw + (oc - 1))
                ds_wo.load(g,
                           wbufb[oc % 2].rearrange("p (k n) -> p k n", k=16),
                           P.wo[:, oc * 128:(oc + 1) * 128].rearrange(
                               "(k p) n -> p k n", p=128), oc)
        P.sec("gpsimd", f_wl)

        def f_xl(sy):
            for oc in range(16):
                P.inc(sy.dma_start(out=P.gf(R0 + oc * 1024, 128, 1024),
                                   in_=P.x_spill[oc * 128:(oc + 1) * 128, :]),
                      P.sd, 16)
        P.sec("sync", f_xl)
        sd_xl_total = P.g(P.sd)

        def f_mw(t):
            for oc in range(16):
                ds_wo.wait(t, oc)
                if oc >= 2:
                    t.wait_ge(P.sv, svw + (oc - 1))
                pp = P.pA if oc % 2 == 0 else P.pB
                for kc in range(16):
                    for hh in range(2):
                        ins = t.matmul(pp[:, hh * 512:(hh + 1) * 512],
                                       wbufb[oc % 2][:, kc * 128:(kc + 1) * 128],
                                       P.gb(R1 + kc * CSZ, 128, 1024)
                                       [:, hh * 512:(hh + 1) * 512],
                                       start=(kc == 0), stop=(kc == 15))
                P.inc(ins, P.sp, 1)
        P.sec("tensor", f_mw)

        def f_vres(v):
            v.wait_ge(P.sd, sd_xl_total)
            for oc in range(16):
                v.wait_ge(P.sp, spw + oc + 1)
                pp = P.pA if oc % 2 == 0 else P.pB
                P.inc(v.tensor_add(P.gf(R0 + oc * 1024, 128, 1024),
                                   P.gf(R0 + oc * 1024, 128, 1024), pp[:, :]),
                      P.sv, 1)
        P.sec("vector", f_vres)

    def ffn_core_fix(self):
        pass

    def ph_finish(self):
        P = self
        P.full_barrier()
        if self.debug and self.dbg:
            sdf = P.g(P.sd)

            def f_dbg(sy):
                n = 0
                for d, ap in self.dbg:
                    P.inc(sy.dma_start(out=d, in_=ap), P.sd, 16)
                    n += 1
                sy.wait_ge(P.sd, sdf + 16 * n)
            P.sec("sync", f_dbg)
        ds = getattr(self, "out_streams", None)

        def f_end(g):
            if ds is not None:
                for s in ds.sems:
                    g.wait_ge(s, P.g(s))
        P.sec("gpsimd", f_end)


def _hadamard(n):
    h = np.array([[1.0]], dtype=np.float64)
    while h.shape[0] < n:
        h = np.block([[h, h], [h, -h]])
    return h


def build_host_inputs(inputs, core):
    b, half = core // 2, core % 2
    x = np.asarray(inputs["L0_attn_x"], np.float32)
    wg = np.ascontiguousarray(np.asarray(inputs["L0_ffn_w_gate"], np.float32).T)
    wu = np.ascontiguousarray(np.asarray(inputs["L0_ffn_w_up"], np.float32).T)
    wd = np.ascontiguousarray(np.asarray(inputs["L0_ffn_w_down"], np.float32).T)

    xT = np.ascontiguousarray(x[b, half * 1024:(half + 1) * 1024, :].T)

    def gu_shard(wT):
        sh = wT[256 * core:256 * (core + 1)]
        a = sh.reshape(2, 128, 64, 128).transpose(0, 2, 1, 3)
        return a

    gsh = np.stack([gu_shard(wg), gu_shard(wu)])
    wgt_sh = np.ascontiguousarray(gsh.reshape(32768, 128))

    dsh = wd[1024 * core:1024 * (core + 1)]
    dd = dsh.reshape(4, 2, 128, 16, 128).transpose(3, 0, 2, 1, 4)
    wdt_sh = np.ascontiguousarray(dd.reshape(16384, 128))

    u64 = np.triu(np.ones((64, 64), np.float32))
    su16 = np.triu(np.ones((16, 16), np.float32), 1)
    oh = np.zeros((64, 256), np.float32)
    for jj in range(16):
        oh[:, jj * 16 + jj] = 1.0 / 16.0
    maskv = np.zeros((1, 8), np.float32)
    if half == 1:
        maskv[0, core - 1] = 1.0

    m = {
        "xT": xT,
        "wq": np.asarray(inputs["L0_attn_Wq"], np.float32),
        "wk": np.asarray(inputs["L0_attn_Wk"], np.float32),
        "wv": np.asarray(inputs["L0_attn_Wv"], np.float32),
        "wgx": np.asarray(inputs["L0_attn_Wg"], np.float32),
        "wo": np.asarray(inputs["L0_attn_Wo"], np.float32),
        "wgk1": np.asarray(inputs["L0_attn_Wgk1"], np.float32),
        "wgk2": np.asarray(inputs["L0_attn_Wgk2"], np.float32),
        "bgk2r": np.asarray(inputs["L0_attn_bgk2"], np.float32)[None, :],
        "nwrow": np.ascontiguousarray(
            np.tile(np.asarray(inputs["L0_attn_norm_w"], np.float32), 16))[None, :],
        "wgt_sh": wgt_sh,
        "wdt_sh": wdt_sh,
        "h128": _hadamard(128).astype(BF),
        "idbf": np.eye(128, dtype=np.float32).astype(BF),
        "idf32": np.eye(128, dtype=np.float32),
        "u64s": np.vstack([u64, u64]).astype(np.float32).astype(BF) * BF(1.0 / 16.0),
        "u64m": np.vstack([u64, u64]).astype(BF),
        "su16": su16.astype(BF),
        "oh16": np.vstack([oh, oh]).astype(BF),
        "maskv": maskv,
    }
    return m


_CACHE = {}


def get_nc(debug=False, upto=99, aupto=99):
    key = (debug, upto, aupto)
    if key not in _CACHE:
        kb = KB(debug=debug, upto=upto, aupto=aupto)
        _CACHE[key] = (kb, kb.build())
    return _CACHE[key]


def kernel(**inputs):
    kb, nc = get_nc()
    in_maps = [build_host_inputs(inputs, c) for c in range(8)]
    res = run_bass_kernel_spmd(nc, in_maps, core_ids=list(range(8)))
    out = np.zeros((4, 2048, 2048), np.float32)
    for c in range(8):
        b, half = c // 2, c % 2
        out[b, half * 1024:(half + 1) * 1024, :] = res.results[c]["out"].T
    return out


# version 02 (final)
# speedup vs baseline: 1.0258x; 1.0258x over previous
import math
import numpy as np
import ml_dtypes
import concourse.bass as bass
import concourse.mybir as mybir
from concourse.bass_utils import run_bass_kernel_spmd

F32 = mybir.dt.float32
BF16 = mybir.dt.bfloat16
I16 = mybir.dt.int16
AF = mybir.ActivationFunctionType
ALU = mybir.AluOpType
AX = mybir.AxisListType
BF = ml_dtypes.bfloat16

D, T, NCH, I8, ICH = 2048, 1024, 16, 8192, 64
H, DK, DV, KD, CH, NGC = 16, 64, 128, 1024, 64, 16
EPS = 1e-5
SQD, SQ8 = math.sqrt(2048.0), math.sqrt(8192.0)
LN8 = math.log(1.0 / 8.0)
GSIZE = 45568
CSZ = 512
R0, R1, R2, R3A, R3X = 0, 16384, 24576, 32768, 40960
R3X = 40960

R3X_END = GSIZE


def es_enter(es, cm):
    v = cm.__enter__()
    es.append(cm)
    return v


class DStream:

    _made = {}

    def __init__(self, kb, name, depth=2, incs=1):
        self.kb = kb
        self.depth = depth
        self.incs = incs
        key = (name, depth)
        if key not in DStream._made:
            sems = [es_enter(kb._es, kb.nc.semaphore(f"{name}{d}"))
                    for d in range(depth)]
            DStream._made[key] = sems
            if not hasattr(kb, "all_dsems"):
                kb.all_dsems = []
            kb.all_dsems.extend(sems)
        self.sems = DStream._made[key]
        self.base = [kb.g(s) for s in self.sems]

    def expected(self, i):
        return self.base[i % self.depth] + 16 * self.incs * (i // self.depth + 1)

    def load(self, eng, out, in_, i):
        s = self.sems[i % self.depth]
        if isinstance(out, (list, tuple)):
            assert len(out) == self.incs
            for o, ii in zip(out, in_):
                self.kb.inc(eng.dma_start(out=o, in_=ii), s, 16)
        else:
            assert self.incs == 1
            self.kb.inc(eng.dma_start(out=out, in_=in_), s, 16)
        assert self.kb.g(s) == self.expected(i), (self.kb.g(s), self.expected(i))

    def wait(self, eng, i):
        eng.wait_ge(self.sems[i % self.depth], self.expected(i))


class Scratch:

    def __init__(self, kb):
        self.kb = kb
        self.off = R3X

    def f(self, n):
        o = self.off
        self.off += n
        assert self.off <= R3X_END, f"R3X overflow {self.off}"
        return self.kb.gf(o, 128, n)

    def b(self, n):
        o = self.off
        self.off += (n + 1) // 2
        assert self.off <= R3X_END, f"R3X overflow {self.off}"
        return self.kb.gb(o, 128, n)

    def i(self, n):
        o = self.off
        self.off += (n + 1) // 2
        assert self.off <= R3X_END, f"R3X overflow {self.off}"
        return self.kb.gi(o, 128, n)


class KB:
    def __init__(self, debug=False, upto=99, aupto=99):
        self.debug = debug
        self.upto = upto
        self.aupto = aupto
        self.nc = bass.Bass("TRN2", target_bir_lowering=False)
        self.cnt = {}
        self.dbg = []
        self._es = []
        DStream._made = {}

    def g(self, sem):
        return self.cnt.get(sem.name, 0)

    def inc(self, inst, sem, n=1):
        inst.then_inc(sem, n)
        self.cnt[sem.name] = self.g(sem) + n
        return self.cnt[sem.name]

    def sec(self, name, fn):
        getattr(self.block, name)(fn)

    def dstream(self, name, depth=2, incs=1):
        return DStream(self, name, depth, incs)

    def fence(self, eng, ins, which="v"):
        sem = {"v": self.srv, "s": self.srs, "g": self.srg}[which]
        n = self.inc(ins, sem, 1)
        eng.wait_ge(sem, n)
        return ins

    def full_barrier(self):
        sems = [self.sd, self.sg, self.sp, self.sv, self.ss, self.sc,
                self.sgc]
        sems += getattr(self, "all_dsems", [])
        c = {s.name: self.g(s) for s in sems}
        sems = tuple(sems)
        for name in ("sync", "gpsimd", "tensor", "vector", "scalar"):
            def f(e, c=c, sems=sems):
                for s in sems:
                    if c[s.name] > 0:
                        e.wait_ge(s, c[s.name])
            self.sec(name, f)

    def gf(self, off, p, n):
        return self.G[0:p, off:off + n]

    def gb(self, off, p, n):
        return self.G[0:p, off:off + (n + 1) // 2].bitcast(BF16)[0:p, 0:n]

    def gi(self, off, p, n):
        return self.G[0:p, off:off + (n + 1) // 2].bitcast(I16)[0:p, 0:n]

    def zbf(self, slot):
        if slot < 0:
            off = (-slot - 1) * 512
            return self.WA[:, off:off + 512].bitcast(BF16)[:, 0:1024]
        return self.gb(slot, 128, 1024)

    def zi16(self, slot):
        if slot < 0:
            off = (-slot - 1) * 512
            return self.WA[:, off:off + 512].bitcast(I16)[:, 0:1024]
        return self.gi(slot, 128, 1024)

    def snap(self, name, ap):
        if not self.debug:
            return
        self.full_barrier()
        d = self.nc.dram_tensor("dbg_" + name, list(ap.shape), ap.dtype,
                                kind="ExternalOutput").ap()
        sd0 = self.g(self.sd)

        def f(sy):
            self.inc(sy.dma_start(out=d, in_=ap), self.sd, 16)
            sy.wait_ge(self.sd, sd0 + 16)
        self.sec("sync", f)
        self.full_barrier()

    def tap(self, name, ap, dtype=None, shape=None):
        if not self.debug:
            return
        dt = dtype or ap.dtype
        sh = shape or list(ap.shape)
        d = self.nc.dram_tensor("dbg_" + name, sh, dt, kind="ExternalOutput").ap()
        self.dbg.append((d, ap))

    def build(self):
        nc = self.nc
        din = lambda n, s, dt=F32: nc.dram_tensor(n, s, dt, kind="ExternalInput").ap()
        self.xT = din("xT", [D, T])
        self.wq = din("wq", [D, KD]); self.wk = din("wk", [D, KD])
        self.wv = din("wv", [D, D]); self.wgx = din("wgx", [D, D])
        self.wo = din("wo", [D, D])
        self.wgk1 = din("wgk1", [D, 16]); self.wgk2 = din("wgk2", [16, KD])
        self.bgk2r = din("bgk2r", [1, KD])
        self.nwrow = din("nwrow", [1, D])
        self.wgt_sh = din("wgt_sh", [32768, 128])
        self.wdt_sh = din("wdt_sh", [16384, 128])
        self.h128 = din("h128", [128, 128], BF16)
        self.idbf = din("idbf", [128, 128], BF16)
        self.idf32 = din("idf32", [128, 128])
        self.u64s = din("u64s", [128, 64], BF16)
        self.u64m = din("u64m", [128, 64], BF16)
        self.su16 = din("su16", [16, 16], BF16)
        self.oh16 = din("oh16", [128, 256], BF16)
        self.maskv = din("maskv", [1, 8])

        self.out = nc.dram_tensor("out", [D, T], F32, kind="ExternalOutput").ap()

        self.ag_s_in = nc.dram_tensor("ag_s_in", [1, 4], F32).ap()
        self.ag_s_out = nc.dram_tensor("ag_s_out", [8, 4], F32).ap()
        self.ag_w_in = nc.dram_tensor("ag_w_in", [32768, 128], BF16).ap()
        self.ag_w_out = nc.dram_tensor("ag_w_out", [8 * 32768, 128], BF16).ap()
        self.ag_d_in = nc.dram_tensor("ag_d_in", [16384, 128], BF16).ap()
        self.ag_d_out = nc.dram_tensor("ag_d_out", [8 * 16384, 128], BF16).ap()
        self.ag_m_in = nc.dram_tensor("ag_m_in", [128, KD], F32).ap()
        self.ag_m_out = nc.dram_tensor("ag_m_out", [1024, KD], F32).ap()
        self.x_spill = nc.dram_tensor("x_spill", [D, T], F32).ap()

        es = self._es
        sb = lambda n, s, dt=F32: es_enter(es, nc.sbuf_tensor(n, s, dt))
        self.G = sb("G", [128, GSIZE])
        self.WA = sb("WA", [128, 4096])
        self.MISC = sb("MISC", [128, 1408])
        self.ROW = sb("ROW", [1, 1152])
        self.TMS = sb("TMS", [128, 96])
        self.CB = sb("CB", [128, 658], BF16)
        self.CF = sb("CF", [128, 148])
        self.pA = es_enter(es, nc.psum_tensor("pA", [128, 1024], F32))
        self.pB = es_enter(es, nc.psum_tensor("pB", [128, 1024], F32))
        self.pC = es_enter(es, nc.psum_tensor("pC", [128, 2048], F32))
        for nm in ("sd", "sg", "sp", "sv", "ss", "sc", "sst", "srv", "srs",
                   "srg", "sgc"):
            setattr(self, nm, es_enter(es, nc.semaphore(nm)))
        self.block = es_enter(es, nc.Block())

        self.c_h128 = self.CB[:, 0:128]
        self.c_idbf = self.CB[:, 128:256]
        self.c_u64s = self.CB[:, 256:320]
        self.c_u64m = self.CB[:, 320:384]
        self.c_su16 = self.CB[0:16, 384:400]
        self.c_oh = self.CB[:, 400:656]
        self.c_e0 = self.CB[:, 656:657]
        self.c_onesb = self.CB[:, 657:658]
        self.c_idf = self.CF[:, 0:128]
        self.c_ones = self.CF[:, 128:129]
        self.c_ones1 = self.CF[0:1, 129:130]
        self.c_mask = self.CF[:, 130:138]
        self.c_scl = self.CF[:, 138:146]
        self.c_eps = self.CF[:, 146:147]
        self.c_onesrow = self.ROW[0:1, 1024:1152]

        t = self.TMS
        self.t_r = t[:, 0:8]; self.t_lnr8 = t[:, 8:16]; self.t_lnr2 = t[:, 16:24]
        self.t_amax = t[:, 24:32]; self.t_m = t[:, 32:40]; self.t_dq = t[:, 40:48]
        self.t_tmp = t[:, 48:56]; self.t_tmp2 = t[:, 56:64]; self.t_c = t[:, 64:72]
        self.t_e = t[:, 72:80]; self.t_lnr = t[:, 80:88]; self.t_rg = t[:, 88:96]

        self.ph_startup()
        if self.upto >= 1:
            self.ph_quant()
            self.tap("scl", self.c_scl[:, 0:6])
            self.tap("wsh", self.ag_w_in)
            self.tap("dsh", self.ag_d_in)
        if self.upto >= 2:
            self.full_barrier()
            self.ffn(1)
            self.tap("x1", self.x_spill)
            self.tap("tr", self.t_r)
            self.tap("tdq", self.t_dq)
        if self.upto >= 3:
            self.full_barrier()
            self.attn()
            self.tap("x2", self.gf(R0, 128, 16384))
        if self.upto >= 4:
            self.full_barrier()
            self.ffn(2)
        self.ph_finish()
        return nc

    def ph_startup(self):
        P = self

        def f_sync(sy):
            for d, s in ((P.c_h128, P.h128), (P.c_idbf, P.idbf),
                         (P.c_u64s, P.u64s), (P.c_u64m, P.u64m),
                         (P.c_su16, P.su16), (P.c_oh, P.oh16),
                         (P.c_idf, P.idf32)):
                P.inc(sy.dma_start(out=d, in_=s), P.sd, 16)
            P.inc(sy.dma_start(out=P.c_mask[0:1, :], in_=P.maskv), P.sd, 16)
            P.inc(sy.dma_start(
                out=P.gf(R0, 128, 16384).rearrange("p (c t) -> p c t", c=16),
                in_=P.xT.rearrange("(c p) t -> p c t", p=128)), P.sd, 16)
        P.sec("sync", f_sync)


        def f_vec(v):
            v.memset(P.c_ones, 1.0)
            v.memset(P.c_ones1, 1.0)
            v.memset(P.c_onesrow, 1.0)
            v.memset(P.c_onesb, 1.0)
            v.memset(P.c_eps, EPS)
            P.fence(v, v.memset(P.c_e0, 0.0))
            P.inc(v.memset(P.c_e0[0:1, 0:1], 1.0), P.sv, 1)
        P.sec("vector", f_vec)

        sd0, sv0 = P.g(P.sd), P.g(P.sv)

        def f_tensor(t):
            t.wait_ge(P.sd, sd0)
            t.wait_ge(P.sv, sv0)
            P.inc(t.matmul(P.pA[:, 0:8], P.c_onesrow, P.c_mask[0:1, :],
                           start=True, stop=True), P.sp, 1)
        P.sec("tensor", f_tensor)

        sp0 = P.g(P.sp)

        def f_vec2(v):
            v.wait_ge(P.sp, sp0)
            P.inc(v.tensor_copy(out=P.c_mask, in_=P.pA[:, 0:8]), P.sv, 1)
        P.sec("vector", f_vec2)

    def ph_quant(self):
        P = self
        NT = 96
        gu_in = P.wgt_sh.rearrange("(i p a) n -> i p (a n)", p=128, a=4)
        d_in = P.wdt_sh.rearrange("(i p a) n -> i p (a n)", p=128, a=4)
        gu_o = P.ag_w_in.rearrange("(i p a) n -> i p (a n)", p=128, a=4)
        d_o = P.ag_d_in.rearrange("(i p a) n -> i p (a n)", p=128, a=4)
        src = lambda i: gu_in[i] if i < 64 else d_in[i - 64]
        dst = lambda i: gu_o[i] if i < 64 else d_o[i - 64]

        bufs = [P.gf(R1, 128, 512), P.gf(R1 + 512, 128, 512)]
        obufs = [P.gb(R1 + 1024, 128, 512), P.gb(R1 + 1280, 128, 512)]
        i16 = P.gi(R1 + 1536, 128, 512)

        ss0 = P.g(P.ss)
        sv_pre = P.g(P.sv)
        ds_q = P.dstream("dq")

        def f_sync(sy):
            for i in range(NT):
                if i >= 2:
                    sy.wait_ge(P.ss, ss0 + (i - 1))
                ds_q.load(sy, bufs[i % 2], src(i), i)
        P.sec("sync", f_sync)

        def f_scal(s_):
            for i in range(NT):
                ds_q.wait(s_, i)
                P.inc(s_.activation(out=bufs[i % 2], in_=bufs[i % 2], func=AF.Abs,
                                    accum_out=P.WA[:, i:i + 1]), P.ss, 1)
        P.sec("scalar", f_scal)

        sp0 = P.g(P.sp)

        def f_tensor(t):
            t.wait_ge(P.ss, ss0 + NT)
            t.wait_ge(P.sv, sv_pre)
            P.inc(t.matmul(P.pA[0:1, 0:NT], P.c_ones, P.WA[:, 0:NT],
                           start=True, stop=True), P.sp, 1)
        P.sec("tensor", f_tensor)

        sv0 = P.g(P.sv)

        def f_vec(v):
            v.wait_ge(P.sp, sp0 + 1)
            v.reduce_sum(out=P.ROW[0:1, 0:1], in_=P.pA[0:1, 0:32], axis=AX.X)
            v.reduce_sum(out=P.ROW[0:1, 1:2], in_=P.pA[0:1, 32:64], axis=AX.X)
            v.reduce_sum(out=P.ROW[0:1, 2:3], in_=P.pA[0:1, 64:96], axis=AX.X)
            P.inc(v.memset(P.ROW[0:1, 3:4], 0.0), P.sv, 1)
        P.sec("vector", f_vec)

        def f_gps(g):
            g.wait_ge(P.sv, sv0 + 1)
            P.inc(g.dma_start(out=P.ag_s_in, in_=P.ROW[0:1, 0:4]), P.sg, 16)
            g.wait_ge(P.sg, P.g(P.sg))
            P.inc(g.collective_compute(
                "AllGather", ALU.bypass, replica_groups=[list(range(8))],
                ins=[P.ag_s_in.opt()], outs=[P.ag_s_out.opt()]), P.sc, 1)
            g.wait_ge(P.sc, P.g(P.sc))
            P.inc(g.dma_start(out=P.ROW[0:1, 16:48],
                              in_=P.ag_s_out.rearrange("a b -> (a b)")[None, :]), P.sg, 16)
        P.sec("gpsimd", f_gps)

        sg1 = P.g(P.sg)
        sv1 = P.g(P.sv)

        def f_vec2(v):
            v.wait_ge(P.sg, sg1)
            P.fence(v, v.reduce_sum(
                out=P.ROW[0:1, 4:8],
                in_=P.ROW[0:1, 16:48].rearrange("p (r k) -> p k r", r=8),
                axis=AX.X))
            P.fence(v, v.tensor_scalar(out=P.ROW[0:1, 8:11], in0=P.ROW[0:1, 4:7],
                            scalar1=1.0 / 16777216.0, scalar2=EPS,
                            op0=ALU.mult, op1=ALU.max))
            P.inc(v.reciprocal(out=P.ROW[0:1, 12:15], in_=P.ROW[0:1, 8:11]), P.sv, 1)
        P.sec("vector", f_vec2)

        sv2, sp1 = P.g(P.sv), P.g(P.sp)

        def f_tensor2(t):
            t.wait_ge(P.sv, sv2)
            P.inc(t.matmul(P.pA[:, 16:19], P.c_onesrow, P.ROW[0:1, 8:11],
                           start=True, stop=True), P.sp, 1)
            P.inc(t.matmul(P.pA[:, 19:22], P.c_onesrow, P.ROW[0:1, 12:15],
                           start=True, stop=True), P.sp, 1)
        P.sec("tensor", f_tensor2)

        def f_vec3(v):
            v.wait_ge(P.sp, sp1 + 2)
            v.tensor_copy(out=P.c_scl[:, 3:6], in_=P.pA[:, 16:19])
            P.inc(v.tensor_copy(out=P.c_scl[:, 0:3], in_=P.pA[:, 19:22]), P.sv, 1)
        P.sec("vector", f_vec3)

        sv3 = P.g(P.sv)
        ds_q2 = P.dstream("dq2")
        ds_qs = P.dstream("dqs")

        def f_sync2(sy):
            sy.wait_ge(P.sv, sv3)
            for i in range(NT):
                if i >= 2:
                    sy.wait_ge(P.sv, sv3 + 3 * (i - 1))
                ds_q2.load(sy, bufs[i % 2], src(i), i)
        P.sec("sync", f_sync2)

        def f_vec4(v):
            for i in range(NT):
                ds_q2.wait(v, i)
                if i >= 2:
                    ds_qs.wait(v, i - 2)
                iv = P.c_scl[:, 0:1] if i < 32 else (
                    P.c_scl[:, 1:2] if i < 64 else P.c_scl[:, 2:3])
                P.fence(v, v.tensor_scalar(out=bufs[i % 2], in0=bufs[i % 2],
                                            scalar1=iv, scalar2=-1.0,
                                            op0=ALU.mult, op1=ALU.max))
                P.fence(v, v.tensor_scalar(out=i16, in0=bufs[i % 2], scalar1=1.0,
                                           scalar2=None, op0=ALU.min))
                P.inc(v.tensor_copy(out=obufs[i % 2], in_=i16), P.sv, 3)
        P.sec("vector", f_vec4)

        def f_gps2(g):
            for i in range(NT):
                g.wait_ge(P.sv, sv3 + 3 * (i + 1))
                ds_qs.load(g, dst(i), obufs[i % 2], i)
            for s in ds_qs.sems:
                g.wait_ge(s, P.g(s))
            P.inc(g.collective_compute(
                "AllGather", ALU.bypass, replica_groups=[list(range(8))],
                ins=[P.ag_w_in.opt()], outs=[P.ag_w_out.opt()]), P.sc, 1)
            P.inc(g.collective_compute(
                "AllGather", ALU.bypass, replica_groups=[list(range(8))],
                ins=[P.ag_d_in.opt()], outs=[P.ag_d_out.opt()]), P.sc, 1)
        P.sec("gpsimd", f_gps2)
        self.cc_w = P.g(P.sc)

    def stats(self, x_resident):
        P = self
        S = Scratch(P)
        sq = [S.f(1024), S.f(1024)]
        sp0, ss0, sd0 = P.g(P.sp), P.g(P.ss), P.g(P.sd)

        if x_resident:
            xch = lambda c: P.gf(R0 + c * 1024, 128, 1024)
        else:
            xbufs = [P.gf(R3A, 128, 1024), P.gf(R3A + 1024, 128, 1024)]
            ds_st = P.dstream("dst")

            def f_sync(sy):
                for c in range(16):
                    if c >= 2:
                        sy.wait_ge(P.sp, sp0 + 2 * (c - 1))
                        sy.wait_ge(P.ss, ss0 + (c - 1))
                    ds_st.load(sy, xbufs[c % 2],
                               P.x_spill[c * 128:(c + 1) * 128, :], c)
            P.sec("sync", f_sync)
            xch = lambda c: xbufs[c % 2]
            self._ds_st = ds_st

        def f_scal(s_):
            for c in range(16):
                if not x_resident:
                    self._ds_st.wait(s_, c)
                if c >= 2:
                    s_.wait_ge(P.sp, sp0 + 2 * (c - 1))
                P.inc(s_.activation(out=sq[c % 2], in_=xch(c), func=AF.Square),
                      P.ss, 1)
        P.sec("scalar", f_scal)

        def f_tensor(t):
            for c in range(16):
                if not x_resident:
                    self._ds_st.wait(t, c)
                t.wait_ge(P.ss, ss0 + (c + 1))
                for hh in range(2):
                    t.matmul(P.pA[0:1, hh * 512:(hh + 1) * 512], P.c_ones,
                             xch(c)[:, hh * 512:(hh + 1) * 512],
                             start=(c == 0), stop=(c == 15))
                    ins = t.matmul(P.pB[0:1, hh * 512:(hh + 1) * 512], P.c_ones,
                                   sq[c % 2][:, hh * 512:(hh + 1) * 512],
                                   start=(c == 0), stop=(c == 15))
                P.inc(ins, P.sp, 2)
        P.sec("tensor", f_tensor)

        sp1 = P.g(P.sp)

        def f_v1(v):
            v.wait_ge(P.sp, sp1)
            P.inc(v.tensor_copy(out=P.ROW[0:1, 0:1024], in_=P.pA[0:1, :]),
                  P.sv, 1)
        P.sec("vector", f_v1)

        sp2 = P.g(P.sp)

        def f_t1(t):
            t.wait_ge(P.sv, P.g(P.sv))
            for ti in range(8):
                ins = t.matmul(P.pC[:, ti:ti + 1],
                               P.ROW[0:1, ti * 128:(ti + 1) * 128],
                               P.c_ones1, start=True, stop=True)
            P.inc(ins, P.sp, 1)
        P.sec("tensor", f_t1)

        def f_v2(v):
            v.wait_ge(P.sp, sp2 + 1)
            P.fence(v, v.tensor_copy(out=P.t_rg, in_=P.pC[:, 0:8]))
            P.inc(v.tensor_copy(out=P.ROW[0:1, 0:1024], in_=P.pB[0:1, :]),
                  P.sv, 1)
        P.sec("vector", f_v2)

        sp3 = P.g(P.sp)

        def f_t2(t):
            t.wait_ge(P.sv, P.g(P.sv))
            for ti in range(8):
                ins = t.matmul(P.pC[:, 8 + ti:9 + ti],
                               P.ROW[0:1, ti * 128:(ti + 1) * 128],
                               P.c_ones1, start=True, stop=True)
            P.inc(ins, P.sp, 1)
        P.sec("tensor", f_t2)

        def f_v3(v):
            v.wait_ge(P.sp, sp3 + 1)
            P.fence(v, v.tensor_scalar(out=P.t_rg, in0=P.t_rg,
                                       scalar1=1.0 / 2048.0, scalar2=None,
                                       op0=ALU.mult))
            P.fence(v, v.tensor_scalar(out=P.t_tmp, in0=P.pC[:, 8:16],
                                       scalar1=1.0 / 2048.0, scalar2=None,
                                       op0=ALU.mult))
            P.fence(v, v.tensor_tensor(out=P.t_tmp2, in0=P.t_rg, in1=P.t_rg,
                                       op=ALU.mult))
            P.inc(v.tensor_sub(P.t_tmp, P.t_tmp, P.t_tmp2), P.sv, 1)
        P.sec("vector", f_v3)

        def f_s1(s_):
            s_.wait_ge(P.sv, P.g(P.sv))
            P.inc(s_.activation(out=P.t_lnr, in_=P.t_tmp, func=AF.Ln,
                                bias=P.c_eps[:, 0:1]), P.ss, 1)
        P.sec("scalar", f_s1)

        def f_v4(v):
            v.wait_ge(P.ss, P.g(P.ss))
            P.fence(v, v.tensor_scalar(out=P.t_lnr, in0=P.t_lnr, scalar1=-0.5,
                                       scalar2=None, op0=ALU.mult))
            v.tensor_scalar(out=P.t_lnr8, in0=P.t_lnr, scalar1=LN8,
                            scalar2=None, op0=ALU.add)
            P.inc(v.tensor_scalar(out=P.t_lnr2, in0=P.t_lnr, scalar1=2.0,
                                  scalar2=None, op0=ALU.mult), P.sv, 1)
        P.sec("vector", f_v4)

        def f_s2(s_):
            s_.wait_ge(P.sv, P.g(P.sv))
            P.inc(s_.activation(out=P.t_r, in_=P.t_lnr, func=AF.Exp), P.ss, 1)
        P.sec("scalar", f_s2)

    def amax_fm(self, fbase, nch, t_out):
        P = self
        S = Scratch(P)
        runm = S.f(1024)
        sv0 = P.g(P.sv)

        def f_vec(v):
            v.wait_ge(P.ss, P.g(P.ss))
            src = P.gf(fbase, 128, nch * 1024).rearrange(
                "p (c t) -> p t c", c=nch)
            P.inc(v.reduce_max(out=runm, in_=src, axis=AX.X,
                               apply_absolute_value=True), P.sv, 1)
        P.sec("vector", f_vec)

        sp0 = P.g(P.sp)

        def f_tensor(t):
            t.wait_ge(P.sv, sv0 + 1)
            for ti in range(8):
                ins = t.transpose(P.pC[:, ti * 128:(ti + 1) * 128],
                                  runm[:, ti * 128:(ti + 1) * 128], P.c_idf)
            P.inc(ins, P.sp, 1)
        P.sec("tensor", f_tensor)

        sv1 = P.g(P.sv)

        def f_vec2(v):
            v.wait_ge(P.sp, sp0 + 1)
            ins = v.reduce_max(
                out=t_out,
                in_=P.pC[:, 0:1024].rearrange("p (a b) -> p a b", a=8),
                axis=AX.X)
            P.fence(v, ins)
            P.inc(v.tensor_copy(out=P.t_tmp2[:, 0:1], in_=t_out[:, 0:1]),
                  P.sv, 1)
        P.sec("vector", f_vec2)

    def col2row(self, cols, rowdst8, k):
        P = self
        sp0 = P.g(P.sp)

        def f_tensor(t):
            t.wait_ge(P.sv, P.g(P.sv))
            for j in range(k):
                ins = t.transpose(P.pB[0:1, j * 128:(j + 1) * 128],
                                  cols[:, j:j + 1], P.c_idf)
            P.inc(ins, P.sp, 1)
        P.sec("tensor", f_tensor)

        sv0 = P.g(P.sv)

        def f_vec(v):
            v.wait_ge(P.sp, sp0 + 1)
            P.inc(v.tensor_copy(out=rowdst8, in_=P.pB[0:1, 0:k * 128]), P.sv, 1)
        P.sec("vector", f_vec)

    def ffn(self, call):
        P = self
        x_old_src = P.xT if call == 1 else P.x_spill

        self.stats(x_resident=True)

        if call == 2:
            def f_gspill(g):
                g.wait_ge(P.ss, P.g(P.ss))
                g.wait_ge(P.sp, P.g(P.sp))
                P.inc(g.dma_start(
                    out=P.x_spill.rearrange("(c p) t -> p c t", p=128),
                    in_=P.gf(R0, 128, 16384).rearrange("p (c t) -> p c t", c=16)),
                    P.sg, 16)
            P.sec("gpsimd", f_gspill)

        P.full_barrier()
        u = [R1 + c * CSZ for c in range(16)]
        pong = [R2 + c * CSZ for c in range(16)]
        self.butterfly(u, pong, 16,
                       first_src=lambda c: P.gf(R0 + c * 1024, 128, 1024))
        P.full_barrier()
        u_slots = list(u)

        sv0 = P.g(P.sv)

        def f_vece0(v):
            P.fence(v, v.tensor_scalar(out=P.t_tmp, in0=P.t_rg, scalar1=-2048.0,
                                       scalar2=None, op0=ALU.mult))
            P.fence(v, v.memset(P.ROW[0:1, 512:576].bitcast(BF16)[0:1, 0:128],
                                0.0))
            P.inc(v.memset(P.ROW[0:1, 512:576].bitcast(BF16)[0:1, 0:1], 1.0),
                  P.sv, 1)
        P.sec("vector", f_vece0)
        e0row = P.ROW[0:1, 0:512].bitcast(BF16)[0:1, 0:1024]
        e0lhs = P.ROW[0:1, 512:576].bitcast(BF16)[0:1, 0:128]
        self.col2row(P.t_tmp, e0row, 8)

        sp0, ss0 = P.g(P.sp), P.g(P.ss)

        def f_mmh(t):
            t.wait_ge(P.sv, P.g(P.sv))
            for c in range(16):
                if c >= 2:
                    t.wait_ge(P.ss, ss0 + (c - 1))
                pp = P.pA if c % 2 == 0 else P.pB
                for hh in range(2):
                    ins = t.matmul(pp[:, hh * 512:(hh + 1) * 512], P.c_h128,
                                   P.gb(u_slots[c], 128, 1024)
                                   [:, hh * 512:(hh + 1) * 512],
                                   start=True, stop=(c != 0))
                    if c == 0:
                        ins = t.matmul(pp[:, hh * 512:(hh + 1) * 512], e0lhs,
                                       e0row[0:1, hh * 512:(hh + 1) * 512],
                                       start=False, stop=True)
                P.inc(ins, P.sp, 1)
        P.sec("tensor", f_mmh)

        def f_scevac(s_):
            for c in range(16):
                s_.wait_ge(P.sp, sp0 + (c + 1))
                pp = P.pA if c % 2 == 0 else P.pB
                P.inc(s_.activation(out=P.gf(R0 + c * 1024, 128, 1024),
                                    in_=pp[:, :], func=AF.Copy), P.ss, 1)
        P.sec("scalar", f_scevac)

        self.amax_fm(R0, 16, P.t_amax)
        sv1 = P.g(P.sv)

        def f_vrows(v):
            P.fence(v, v.tensor_tensor(out=P.t_tmp, in0=P.t_amax, in1=P.t_r,
                                       op=ALU.mult))
            P.fence(v, v.tensor_scalar(out=P.t_tmp, in0=P.t_tmp, scalar1=1.0 / SQD,
                            scalar2=EPS, op0=ALU.mult, op1=ALU.max))
            v.tensor_scalar(out=P.t_dq, in0=P.t_tmp, scalar1=1.0 / 127.0,
                            scalar2=None, op0=ALU.mult)
            P.fence(v, v.reciprocal(out=P.t_tmp2, in_=P.t_tmp))
            P.fence(v, v.tensor_tensor(out=P.t_m, in0=P.t_tmp2, in1=P.t_r,
                                       op=ALU.mult))
            P.inc(v.tensor_scalar(out=P.t_m, in0=P.t_m, scalar1=127.0 / SQD,
                                  scalar2=None, op0=ALU.mult), P.sv, 1)
        P.sec("vector", f_vrows)

        self.col2row(P.t_m, P.ROW[0:1, 0:1024], 8)

        sp1 = P.g(P.sp)

        def f_mmrep(t):
            t.wait_ge(P.sv, P.g(P.sv))
            for hh in range(2):
                ins = t.matmul(P.pC[:, hh * 512:(hh + 1) * 512], P.c_onesrow,
                               P.ROW[0:1, hh * 512:(hh + 1) * 512],
                               start=True, stop=True)
            P.inc(ins, P.sp, 1)
        P.sec("tensor", f_mmrep)

        sv2 = P.g(P.sv)

        def f_vround(v):
            v.wait_ge(P.sp, sp1 + 1)
            for c in range(16):
                ins = v.tensor_tensor(out=P.gi(R1 + c * CSZ, 128, 1024),
                                in0=P.gf(R0 + c * 1024, 128, 1024),
                                in1=P.pC[:, 0:1024], op=ALU.mult)
            P.fence(v, ins)
            P.inc(v.tensor_copy(out=P.gb(R2, 128, 16384),
                                in_=P.gi(R1, 128, 16384)), P.sv, 2)
        P.sec("vector", f_vround)
        fq = lambda c: P.gb(R2 + c * CSZ, 128, 1024)
        if call == 1:
            self.snap("fq0", P.gb(R2, 128, 2048))
            self.snap("amax", P.t_amax)
            self.snap("tm", P.t_m)
            self.snap("tdq1", P.t_dq)
            self.snap("fraw0", P.gf(R0, 128, 1024))

        self.col2row(P.t_dq, P.ROW[0:1, 0:1024], 8)

        S = Scratch(P)
        dqgrep = S.b(1024)
        gsc = [S.b(1024), S.b(1024)]
        sgb = [S.b(1024), S.b(1024)]
        tb = [S.b(1024)]
        upb = [S.b(1024)]

        sp2 = P.g(P.sp)

        def f_mmdq(t):
            t.wait_ge(P.sv, P.g(P.sv))
            for hh in range(2):
                ins = t.matmul(P.pA[:, hh * 512:(hh + 1) * 512], P.c_onesrow,
                               P.ROW[0:1, hh * 512:(hh + 1) * 512],
                               start=True, stop=True)
            P.inc(ins, P.sp, 1)
        P.sec("tensor", f_mmdq)

        sv3 = P.g(P.sv)

        def f_vdq(v):
            v.wait_ge(P.sp, sp2 + 1)
            P.inc(v.tensor_scalar(out=dqgrep, in0=P.pA[:, 0:1024],
                                  scalar1=P.c_scl[:, 3:4], scalar2=None,
                                  op0=ALU.mult), P.sv, 1)
        P.sec("vector", f_vdq)
        P.full_barrier()

        wview = P.ag_w_out.rearrange("(r w a o p) n -> r w a o p n",
                                     r=8, w=2, a=2, o=64)
        wbuf = [P.WA.bitcast(BF16)[:, 0:2048], P.WA.bitcast(BF16)[:, 2048:4096]]
        sp3 = P.g(P.sp)
        sv4 = P.g(P.sv)
        ss1 = P.g(P.ss)
        ds_w = P.dstream("dw", incs=2)

        def f_wload(sy):
            sy.wait_ge(P.sc, P.cc_w)
            for oc in range(64):
                for w in range(2):
                    i = oc * 2 + w
                    if i >= 2:
                        sy.wait_ge(P.sp, sp3 + (i - 1))
                    outs = [wbuf[i % 2].rearrange("p (k n) -> p k n",
                                                  k=16)[:, a::2, :]
                            for a in range(2)]
                    ins_ = [wview[:, w, a, oc, :, :].rearrange("r p n -> p r n")
                            for a in range(2)]
                    ds_w.load(sy, outs, ins_, i)
        P.sec("sync", f_wload)

        def f_mmgu(t):
            for oc in range(64):
                for w in range(2):
                    i = oc * 2 + w
                    ds_w.wait(t, i)
                    if oc >= 2 and w == 0:
                        t.wait_ge(P.sv, sv4 + 5 * (oc - 1))
                    pp = (P.pA if w == 0 else P.pB) if oc % 2 == 0 else \
                        (P.pC[:, 0:1024] if w == 0 else P.pC[:, 1024:2048])
                    for kc in range(16):
                        for hh in range(2):
                            ins = t.matmul(
                                pp[:, hh * 512:(hh + 1) * 512],
                                wbuf[i % 2][:, kc * 128:(kc + 1) * 128],
                                fq(kc)[:, hh * 512:(hh + 1) * 512],
                                start=(kc == 0), stop=(kc == 15))
                    P.inc(ins, P.sp, 1)
        P.sec("tensor", f_mmgu)

        def f_vzchain(v):
            for oc in range(64):
                v.wait_ge(P.sp, sp3 + 2 * (oc + 1))
                gp = P.pA[:, :] if oc % 2 == 0 else P.pC[:, 0:1024]
                up = P.pB[:, :] if oc % 2 == 0 else P.pC[:, 1024:2048]
                P.inc(v.tensor_tensor(out=gsc[oc % 2], in0=gp, in1=dqgrep,
                                      op=ALU.mult), P.sv, 1)
                v.wait_ge(P.ss, ss1 + oc + 1)
                v.tensor_tensor(out=tb[0], in0=gp, in1=sgb[oc % 2],
                                op=ALU.mult)
                P.fence(v, v.tensor_copy(out=upb[0], in_=up))
                P.inc(v.tensor_tensor(out=P.gb(self.zslot(oc), 128, 1024),
                                      in0=tb[0], in1=upb[0],
                                      op=ALU.mult), P.sv, 4)
        P.sec("vector", f_vzchain)

        def f_ssig(s_):
            for oc in range(64):
                s_.wait_ge(P.sv, sv4 + 5 * oc + 1)
                P.inc(s_.activation(out=sgb[oc % 2], in_=gsc[oc % 2],
                                    func=AF.Sigmoid), P.ss, 1)
        P.sec("scalar", f_ssig)
        P.full_barrier()
        if call == 1:
            self.snap("z0", P.gb(self.zslot(0), 128, 1024))
            self.snap("z63", P.gb(self.zslot(63), 128, 1024))
            self.snap("gsc63", gsc[1])
            self.snap("sgb63", sgb[1])
            self.snap("upb63", upb[0])
            self.snap("dqgrep", dqgrep)
            self.snap("fqall", P.gb(R2, 128, 16384))
            self.snap("wbuf0", P.WA.bitcast(BF16)[:, 0:2048])
            self.snap("wbuf1", P.WA.bitcast(BF16)[:, 2048:4096])

        zsl = [self.zslot(c) for c in range(64)]
        spares = ([R3X + k * CSZ for k in range(9)]
                  + [R2 + k * CSZ for k in range(16)]
                  + [-(k + 1) for k in range(8)])
        self.butterfly(zsl, spares, 64)
        bad = lambda s: s < 0 or (R2 <= s < R2 + 8192)
        all_g = set([self.zslot(c) for c in range(64)]
                    + [R3X + k * CSZ for k in range(9)])
        free_g = sorted(all_g - set(s for s in zsl if not bad(s))
                        - set([s for s in zsl if bad(s)]))
        free_g = sorted(all_g - set(zsl))
        free_ok = [s for s in free_g if not bad(s)]
        moves = []
        for c in range(64):
            if bad(zsl[c]):
                dst = free_ok.pop(0)
                moves.append((dst, zsl[c]))
                zsl[c] = dst
        sv_mv = P.g(P.sv)

        def f_mv(v):
            v.wait_ge(P.sgc, P.g(P.sgc))
            v.wait_ge(P.sv, P.g(P.sv))
            ins = None
            for dst, srcslot in moves:
                ins = v.tensor_copy(out=P.zbf(dst), in_=P.zbf(srcslot))
            if ins is not None:
                P.inc(ins, P.sv, 1)
        P.sec("vector", f_mv)
        P.full_barrier()
        self.z_final = list(zsl)

        self.quant8(call)
        P.full_barrier()
        if call == 1:
            self.snap("zq0", P.zbf(self.z_final[0]))
            self.snap("te", P.t_e)

        self.down(call, x_old_src)

    def butterfly(self, slots, spares, nchunks, first_src=None):
        P = self
        nst = int(math.log2(nchunks))
        npairs = nchunks // 2
        n_g = sum(1 for k in range(npairs) if k % 3 == 2)
        n_v = npairs - n_g
        nsp = len(spares)
        pool_g = spares[:min(n_g, nsp)]
        pool_v = spares[min(n_g, nsp):]
        vops, gops = [], []
        marks_v, marks_g = [], []
        for s in range(nst):
            h = 1 << s
            pairs = []
            for base in range(0, nchunks, 2 * h):
                for i in range(h):
                    pairs.append((base + i, base + h + i))
            freed_v, freed_g = [], []
            for k, (ca, cb) in enumerate(pairs):
                gps = (k % 3 == 2)
                tgt = gops if gps else vops
                if s == 0 and first_src is not None:
                    tgt.append((slots[ca], ("x", ca), ("x", cb), ALU.add))
                    tgt.append((slots[cb], ("x", ca), ("x", cb), ALU.subtract))
                else:
                    pool = pool_g if gps else pool_v
                    freed = freed_g if gps else freed_v
                    if pool:
                        sp_off = pool.pop(0)
                    else:
                        sp_off = freed.pop(0)
                        tgt.append(("fence",))
                    tgt.append((sp_off, ("s", slots[ca]), ("s", slots[cb]),
                                ALU.subtract))
                    tgt.append((slots[ca], ("s", slots[ca]), ("s", slots[cb]),
                                ALU.add))
                    freed.append(slots[cb])
                    slots[cb] = sp_off
            pool_v.extend(freed_v)
            pool_g.extend(freed_g)
            marks_v.append(sum(1 for o in vops if o[0] != "fence"))
            marks_g.append(sum(1 for o in gops if o[0] != "fence"))

        sv0, sg0 = P.g(P.sv), P.g(P.sgc)

        def slot_ap(o):
            return P.zbf(o)

        def ap_of(x):
            kind, v = x
            if kind == "x":
                return first_src(v)
            return slot_ap(v)

        def emit(eng, ops, marks, own_sem, other_sem, other_marks):
            other0 = sg0 if own_sem is P.sv else sv0
            own0 = sv0 if own_sem is P.sv else sg0
            idx = 0
            for s in range(nst):
                if s > 0 and other_marks[s - 1] > 0:
                    eng.wait_ge(other_sem, other0 + other_marks[s - 1])
                if s > 0 and marks[s - 1] > 0:
                    eng.wait_ge(own_sem, own0 + marks[s - 1])
                while idx < marks[s]:
                    op_t = ops[idx]
                    if op_t[0] == "fence":
                        eng.wait_ge(own_sem, P.g(own_sem))
                        idx += 1
                        continue
                    o, a, b, op = op_t
                    ins = eng.tensor_tensor(out=slot_ap(o),
                                            in0=ap_of(a), in1=ap_of(b), op=op)
                    P.inc(ins, own_sem, 1)
                    idx += 1

        def f_vec(v):
            emit(v, vops, marks_v, P.sv, P.sgc, marks_g)
        def f_gps(g):
            emit(g, gops, marks_g, P.sgc, P.sv, marks_v)
        P.sec("vector", f_vec)
        P.sec("gpsimd", f_gps)

    def zslot(self, c):
        if c < 32:
            return R0 + c * CSZ
        if c < 48:
            return R1 + (c - 32) * CSZ
        return R3A + (c - 48) * CSZ

    def quant8(self, call):
        P = self
        zf = self.z_final
        sv0 = P.g(P.sv)

        def f_vc(v):
            P.fence(v, v.tensor_tensor(out=P.t_c, in0=P.t_dq, in1=P.t_dq,
                                       op=ALU.mult))
            P.fence(v, v.tensor_scalar(out=P.t_c, in0=P.t_c,
                                       scalar1=P.c_scl[:, 3:4],
                                       scalar2=None, op0=ALU.mult))
            P.inc(v.tensor_scalar(out=P.t_c, in0=P.t_c,
                                  scalar1=P.c_scl[:, 4:5], scalar2=None,
                                  op0=ALU.mult), P.sv, 1)
        P.sec("vector", f_vc)

        for tq in range(8):
            self._quant8_tq(tq)
        self.quant8_fin()

    def _quant8_tq(self, tq):
        P = self
        zf = self.z_final
        tsl = slice(tq * 128, (tq + 1) * 128)
        sp0, ss0, sv0 = P.g(P.sp), P.g(P.ss), P.g(P.sv)

        def f_mm(t):
            if tq > 0:
                t.wait_ge(P.sv, sv0)
            for c in range(64):
                grp = c // 8
                if c % 8 == 0 and grp >= 2:
                    t.wait_ge(P.ss, ss0 + (grp - 1))
                pp = P.pA if grp % 2 == 0 else P.pB
                ins = t.matmul(pp[:, (c % 8) * 128:(c % 8 + 1) * 128],
                               P.c_h128, P.zbf(zf[c])[:, tsl],
                               start=True, stop=True)
                if c % 8 == 7:
                    P.inc(ins, P.sp, 1)
        P.sec("tensor", f_mm)

        def f_sevac(s_):
            for grp in range(8):
                s_.wait_ge(P.sp, sp0 + grp + 1)
                pp = P.pA if grp % 2 == 0 else P.pB
                P.inc(s_.activation(
                    out=P.gf(R2 + grp * 1024, 128, 1024), in_=pp[:, :],
                    func=AF.Copy), P.ss, 1)
        P.sec("scalar", f_sevac)

        sv1 = P.g(P.sv)
        ss1 = P.g(P.ss)

        def f_vamax(v):
            v.wait_ge(P.ss, ss1)
            P.inc(v.reduce_max(out=P.WA[:, 0:128],
                         in_=P.gf(R2, 128, 8192).rearrange("p (c t) -> p t c", c=64),
                         axis=AX.X, apply_absolute_value=True), P.sv, 1)
        P.sec("vector", f_vamax)

        sp1 = P.g(P.sp)

        def f_tp(t):
            t.wait_ge(P.sv, sv1 + 1)
            ins = t.transpose(P.pC[:, 0:128], P.WA[:, 0:128], P.c_idf)
            P.inc(ins, P.sp, 1)
        P.sec("tensor", f_tp)

        sv2 = P.g(P.sv)

        def f_vscale(v):
            v.wait_ge(P.sp, sp1 + 1)
            P.fence(v, v.reduce_max(out=P.t_tmp[:, 0:1], in_=P.pC[:, 0:128],
                                    axis=AX.X))
            P.fence(v, v.tensor_tensor(out=P.t_tmp[:, 1:2], in0=P.t_tmp[:, 0:1],
                            in1=P.t_c[:, tq:tq + 1], op=ALU.mult))
            P.fence(v, v.tensor_scalar(out=P.t_tmp[:, 1:2], in0=P.t_tmp[:, 1:2],
                            scalar1=1.0 / SQ8, scalar2=EPS,
                            op0=ALU.mult, op1=ALU.max))
            v.tensor_scalar(out=P.t_e[:, tq:tq + 1], in0=P.t_tmp[:, 1:2],
                            scalar1=P.c_scl[:, 5:6], scalar2=0.5 / 127.0,
                            op0=ALU.mult, op1=ALU.mult)
            P.fence(v, v.reciprocal(out=P.t_tmp[:, 2:3], in_=P.t_tmp[:, 1:2]))
            P.fence(v, v.tensor_tensor(out=P.t_tmp[:, 2:3], in0=P.t_tmp[:, 2:3],
                            in1=P.t_c[:, tq:tq + 1], op=ALU.mult))
            P.inc(v.tensor_scalar(out=P.t_tmp[:, 2:3], in0=P.t_tmp[:, 2:3],
                                  scalar1=127.0 / SQ8, scalar2=None,
                                  op0=ALU.mult), P.sv, 1)
        P.sec("vector", f_vscale)

        sp2 = P.g(P.sp)

        def f_tp2(t):
            t.wait_ge(P.sv, sv2 + 1)
            P.inc(t.transpose(P.pC[0:1, 128:256], P.t_tmp[:, 2:3], P.c_idf),
                  P.sp, 1)
        P.sec("tensor", f_tp2)

        sv3 = P.g(P.sv)

        def f_vrow(v):
            v.wait_ge(P.sp, sp2 + 1)
            P.inc(v.tensor_copy(out=P.ROW[0:1, 0:128], in_=P.pC[0:1, 128:256]),
                  P.sv, 1)
        P.sec("vector", f_vrow)

        sp3 = P.g(P.sp)

        def f_rep(t):
            t.wait_ge(P.sv, sv3 + 1)
            P.inc(t.matmul(P.pC[:, 256:384], P.c_onesrow, P.ROW[0:1, 0:128],
                           start=True, stop=True), P.sp, 1)
        P.sec("tensor", f_rep)

        def f_vround(v):
            v.wait_ge(P.sp, sp3 + 1)
            runs = []
            c = 0
            while c < 64:
                c2 = c + 1
                while (c2 < 64 and zf[c2] >= 0 and zf[c2 - 1] >= 0
                       and zf[c2] == zf[c2 - 1] + CSZ):
                    c2 += 1
                runs.append((c, c2))
                c = c2
            ins = None
            for (a, b) in runs:
                n = b - a
                if n == 1:
                    out = P.zi16(zf[a])[:, None, :].broadcast_to([128, 1, 1024]) \
                        if False else P.zi16(zf[a]).rearrange(
                            "p (c t) -> p c t", c=1)[:, :, tsl]
                else:
                    out = P.G[:, zf[a]:zf[a] + n * CSZ].bitcast(I16) \
                        .rearrange("p (c t) -> p c t", c=n)[:, :, tsl]
                in0 = P.gf(R2, 128, 8192).rearrange(
                    "p (c t) -> p c t", c=64)[:, a:b, :]
                in1 = P.pC[:, 256:384][:, None, :].broadcast_to([128, n, 128])
                ins = v.tensor_tensor(out=out, in0=in0, in1=in1, op=ALU.mult)
            P.inc(ins, P.sv, 1)
        P.sec("vector", f_vround)

    def quant8_fin(self):
        P = self
        zf = self.z_final
        sv0 = P.g(P.sv)

        def f_v(v):
            v.wait_ge(P.sv, P.g(P.sv))
            runs = []
            c = 0
            while c < 64:
                c2 = c + 1
                while (c2 < 64 and zf[c2] >= 0 and zf[c2 - 1] >= 0
                       and zf[c2] == zf[c2 - 1] + CSZ):
                    c2 += 1
                runs.append((c, c2))
                c = c2
            ins = None
            for (a, b) in runs:
                n = b - a
                if n == 1:
                    ins = v.tensor_copy(out=P.zbf(zf[a]), in_=P.zi16(zf[a]))
                else:
                    ins = v.tensor_copy(
                        out=P.G[:, zf[a]:zf[a] + n * CSZ].bitcast(BF16),
                        in_=P.G[:, zf[a]:zf[a] + n * CSZ].bitcast(I16))
            P.inc(ins, P.sv, 1)
        P.sec("vector", f_v)

    def down(self, call, x_old_src):
        P = self
        zf = self.z_final
        dst = P.x_spill if call == 1 else P.out

        self.col2row(P.t_e, P.ROW[0:1, 0:1024], 8)
        erep = P.gf(R2, 128, 1024)
        xold = [P.gf(R2 + 1024, 128, 1024), P.gf(R2 + 2048, 128, 1024)]
        xnew = [P.gf(R2 + 3072, 128, 1024), P.gf(R2 + 4096, 128, 1024)]
        tbuf = P.gf(R2 + 5120, 128, 1024)

        sp0 = P.g(P.sp)

        def f_erep(t):
            t.wait_ge(P.sv, P.g(P.sv))
            for hh in range(2):
                ins = t.matmul(P.pC[:, hh * 512:(hh + 1) * 512], P.c_onesrow,
                               P.ROW[0:1, hh * 512:(hh + 1) * 512],
                               start=True, stop=True)
            P.inc(ins, P.sp, 1)
        P.sec("tensor", f_erep)

        def f_verep(v):
            v.wait_ge(P.sp, sp0 + 1)
            P.inc(v.tensor_copy(out=erep, in_=P.pC[:, 0:1024]), P.sv, 1)
        P.sec("vector", f_verep)

        wv = P.ag_d_out.rearrange("(r m k q) n -> r m k q n", r=8, m=16, k=4)
        wbuf = [P.WA.bitcast(BF16)[:, 0:1024], P.WA.bitcast(BF16)[:, 1024:2048]]

        ds_dw = P.dstream("ddw")
        ds_dxo = P.dstream("dxo")
        ds_dxs = P.dstream("dxs")
        wl_done = {}
        ld_done = {}
        st_done = {}
        mm_done = {}
        ev_done = {}

        sd0 = P.g(P.sd)
        sv_base = P.g(P.sv)

        sd_c = P.g(P.sd)
        for m in range(16):
            for r in range(8):
                sd_c += 16
                wl_done[(m, r)] = sd_c
        sp_c = P.g(P.sp)
        for m in range(16):
            sp_c += 8
            mm_done[m] = sp_c
        sv_c = P.g(P.sv) + 1
        sv_c = P.g(P.sv)
        for m in range(16):
            sv_c += 2
            ev_done[m] = sv_c
        sg_c = P.g(P.sg)
        for m in range(16):
            sg_c += 16
            ld_done[m] = sg_c
            if m >= 2:
                sg_c += 16
                st_done[m - 2] = sg_c
        for m in (14, 15):
            sg_c += 16
            st_done[m] = sg_c

        def f_wload(sy):
            for m in range(16):
                for r in range(8):
                    i = m * 8 + r
                    if i >= 2:
                        m2, r2 = (i - 2) // 8, (i - 2) % 8
                        t_sp = P.g(P.sp) - 0
                        pass
                    P.inc(sy.dma_start(
                        out=wbuf[i % 2].rearrange("p (k j n) -> p k j n",
                                                  k=4, j=2),
                        in_=wv[r, m].rearrange("k (p j) n -> p k j n", p=128)),
                        P.sd, 16)
        mmr_done = {}
        sp_c = P.g(P.sp) + 1
        sp_c = P.g(P.sp)
        for m in range(16):
            for r in range(8):
                sp_c += 1
                mmr_done[(m, r)] = sp_c

        def f_wload2(sy):
            for m in range(16):
                for r in range(8):
                    i = m * 8 + r
                    if i >= 2:
                        m2, r2 = (i - 2) // 8, (i - 2) % 8
                        sy.wait_ge(P.sp, mmr_done[(m2, r2)])
                    ds_dw.load(sy,
                               wbuf[i % 2].rearrange("p (k j n) -> p k j n",
                                                     k=4, j=2),
                               wv[r, m].rearrange("k (p j) n -> p k j n", p=128),
                               i)
        P.sec("sync", f_wload2)

        def f_gx(g):
            for m in range(16):
                if m >= 2:
                    g.wait_ge(P.sv, ev_done[m - 2])
                ds_dxo.load(g, xold[m % 2],
                            x_old_src[m * 128:(m + 1) * 128, :], m)
                if m >= 2:
                    ds_dxs.load(g, dst[(m - 2) * 128:(m - 1) * 128, :],
                                xnew[(m - 2) % 2], m - 2)
            for m in (14, 15):
                g.wait_ge(P.sv, ev_done[m])
                ds_dxs.load(g, dst[m * 128:(m + 1) * 128, :], xnew[m % 2], m)
        P.sec("gpsimd", f_gx)

        def f_mm(t):
            for m in range(16):
                pp = P.pA if m % 2 == 0 else P.pB
                if m >= 2:
                    t.wait_ge(P.sv, ev_done[m - 2])
                for r in range(8):
                    ds_dw.wait(t, m * 8 + r)
                    for kp in range(4):
                        for j in range(2):
                            kc = r * 8 + kp * 2 + j
                            for hh in range(2):
                                ins = t.matmul(
                                    pp[:, hh * 512:(hh + 1) * 512],
                                    wbuf[(m * 8 + r) % 2].rearrange(
                                        "p (k j n) -> p k j n", k=4, j=2)
                                    [:, kp, j, :],
                                    P.zbf(zf[kc])
                                    [:, hh * 512:(hh + 1) * 512],
                                    start=(kc == 0), stop=(kc == 63))
                    P.inc(ins, P.sp, 1)
        P.sec("tensor", f_mm)

        def f_vevac(v):
            v.wait_ge(P.sv, P.g(P.sv))
            for m in range(16):
                v.wait_ge(P.sp, mmr_done[(m, 7)])
                ds_dxo.wait(v, m)
                if m >= 2:
                    ds_dxs.wait(v, m - 2)
                pp = P.pA if m % 2 == 0 else P.pB
                P.fence(v, v.tensor_tensor(out=tbuf, in0=pp[:, :], in1=erep,
                                           op=ALU.mult))
                n = P.inc(v.tensor_tensor(out=xnew[m % 2], in0=tbuf,
                                          in1=xold[m % 2], op=ALU.add), P.sv, 2)
                v.wait_ge(P.sv, n)
        P.sec("vector", f_vevac)
        if call == 2:
            self.out_streams = ds_dxs

    def attn(self):
        P = self
        self.stats(x_resident=False)

        sp0 = P.g(P.sp)

        self.col2row(P.t_rg, P.ROW[0:1, 0:1024], 8)

        def f_mmu(t):
            t.wait_ge(P.sv, P.g(P.sv))
            t.wait_ge(P.ss, P.g(P.ss))
            for hh in range(2):
                ins = t.matmul(P.pC[:, hh * 512:(hh + 1) * 512], P.c_onesrow,
                               P.ROW[0:1, hh * 512:(hh + 1) * 512],
                               start=True, stop=True)
            P.inc(ins, P.sp, 1)
        P.sec("tensor", f_mmu)

        xbufs = [P.gf(R3A, 128, 1024), P.gf(R3A + 1024, 128, 1024)]
        sv0 = P.g(P.sv)
        ss_pre, sp_pre = P.g(P.ss), P.g(P.sp)
        ds_xb = P.dstream("dst")

        def f_xs(sy):
            sy.wait_ge(P.ss, ss_pre)
            sy.wait_ge(P.sp, sp_pre)
            for c in range(16):
                if c >= 2:
                    sy.wait_ge(P.sv, sv0 + (c - 1))
                ds_xb.load(sy, xbufs[c % 2],
                           P.x_spill[c * 128:(c + 1) * 128, :], c)
        P.sec("sync", f_xs)

        xb = lambda c: P.gb(R0 + c * CSZ, 128, 1024)

        def f_vxb(v):
            v.wait_ge(P.sp, sp0 + 1)
            for c in range(16):
                ds_xb.wait(v, c)
                P.inc(v.tensor_sub(xb(c), xbufs[c % 2], P.pC[:, 0:1024]),
                      P.sv, 1)
        P.sec("vector", f_vxb)
        P.full_barrier()
        if self.aupto <= 1:
            return

        S = Scratch(P)
        bgk2rep = S.f(1024)
        ds_row = P.dstream("drow")

        def f_rows(sy):
            ds_row.load(sy, P.ROW[0:1, 0:KD], P.bgk2r, 0)
        P.sec("sync", f_rows)
        self._ds_row = ds_row

        spb = P.g(P.sp)

        def f_mmb(t):
            ds_row.wait(t, 0)
            for hh in range(2):
                ins = t.matmul(P.pC[:, hh * 512:(hh + 1) * 512], P.c_onesrow,
                               P.ROW[0:1, hh * 512:(hh + 1) * 512],
                               start=True, stop=True)
            P.inc(ins, P.sp, 1)
        P.sec("tensor", f_mmb)

        def f_vb(v):
            v.wait_ge(P.sp, spb + 1)
            P.inc(v.tensor_copy(out=bgk2rep, in_=P.pC[:, 0:1024]), P.sv, 1)
        P.sec("vector", f_vb)

        wbufb = [P.WA.bitcast(BF16)[:, 0:2048], P.WA.bitcast(BF16)[:, 2048:4096]]

        def proj_sweep(groups):
            ds_aw = P.dstream("daw")
            mmr = {}
            sp_c = P.g(P.sp)
            idx = [0]
            total = sum(n for _, n, _ in groups)
            for gi, (w, n, ef) in enumerate(groups):
                for oc in range(n):
                    sp_c += 1
                    mmr[idx[0]] = sp_c
                    idx[0] += 1
            ev = {}
            sv_c = P.g(P.sv)
            for i in range(total):
                sv_c += 1
                ev[i] = sv_c

            def f_wl(g):
                i = 0
                for w, n, ef in groups:
                    for oc in range(n):
                        if i >= 2:
                            g.wait_ge(P.sp, mmr[i - 2])
                        ds_aw.load(g,
                                   wbufb[i % 2].rearrange("p (k n) -> p k n",
                                                          k=16),
                                   w[:, oc * 128:(oc + 1) * 128].rearrange(
                                       "(k p) n -> p k n", p=128), i)
                        i += 1
            P.sec("gpsimd", f_wl)

            def f_pm(t):
                i = 0
                for w, n, ef in groups:
                    for oc in range(n):
                        ds_aw.wait(t, i)
                        if i >= 2:
                            t.wait_ge(P.sv, ev[i - 2])
                        pp = P.pA if i % 2 == 0 else P.pB
                        for kc in range(16):
                            for hh in range(2):
                                ins = t.matmul(
                                    pp[:, hh * 512:(hh + 1) * 512],
                                    wbufb[i % 2][:, kc * 128:(kc + 1) * 128],
                                    xb(kc)[:, hh * 512:(hh + 1) * 512],
                                    start=(kc == 0), stop=(kc == 15))
                        P.inc(ins, P.sp, 1)
                        i += 1
            P.sec("tensor", f_pm)

            def f_pv(v):
                i = 0
                for w, n, ef in groups:
                    for oc in range(n):
                        v.wait_ge(P.sp, mmr[i])
                        pp = P.pA if i % 2 == 0 else P.pB
                        P.inc(v.tensor_copy(out=ef(oc), in_=pp[:, :]), P.sv, 1)
                        i += 1
            P.sec("vector", f_pv)

        def transpose_phase(src_base, dst_base):
            sp_c = P.g(P.sp)
            tp = {}
            for oc in range(16):
                sp_c += 1
                tp[oc] = sp_c
            ev = {}
            sv_c = P.g(P.sv)
            for oc in range(16):
                sv_c += 1
                ev[oc] = sv_c

            def f_t(t):
                t.wait_ge(P.sv, P.g(P.sv))
                for oc in range(16):
                    if oc >= 2:
                        t.wait_ge(P.sv, ev[oc - 2])
                    half = P.pC[:, 0:1024] if oc % 2 == 0 else P.pC[:, 1024:2048]
                    for ti in range(8):
                        ins = t.transpose(
                            half.bitcast(BF16)[:, ti * 256:ti * 256 + 128]
                            if False else half[:, ti * 128:(ti + 1) * 128],
                            P.gb(src_base + oc * CSZ, 128, 1024)
                            [:, ti * 128:(ti + 1) * 128],
                            P.c_idbf)
                    P.inc(ins, P.sp, 1)
            P.sec("tensor", f_t)

            def f_v(v):
                for oc in range(16):
                    v.wait_ge(P.sp, tp[oc])
                    half = P.pC[:, 0:1024] if oc % 2 == 0 else P.pC[:, 1024:2048]
                    P.inc(v.tensor_copy(
                        out=P.gb(dst_base, 128, 16384).rearrange(
                            "p (ti n) -> p ti n", ti=8)[:, :, oc * 128:
                                                        (oc + 1) * 128],
                        in_=half.rearrange("p (ti n) -> p ti n", ti=8)),
                        P.sv, 1)
            P.sec("vector", f_v)

        def transpose_phase2(src_base, dst_base):
            sp_c = P.g(P.sp)
            tp = {}
            for oc in range(16):
                sp_c += 1
                tp[oc] = sp_c
            ev = {}
            sv_c = P.g(P.sv)
            for oc in range(16):
                sv_c += 1
                ev[oc] = sv_c
            pCb = P.pC.bitcast(BF16)

            def f_t(t):
                t.wait_ge(P.sv, P.g(P.sv))
                for oc in range(16):
                    if oc >= 2:
                        t.wait_ge(P.sv, ev[oc - 2])
                    base = 0 if oc % 2 == 0 else 1024
                    for ti in range(8):
                        ins = t.transpose(
                            pCb[:, base + ti * 128:base + (ti + 1) * 128],
                            P.gb(src_base + oc * CSZ, 128, 1024)
                            [:, ti * 128:(ti + 1) * 128],
                            P.c_idbf)
                    P.inc(ins, P.sp, 1)
            P.sec("tensor", f_t)

            def f_v(v):
                for oc in range(16):
                    v.wait_ge(P.sp, tp[oc])
                    base = 0 if oc % 2 == 0 else 1024
                    P.inc(v.tensor_copy(
                        out=P.gb(dst_base, 128, 16384).rearrange(
                            "p (ti n) -> p ti n", ti=8)[:, :, oc * 128:
                                                        (oc + 1) * 128],
                        in_=pCb[:, base:base + 1024].rearrange(
                            "p (ti n) -> p ti n", ti=8)), P.sv, 1)
            P.sec("vector", f_v)

        q1 = lambda oc: P.gb(8192 + oc * CSZ, 128, 1024)
        k1 = lambda oc: P.gb(12288 + oc * CSZ, 128, 1024)
        stage = lambda oc: P.gb(R3A + oc * CSZ, 128, 1024)

        if self.aupto <= 2:
            return
        proj_sweep([(P.wq, 8, q1), (P.wk, 8, k1), (P.wv, 16, stage)])
        P.full_barrier()
        if self.aupto <= 3:
            return
        transpose_phase2(R3A, R1)
        P.full_barrier()
        if self.aupto <= 4:
            return
        proj_sweep([(P.wgx, 16, stage)])
        P.full_barrier()
        transpose_phase2(R3A, R2)
        P.full_barrier()
        if self.aupto <= 5:
            return

        self.attn_gk(bgk2rep, S)
        P.full_barrier()
        if self.aupto <= 6:
            return

        self.attn_core()

    def attn_gk(self, bgk2rep, S):
        P = self
        xb = lambda c: P.gb(R0 + c * CSZ, 128, 1024)
        upre = S.b(1024)
        ubuf = S.f(1024)
        wgk2b = S.b(1024)
        w1 = P.MISC[:, 0:128].bitcast(BF16)[:, 0:256]

        sg0 = P.g(P.sg)

        def f_wl(g):
            P.inc(g.dma_start(
                out=w1.rearrange("p (k n) -> p k n", k=16),
                in_=P.wgk1.rearrange("(k p) n -> p k n", p=128)), P.sg, 16)
            P.inc(g.dma_start(out=wgk2b[0:16, :], in_=P.wgk2), P.sg, 16)
        P.sec("gpsimd", f_wl)

        sp0 = P.g(P.sp)

        def f_mm1(t):
            t.wait_ge(P.sg, sg0 + 32)
            for kc in range(16):
                for hh in range(2):
                    ins = t.matmul(P.pA[0:16, hh * 512:(hh + 1) * 512],
                                   w1.rearrange("p (k n) -> p k n", k=16)[:, kc, :],
                                   xb(kc)[:, hh * 512:(hh + 1) * 512],
                                   start=(kc == 0), stop=(kc == 15))
            P.inc(ins, P.sp, 1)
        P.sec("tensor", f_mm1)

        sv0 = P.g(P.sv)

        def f_vu(v):
            v.wait_ge(P.sp, sp0 + 1)
            P.inc(v.tensor_copy(out=upre[0:16, :], in_=P.pA[0:16, :]), P.sv, 1)
        P.sec("vector", f_vu)

        sp1, sv1, ss0 = P.g(P.sp), P.g(P.sv), P.g(P.ss)

        def f_mm2(t):
            t.wait_ge(P.sv, sv1)
            for ti in range(8):
                if ti >= 2:
                    t.wait_ge(P.sv, sv1 + 2 * (ti - 1))
                pp = P.pA if ti % 2 == 0 else P.pB
                for hh in range(2):
                    ins = t.matmul(pp[:, hh * 512:(hh + 1) * 512],
                                   upre[0:16, ti * 128:(ti + 1) * 128],
                                   wgk2b[0:16, hh * 512:(hh + 1) * 512],
                                   start=True, stop=True)
                    P.inc(ins, P.sp, 1)
        P.sec("tensor", f_mm2)

        def f_vu2(v):
            for ti in range(8):
                v.wait_ge(P.sp, sp1 + 2 * (ti + 1))
                if ti >= 1:
                    v.wait_ge(P.ss, ss0 + ti)
                pp = P.pA if ti % 2 == 0 else P.pB
                P.fence(v, v.tensor_scalar(out=ubuf, in0=pp[:, :],
                                scalar1=P.t_r[:, ti:ti + 1], scalar2=None,
                                op0=ALU.mult))
                P.inc(v.tensor_add(ubuf, ubuf, bgk2rep), P.sv, 2)
        P.sec("vector", f_vu2)

        gkn = lambda ti: P.gb(R0 + ti * CSZ, 128, 1024)

        def f_su(s_):
            for ti in range(8):
                s_.wait_ge(P.sv, sv1 + 2 * (ti + 1))
                P.fence(s_, s_.activation(out=ubuf, in_=ubuf, func=AF.Exp,
                                          scale=-1.0), "s")
                P.inc(s_.activation(out=gkn(ti), in_=ubuf,
                                    func=AF.Ln, bias=1.0), P.ss, 1)
        P.sec("scalar", f_su)
        self.gkn = gkn

    def attn_core(self):
        P = self
        gkn = self.gkn
        SL = 4096
        s_loc = P.gf(SL, 128, 1024)
        s_init = P.gf(SL + 1024, 128, 1024)
        s_bar = [P.gb(SL + 2048, 128, 1024),
                 P.gb(R3X + 1536, 128, 1024)]
        tmA = lambda b: P.gb(SL + 2560, 128, 1024)
        tmB = lambda b: P.gb(SL + 3072, 128, 1024)
        eaqf = P.gb(SL + 3584, 128, 512)
        eakf = P.gb(SL + 3840, 128, 512)
        kbarf = P.gb(R3X, 128, 512)
        kbtm = P.gb(R3X + 256, 128, 1024)
        pbar = P.gb(R3X + 768, 128, 1024)
        qhat = P.gb(R3X + 1280, 128, 512)
        gath = [P.gf(R3X + 1536, 128, 1024), P.gf(R3X + 2560, 128, 1024)]

        q1v = P.gb(8192, 128, 8192).rearrange("p (kc n) -> p kc n", kc=8)
        k1v = P.gb(12288, 128, 8192).rearrange("p (kc n) -> p kc n", kc=8)
        vtm = P.gb(R1, 128, 16384).rearrange("p (ti n) -> p ti n", ti=8)
        gtm = P.gb(R2, 128, 16384).rearrange("p (ti n) -> p ti n", ti=8)
        otm = P.gb(R3A, 128, 16384).rearrange("p (ti n) -> p ti n", ti=8)
        eG = P.MISC[:, 128:256].rearrange("p (a b) -> p a b", a=8)
        eA = P.MISC[:, 256:384].rearrange("p (a b) -> p a b", a=8)
        g16b = P.MISC[0:16, 384:896].bitcast(BF16)

        sv0, ss0 = P.g(P.sv), P.g(P.ss)

        def f_vsg(v):
            for ti in range(8):
                P.inc(v.tensor_scalar(out=gtm[:, ti, :], in0=gtm[:, ti, :],
                                      scalar1=P.t_r[:, ti:ti + 1], scalar2=None,
                                      op0=ALU.mult), P.sv, 1)
        P.sec("vector", f_vsg)

        def f_ssg(s_):
            s_.wait_ge(P.sv, sv0 + 8)
            P.inc(s_.activation(out=P.gb(R2, 128, 16384), in_=P.gb(R2, 128, 16384),
                                func=AF.Silu), P.ss, 1)
        P.sec("scalar", f_ssg)
        if self.aupto == 61:
            P.full_barrier()
            return

        sp0 = P.g(P.sp)

        def f_gpre(t):
            t.wait_ge(P.ss, P.g(P.ss))
            for j in range(16):
                b = (j % 2) * 64
                pp = P.pA if j % 2 == 0 else P.pB
                for hh in range(2):
                    ins = t.matmul(pp[0:16, hh * 512:(hh + 1) * 512],
                                   P.c_oh[b:b + 64, j * 16:(j + 1) * 16],
                                   gkn(j // 2)[b:b + 64, hh * 512:(hh + 1) * 512],
                                   start=(j // 2 == 0), stop=(j // 2 == 7))
            P.inc(ins, P.sp, 1)
        P.sec("tensor", f_gpre)

        sv1 = P.g(P.sv)

        def f_vg16(v):
            v.wait_ge(P.sp, sp0 + 1)
            P.fence(v, v.tensor_copy(out=g16b[0:16, :], in_=P.pA[0:16, :]))
            P.inc(v.tensor_tensor(out=g16b[0:16, :], in0=g16b[0:16, :],
                                  in1=P.pB[0:16, :], op=ALU.add), P.sv, 1)
        P.sec("vector", f_vg16)
        if self.aupto == 62:
            P.full_barrier()
            return

        sp1 = P.g(P.sp)

        def f_aex(t):
            t.wait_ge(P.sv, sv1 + 1)
            for hh in range(2):
                ins = t.matmul(P.pB[0:16, hh * 512:(hh + 1) * 512], P.c_su16,
                               g16b[0:16, hh * 512:(hh + 1) * 512],
                               start=True, stop=True)
            P.inc(ins, P.sp, 1)
            for bb in range(8):
                t.transpose(P.pC.bitcast(BF16)[:, bb * 16:(bb + 1) * 16],
                            g16b[0:16, bb * 128:(bb + 1) * 128],
                            P.c_idbf[0:16, 0:16])
            P.inc(t.matmul(P.pA[0:1, 0:1], P.c_ones1, P.c_ones1,
                           start=True, stop=True), P.sp, 1)
        P.sec("tensor", f_aex)
        if self.aupto == 63:
            P.full_barrier()
            return

        sv2 = P.g(P.sv)
        ss1 = P.g(P.ss)

        def f_veg(v):
            v.wait_ge(P.sp, sp1 + 2)
            P.inc(v.tensor_copy(out=P.MISC[:, 128:256],
                                in_=P.pC.bitcast(BF16)[:, 0:128]), P.sv, 1)
        P.sec("vector", f_veg)

        sp2 = P.g(P.sp)

        def f_aex2(t):
            t.wait_ge(P.sv, sv2 + 1)
            pass
        aexb = P.MISC[0:16, 896:1408].bitcast(BF16)

        def f_vaex(v):
            v.wait_ge(P.sp, sp1 + 2)
            P.inc(v.tensor_copy(out=aexb[0:16, :], in_=P.pB[0:16, :]), P.sv, 1)
        P.sec("vector", f_vaex)
        if self.aupto == 64:
            P.full_barrier()
            return

        sp3 = P.g(P.sp)
        sv3 = P.g(P.sv)

        def f_taex(t):
            t.wait_ge(P.sv, sv3)
            for bb in range(8):
                ins = t.transpose(P.pC.bitcast(BF16)[:, 256 + bb * 16:
                                                     256 + (bb + 1) * 16],
                                  aexb[0:16, bb * 128:(bb + 1) * 128],
                                  P.c_idbf[0:16, 0:16])
            P.inc(ins, P.sp, 1)
        P.sec("tensor", f_taex)

        sv4 = P.g(P.sv)

        def f_veA(v):
            v.wait_ge(P.sp, sp3 + 1)
            P.inc(v.tensor_copy(out=P.MISC[:, 256:384],
                                in_=P.pC.bitcast(BF16)[:, 256:384]), P.sv, 1)
        P.sec("vector", f_veA)

        ss2 = P.g(P.ss)

        def f_seg(s_):
            s_.wait_ge(P.sv, P.g(P.sv))
            s_.activation(out=P.MISC[:, 128:256], in_=P.MISC[:, 128:256],
                          func=AF.Exp, scale=-1.0)
            P.inc(s_.activation(out=P.MISC[:, 256:384], in_=P.MISC[:, 256:384],
                                func=AF.Exp, scale=-1.0), P.ss, 1)
        P.sec("scalar", f_seg)

        sv5 = P.g(P.sv)

        def f_vz(v):
            P.inc(v.memset(s_loc[:, :], 0.0), P.sv, 1)
        P.sec("vector", f_vz)
        P.full_barrier()
        self.snap("gkn", P.gb(R0, 128, 8192))
        self.snap("eg", P.MISC[:, 128:384])
        if self.aupto <= 7:
            return

        self._s1 = {k: {} for k in
                    ("exp", "tp", "ev", "kt", "kv", "mp", "mask", "o",
                     "sup", "oev", "ointer", "oadd")}
        nchunks_s1 = NGC if not (80 <= self.aupto < 80 + NGC) else \
            (self.aupto - 80)
        if self.aupto > 100:
            nchunks_s1 = 1 if self.aupto < 200 else 2
        if self.aupto in (1071, 1072):
            nchunks_s1 = 1
        for j in range(nchunks_s1):
            self._sweep1_chunk(j, s_loc, s_bar, tmA, tmB, eaqf, eakf, kbarf,
                               kbtm, pbar, q1v, k1v, vtm, otm, eG)

        P.full_barrier()
        self.snap("sloc", s_loc[:, :])
        self.snap("otm1", P.gb(R3A, 128, 16384))
        if self.aupto <= 8 or (80 <= self.aupto < 97) or self.aupto > 100:
            return

        sg1 = P.g(P.sg)

        def f_gag(g):
            P.inc(g.dma_start(out=P.ag_m_in, in_=s_loc[:, :]), P.sg, 16)
            g.wait_ge(P.sg, P.g(P.sg))
            P.inc(g.collective_compute(
                "AllGather", ALU.bypass, replica_groups=[list(range(8))],
                ins=[P.ag_m_in.opt()], outs=[P.ag_m_out.opt()]), P.sc, 1)
            g.wait_ge(P.sc, P.g(P.sc))
            for rr in range(8):
                if rr >= 2:
                    g.wait_ge(P.sv, self._agv0 + (rr - 1))
                self._ds_g.load(g, gath[rr % 2],
                                P.ag_m_out[rr * 128:(rr + 1) * 128, :], rr)
        self._agv0 = P.g(P.sv)
        self._ds_g = P.dstream("dg")
        P.sec("gpsimd", f_gag)

        sv6 = P.g(P.sv)

        def f_vmask(v):
            v.memset(s_init[:, :], 0.0)
            for rr in range(8):
                self._ds_g.wait(v, rr)
                P.fence(v, v.tensor_scalar(out=gath[rr % 2], in0=gath[rr % 2],
                                scalar1=P.c_mask[:, rr:rr + 1], scalar2=None,
                                op0=ALU.mult))
                P.inc(v.tensor_add(s_init[:, :], s_init[:, :], gath[rr % 2]),
                      P.sv, 1)
            P.fence(v, ins_last_add := v.tensor_copy(out=P.TMS[:, 48:49],
                                                     in_=P.TMS[:, 48:49]))
            P.inc(v.tensor_copy(out=s_bar[0], in_=s_init[:, :]), P.sv, 1)
        P.sec("vector", f_vmask)
        P.full_barrier()

        self._s2 = {k: {} for k in ("q", "mm", "add")}
        for j in range(NGC):
            self._sweep2_chunk(j, s_bar[0], qhat, q1v, otm, eA)
        P.full_barrier()
        self.snap("otm2", P.gb(R3A, 128, 16384))
        if self.aupto <= 9:
            return

        self.attn_epilogue(otm, gtm)

    def _sweep1_chunk(self, j, s_loc, s_bar, tmA, tmB, eaqf, eakf, kbarf,
                      kbtm, pbar, q1v, k1v, vtm, otm, eG):
        P = self
        b = (j % 2) * 64
        ti = j // 2
        gkn = self.gkn
        pAb = P.pA.bitcast(BF16)

        st = self._s1

        sp0 = P.g(P.sp)

        def f_cs(t):
            if j >= 1:
                t.wait_ge(P.sv, st["mask"][j - 1])
            for hh in range(2):
                ins = t.matmul(P.pA[b:b + 64, hh * 512:(hh + 1) * 512],
                               P.c_u64s[b:b + 64, :],
                               gkn(ti)[b:b + 64, hh * 512:(hh + 1) * 512],
                               start=True, stop=True)
            P.inc(ins, P.sp, 1)
        P.sec("tensor", f_cs)
        if self.aupto % 100 == 1 and 100 < self.aupto < 1000:
            return

        ss0 = P.g(P.ss)

        def f_exp(s_):
            s_.wait_ge(P.sp, sp0 + 1)
            if j >= 2:
                s_.wait_ge(P.sp, st["tp"][j - 2])
            s_.activation(out=tmA(b)[b:b + 64, :], in_=P.pA[b:b + 64, :],
                          func=AF.Exp, scale=-1.0,
                          bias=P.t_lnr8[b:b + 64, ti:ti + 1])
            P.inc(s_.activation(out=tmB(b)[b:b + 64, :], in_=P.pA[b:b + 64, :],
                                func=AF.Exp, scale=1.0,
                                bias=P.t_lnr2[b:b + 64, ti:ti + 1]), P.ss, 1)
        P.sec("scalar", f_exp)
        if self.aupto % 100 == 2 and 100 < self.aupto < 1000:
            return
        st["exp"][j] = P.g(P.ss)

        def f_tp(t):
            t.wait_ge(P.ss, st["exp"][j])
            for bb in range(8):
                t.transpose(pAb[:, bb * 64:(bb + 1) * 64],
                            tmA(b)[b:b + 64, bb * 128:(bb + 1) * 128],
                            P.c_idbf[b:b + 64, b:b + 64])
            for bb in range(8):
                ins = t.transpose(pAb[:, 1024 + bb * 64:1024 + (bb + 1) * 64],
                                  tmB(b)[b:b + 64, bb * 128:(bb + 1) * 128],
                                  P.c_idbf[b:b + 64, b:b + 64])
            P.inc(ins, P.sp, 1)
        P.sec("tensor", f_tp)
        if self.aupto % 100 == 3 and 100 < self.aupto < 1000:
            return
        st["tp"][j] = P.g(P.sp)

        def f_ev(v):
            v.wait_ge(P.sp, st["tp"][j])
            v.tensor_copy(out=eaqf, in_=pAb[:, 0:512])
            P.fence(v, v.tensor_copy(out=eakf, in_=pAb[:, 1024:1536]))
            qs = q1v[:, :, j * 64:(j + 1) * 64]
            ks = k1v[:, :, j * 64:(j + 1) * 64]
            ea3 = eaqf.rearrange("p (a c) -> p a c", a=8)
            ek3 = eakf.rearrange("p (a c) -> p a c", a=8)
            v.tensor_tensor(out=qs, in0=qs, in1=ea3, op=ALU.mult)
            P.fence(v, v.tensor_tensor(out=ks, in0=ks, in1=ek3, op=ALU.mult))
            kb3 = kbarf.rearrange("p (a c) -> p a c", a=8)
            for hp in range(8):
                ins = v.tensor_scalar(out=kb3[:, hp, :], in0=ks[:, hp, :],
                                      scalar1=eG[:, hp, j:j + 1], scalar2=None,
                                      op0=ALU.mult)
            P.inc(ins, P.sv, 1)
        P.sec("vector", f_ev)
        if self.aupto % 100 == 4 and 100 < self.aupto < 1000:
            return
        st["ev"][j] = P.g(P.sv)

        def f_kt(t):
            t.wait_ge(P.sv, st["ev"][j])
            for bb in range(8):
                ins = t.transpose(pAb[b:b + 64, bb * 128:(bb + 1) * 128],
                                  kbarf[:, bb * 64:(bb + 1) * 64], P.c_idbf)
            P.inc(ins, P.sp, 1)
        P.sec("tensor", f_kt)
        if self.aupto % 100 == 5 and 100 < self.aupto < 1000:
            return
        st["kt"][j] = P.g(P.sp)

        sbj = s_bar[j % 2]

        def f_kv(v):
            v.wait_ge(P.sp, st["kt"][j])
            if j >= 2:
                v.wait_ge(P.sp, st["ointer"][(j - 2) // 2])
            v.tensor_copy(out=kbtm[b:b + 64, :], in_=pAb[b:b + 64, 0:1024])
            P.inc(v.tensor_copy(out=sbj, in_=s_loc[:, :]), P.sv, 1)
        P.sec("vector", f_kv)
        if self.aupto % 100 == 6 and 100 < self.aupto < 1000:
            return
        st["kv"][j] = P.g(P.sv)

        def f_mp(t):
            t.wait_ge(P.sv, st["kv"][j])
            if j >= 1:
                t.wait_ge(P.sv, st["sup"][j - 1])
            kb3 = kbtm.rearrange("p (a c) -> p a c", a=8)
            ins = None
            if self.aupto != 1072:
                for h in range(H):
                    hb = (h % 2) * 64
                    ins = t.matmul(
                        P.pB[hb:hb + 64, (h // 2) * 128:(h // 2 + 1) * 128],
                        kb3[b:b + 64, h // 2, (h % 2) * 64:(h % 2) * 64 + 64],
                        vtm[b:b + 64, ti, h * 128:(h + 1) * 128],
                        start=True, stop=True)
            if self.aupto != 1071:
                for h in range(H):
                    hb = (h % 2) * 64
                    qsh = P.gb(8192 + (h // 2) * 512 - 32, 128, 1152)[
                        hb:hb + 64, j * 64:j * 64 + 128]
                    ins = t.matmul(P.pA[hb:hb + 64,
                                        (h // 2) * 128:(h // 2 + 1) * 128],
                                   k1v[hb:hb + 64, h // 2,
                                       j * 64:(j + 1) * 64],
                                   qsh, start=True, stop=True)
            P.inc(ins, P.sp, 1)
        P.sec("tensor", f_mp)
        if self.aupto % 100 == 7 and 100 < self.aupto < 1000 or self.aupto in (1071, 1072):
            return
        st["mp"][j] = P.g(P.sp)

        def f_mask(v):
            v.wait_ge(P.sp, st["mp"][j])
            pb4 = pbar[b:b + 64, :].rearrange("p (g two c) -> p g two c",
                                              g=8, two=2)
            ins = None
            for par in range(2):
                pbp = par * 64
                in0 = P.pA[pbp:pbp + 64, :].rearrange(
                    "p (g c) -> p g c", g=8)[:, :, 64:128]
                m3 = P.c_u64m[pbp:pbp + 64, :][:, None, :].broadcast_to(
                    [64, 8, 64])
                ins = v.tensor_tensor(out=pb4[:, :, par, :], in0=in0,
                                      in1=m3, op=ALU.mult)
            P.inc(ins, P.sv, 1)
        P.sec("vector", f_mask)
        if self.aupto % 100 == 8 and 100 < self.aupto < 1000:
            return
        st["mask"][j] = P.g(P.sv)

        def f_o(t):
            t.wait_ge(P.sv, st["mask"][j])
            if j >= 2:
                t.wait_ge(P.sv, st["oadd"][(j - 2) // 2])
            pb3 = pbar.rearrange("p (a c) -> p a c", a=16)
            for h in range(H):
                ins = t.matmul(P.pC[b:b + 64, h * 128:(h + 1) * 128],
                               pb3[b:b + 64, h, :],
                               vtm[b:b + 64, ti, h * 128:(h + 1) * 128],
                               start=True, stop=True)
            P.inc(ins, P.sp, 1)
        P.sec("tensor", f_o)
        if self.aupto % 100 == 9 and 100 < self.aupto < 1000:
            return
        st["o"][j] = P.g(P.sp)

        def f_sup(v):
            v.wait_ge(P.sp, st["mp"][j])
            v.wait_ge(P.sv, st["kv"][j])
            sl3 = s_loc.rearrange("p (a c) -> p a c", a=8)
            ins = None
            for hp in range(8):
                ins = v.tensor_scalar(out=sl3[:, hp, :], in0=sl3[:, hp, :],
                                scalar1=eG[:, hp, j:j + 1], scalar2=None,
                                op0=ALU.mult)
            P.fence(v, ins)
            P.inc(v.tensor_add(s_loc[:, :], s_loc[:, :], P.pB[:, :]), P.sv, 1)
        P.sec("vector", f_sup)
        if self.aupto % 100 == 10 and 100 < self.aupto < 1000:
            return
        st["sup"][j] = P.g(P.sv)

        if j % 2 == 1:
            def f_oev(s_):
                s_.wait_ge(P.sp, st["o"][j])
                P.inc(s_.activation(out=otm[:, ti, :], in_=P.pC[:, :],
                                    func=AF.Copy), P.ss, 1)
            P.sec("scalar", f_oev)
            st["oev"][ti] = P.g(P.ss)
            if self.aupto % 100 == 11 and 100 < self.aupto < 1000:
                return

            def f_ointer(t):
                t.wait_ge(P.ss, st["oev"][ti])
                t.wait_ge(P.sv, st["kv"][j])
                for jj, bb in ((j - 1, 0), (j, 64)):
                    qs = q1v[:, :, jj * 64:(jj + 1) * 64]
                    sb3 = s_bar[jj % 2].rearrange("p (a c) -> p a c", a=8)
                    for h in [x for x in range(H) if x % 2 == 0] + \
                             [x for x in range(H) if x % 2 == 1]:
                        hb = (h % 2) * 64
                        ins = t.matmul(P.pC[bb:bb + 64, h * 128:(h + 1) * 128],
                                       qs[hb:hb + 64, h // 2, :],
                                       sb3[hb:hb + 64, h // 2, :],
                                       start=True, stop=True)
                P.inc(ins, P.sp, 1)
            P.sec("tensor", f_ointer)
            st["ointer"][ti] = P.g(P.sp)
            if self.aupto % 100 == 12 and 100 < self.aupto < 1000:
                return

            def f_oadd(v):
                v.wait_ge(P.sp, st["ointer"][ti])
                P.inc(v.tensor_tensor(out=otm[:, ti, :], in0=otm[:, ti, :],
                                      in1=P.pC[:, :], op=ALU.add), P.sv, 1)
            P.sec("vector", f_oadd)
            st["oadd"][ti] = P.g(P.sv)

    def _sweep2_chunk(self, j, s_bar, qhat, q1v, otm, eA):
        P = self
        b = (j % 2) * 64
        ti = j // 2
        st = self._s2

        def f_q(v):
            if j >= 2:
                v.wait_ge(P.sp, st["mm"][j - 2])
            q3 = qhat.rearrange("p (a c) -> p a c", a=8)
            qs = q1v[:, :, j * 64:(j + 1) * 64]
            for hp in range(8):
                ins = v.tensor_scalar(out=q3[:, hp, :], in0=qs[:, hp, :],
                                      scalar1=eA[:, hp, j:j + 1], scalar2=None,
                                      op0=ALU.mult)
            P.inc(ins, P.sv, 1)
        P.sec("vector", f_q)
        st["q"][j] = P.g(P.sv)

        def f_mm(t):
            t.wait_ge(P.sv, st["q"][j])
            if j >= 2:
                t.wait_ge(P.sv, st["add"][j - 2])
            q3 = qhat.rearrange("p (a c) -> p a c", a=8)
            s3 = s_bar.rearrange("p (a c) -> p a c", a=8)
            for h in [x for x in range(H) if x % 2 == 0] + \
                     [x for x in range(H) if x % 2 == 1]:
                hb = (h % 2) * 64
                ins = t.matmul(P.pC[b:b + 64, h * 128:(h + 1) * 128],
                               q3[hb:hb + 64, h // 2, :],
                               s3[hb:hb + 64, h // 2, :],
                               start=True, stop=True)
            P.inc(ins, P.sp, 1)
        P.sec("tensor", f_mm)
        st["mm"][j] = P.g(P.sp)

        def f_add(v):
            v.wait_ge(P.sp, st["mm"][j])
            P.inc(v.tensor_tensor(out=otm[b:b + 64, ti, :],
                                  in0=otm[b:b + 64, ti, :],
                                  in1=P.pC[b:b + 64, :], op=ALU.add), P.sv, 1)
        P.sec("vector", f_add)
        st["add"][j] = P.g(P.sv)

    def attn_epilogue(self, otm, gtm):
        P = self
        o2 = P.gb(8192, 128, 16384)
        sv0 = P.g(P.sv)

        def f_vo2(v):
            P.fence(v, v.tensor_tensor(out=o2, in0=P.gb(R3A, 128, 16384),
                                  in1=P.gb(R3A, 128, 16384), op=ALU.mult))
            P.fence(v, v.reduce_sum(out=P.WA[:, 0:128],
                               in_=o2.rearrange("p (s c) -> p s c", s=128),
                               axis=AX.X))
            P.inc(v.tensor_scalar(out=P.WA[:, 128:256], in0=P.WA[:, 0:128],
                                  scalar1=1.0 / 128.0, scalar2=None,
                                  op0=ALU.mult), P.sv, 3)
        P.sec("vector", f_vo2)

        ss0 = P.g(P.ss)

        def f_sr(s_):
            s_.wait_ge(P.sv, sv0 + 3)
            P.inc(s_.activation(out=P.WA[:, 128:256], in_=P.WA[:, 128:256],
                                func=AF.Ln, bias=P.c_eps[:, 0:1]), P.ss, 1)
        P.sec("scalar", f_sr)

        sv1 = P.g(P.sv)

        def f_vr(v):
            v.wait_ge(P.ss, ss0 + 1)
            P.inc(v.tensor_scalar(out=P.WA[:, 128:256], in0=P.WA[:, 128:256],
                                  scalar1=-0.5, scalar2=None, op0=ALU.mult),
                  P.sv, 1)
        P.sec("vector", f_vr)

        ss1 = P.g(P.ss)

        def f_se(s_):
            s_.wait_ge(P.sv, sv1 + 1)
            P.inc(s_.activation(out=P.WA[:, 128:256], in_=P.WA[:, 128:256],
                                func=AF.Exp), P.ss, 1)
        P.sec("scalar", f_se)

        ds_row = P.dstream("drow")

        nwstage = P.gf(R3X, 1, 2048)

        def f_nw(sy):
            ds_row.load(sy, nwstage, P.nwrow, 1)
        P.sec("sync", f_nw)

        sp0 = P.g(P.sp)

        def f_mmnw(t):
            ds_row.wait(t, 1)
            for hh in range(4):
                ins = t.matmul(P.pC[:, hh * 512:(hh + 1) * 512], P.c_onesrow,
                               nwstage[0:1, hh * 512:(hh + 1) * 512],
                               start=True, stop=True)
            P.inc(ins, P.sp, 1)
        P.sec("tensor", f_mmnw)

        sv2 = P.g(P.sv)

        def f_vap(v):
            v.wait_ge(P.ss, ss1 + 1)
            v.wait_ge(P.sp, sp0 + 1)
            r3 = P.WA[:, 128:256].rearrange("p (a c) -> p a c", a=8)
            for ti in range(8):
                v.tensor_tensor(
                    out=otm[:, ti, :].rearrange("p (a c) -> p a c", a=16),
                    in0=otm[:, ti, :].rearrange("p (a c) -> p a c", a=16),
                    in1=r3[:, ti, :][:, :, None].broadcast_to([128, 16, 128]),
                    op=ALU.mult)
                ins = v.tensor_tensor(out=gtm[:, ti, :], in0=gtm[:, ti, :],
                                in1=P.pC[:, :], op=ALU.mult)
            P.fence(v, ins)
            P.inc(v.tensor_tensor(out=P.gb(R3A, 128, 16384),
                                  in0=P.gb(R3A, 128, 16384),
                                  in1=P.gb(R2, 128, 16384), op=ALU.mult),
                  P.sv, 1)
        P.sec("vector", f_vap)
        P.full_barrier()
        self.snap("ofin", P.gb(R3A, 128, 16384))

        tp = {}
        sp1 = P.g(P.sp)
        sv3 = P.g(P.sv)
        pAb = P.pA.bitcast(BF16)
        pBb = P.pB.bitcast(BF16)

        def f_t(t):
            for oc in range(16):
                if oc >= 2:
                    t.wait_ge(P.sv, sv3 + (oc - 1))
                pb = pAb if oc % 2 == 0 else pBb
                for ti in range(8):
                    ins = t.transpose(pb[:, ti * 128:(ti + 1) * 128],
                                      otm[:, ti, oc * 128:(oc + 1) * 128],
                                      P.c_idbf)
                P.inc(ins, P.sp, 1)
        P.sec("tensor", f_t)

        def f_tv(v):
            for oc in range(16):
                v.wait_ge(P.sp, sp1 + oc + 1)
                pb = pAb if oc % 2 == 0 else pBb
                P.inc(v.tensor_copy(out=P.gb(R1 + oc * CSZ, 128, 1024),
                                    in_=pb[:, 0:1024]), P.sv, 1)
        P.sec("vector", f_tv)
        P.full_barrier()

        wbufb = [P.WA.bitcast(BF16)[:, 0:2048], P.WA.bitcast(BF16)[:, 2048:4096]]
        spw = P.g(P.sp)
        svw = P.g(P.sv)
        sdx = P.g(P.sd)
        ds_wo = P.dstream("daw")

        def f_wl(g):
            for oc in range(16):
                if oc >= 2:
                    g.wait_ge(P.sp, spw + (oc - 1))
                ds_wo.load(g,
                           wbufb[oc % 2].rearrange("p (k n) -> p k n", k=16),
                           P.wo[:, oc * 128:(oc + 1) * 128].rearrange(
                               "(k p) n -> p k n", p=128), oc)
        P.sec("gpsimd", f_wl)

        def f_xl(sy):
            for oc in range(16):
                P.inc(sy.dma_start(out=P.gf(R0 + oc * 1024, 128, 1024),
                                   in_=P.x_spill[oc * 128:(oc + 1) * 128, :]),
                      P.sd, 16)
        P.sec("sync", f_xl)
        sd_xl_total = P.g(P.sd)

        def f_mw(t):
            for oc in range(16):
                ds_wo.wait(t, oc)
                if oc >= 2:
                    t.wait_ge(P.sv, svw + (oc - 1))
                pp = P.pA if oc % 2 == 0 else P.pB
                for kc in range(16):
                    for hh in range(2):
                        ins = t.matmul(pp[:, hh * 512:(hh + 1) * 512],
                                       wbufb[oc % 2][:, kc * 128:(kc + 1) * 128],
                                       P.gb(R1 + kc * CSZ, 128, 1024)
                                       [:, hh * 512:(hh + 1) * 512],
                                       start=(kc == 0), stop=(kc == 15))
                P.inc(ins, P.sp, 1)
        P.sec("tensor", f_mw)

        def f_vres(v):
            v.wait_ge(P.sd, sd_xl_total)
            for oc in range(16):
                v.wait_ge(P.sp, spw + oc + 1)
                pp = P.pA if oc % 2 == 0 else P.pB
                P.inc(v.tensor_add(P.gf(R0 + oc * 1024, 128, 1024),
                                   P.gf(R0 + oc * 1024, 128, 1024), pp[:, :]),
                      P.sv, 1)
        P.sec("vector", f_vres)

    def ffn_core_fix(self):
        pass

    def ph_finish(self):
        P = self
        P.full_barrier()
        if self.debug and self.dbg:
            sdf = P.g(P.sd)

            def f_dbg(sy):
                n = 0
                for d, ap in self.dbg:
                    P.inc(sy.dma_start(out=d, in_=ap), P.sd, 16)
                    n += 1
                sy.wait_ge(P.sd, sdf + 16 * n)
            P.sec("sync", f_dbg)
        ds = getattr(self, "out_streams", None)

        def f_end(g):
            if ds is not None:
                for s in ds.sems:
                    g.wait_ge(s, P.g(s))
        P.sec("gpsimd", f_end)


def _hadamard(n):
    h = np.array([[1.0]], dtype=np.float64)
    while h.shape[0] < n:
        h = np.block([[h, h], [h, -h]])
    return h


def build_host_inputs(inputs, core):
    b, half = core // 2, core % 2
    x = np.asarray(inputs["L0_attn_x"], np.float32)
    wg = np.ascontiguousarray(np.asarray(inputs["L0_ffn_w_gate"], np.float32).T)
    wu = np.ascontiguousarray(np.asarray(inputs["L0_ffn_w_up"], np.float32).T)
    wd = np.ascontiguousarray(np.asarray(inputs["L0_ffn_w_down"], np.float32).T)

    xT = np.ascontiguousarray(x[b, half * 1024:(half + 1) * 1024, :].T)

    def gu_shard(wT):
        sh = wT[256 * core:256 * (core + 1)]
        a = sh.reshape(2, 128, 64, 128).transpose(0, 2, 1, 3)
        return a

    gsh = np.stack([gu_shard(wg), gu_shard(wu)])
    wgt_sh = np.ascontiguousarray(gsh.reshape(32768, 128))

    dsh = wd[1024 * core:1024 * (core + 1)]
    dd = dsh.reshape(4, 2, 128, 16, 128).transpose(3, 0, 2, 1, 4)
    wdt_sh = np.ascontiguousarray(dd.reshape(16384, 128))

    u64 = np.triu(np.ones((64, 64), np.float32))
    su16 = np.triu(np.ones((16, 16), np.float32), 1)
    oh = np.zeros((64, 256), np.float32)
    for jj in range(16):
        oh[:, jj * 16 + jj] = 1.0 / 16.0
    maskv = np.zeros((1, 8), np.float32)
    if half == 1:
        maskv[0, core - 1] = 1.0

    m = {
        "xT": xT,
        "wq": np.asarray(inputs["L0_attn_Wq"], np.float32),
        "wk": np.asarray(inputs["L0_attn_Wk"], np.float32),
        "wv": np.asarray(inputs["L0_attn_Wv"], np.float32),
        "wgx": np.asarray(inputs["L0_attn_Wg"], np.float32),
        "wo": np.asarray(inputs["L0_attn_Wo"], np.float32),
        "wgk1": np.asarray(inputs["L0_attn_Wgk1"], np.float32),
        "wgk2": np.asarray(inputs["L0_attn_Wgk2"], np.float32),
        "bgk2r": np.asarray(inputs["L0_attn_bgk2"], np.float32)[None, :],
        "nwrow": np.ascontiguousarray(
            np.tile(np.asarray(inputs["L0_attn_norm_w"], np.float32), 16))[None, :],
        "wgt_sh": wgt_sh,
        "wdt_sh": wdt_sh,
        "h128": _hadamard(128).astype(BF),
        "idbf": np.eye(128, dtype=np.float32).astype(BF),
        "idf32": np.eye(128, dtype=np.float32),
        "u64s": np.vstack([u64, u64]).astype(np.float32).astype(BF) * BF(1.0 / 16.0),
        "u64m": np.vstack([u64, u64]).astype(BF),
        "su16": su16.astype(BF),
        "oh16": np.vstack([oh, oh]).astype(BF),
        "maskv": maskv,
    }
    return m


_CACHE = {}


def get_nc(debug=False, upto=99, aupto=99):
    key = (debug, upto, aupto)
    if key not in _CACHE:
        kb = KB(debug=debug, upto=upto, aupto=aupto)
        _CACHE[key] = (kb, kb.build())
    return _CACHE[key]


def kernel(**inputs):
    kb, nc = get_nc(debug=True)
    in_maps = [build_host_inputs(inputs, c) for c in range(8)]
    res = run_bass_kernel_spmd(nc, in_maps, core_ids=list(range(8)))
    out = np.zeros((4, 2048, 2048), np.float32)
    for c in range(8):
        b, half = c // 2, c % 2
        out[b, half * 1024:(half + 1) * 1024, :] = res.results[c]["out"].T
    return out
